# version 03
# baseline (speedup 1.0000x reference)
import numpy as np
import ml_dtypes
import concourse.bass as bass
import concourse.mybir as mybir
from concourse.bass_utils import run_bass_kernel_spmd

F32 = mybir.dt.float32
BF16 = mybir.dt.bfloat16
AF = mybir.ActivationFunctionType
ALU = mybir.AluOpType
AX = mybir.AxisListType
BF = ml_dtypes.bfloat16

ENGINES = ("pe", "act", "dve", "pool", "sync")
SAME_ENGINE_COMPUTE_EDGE = True


class Op:
    __slots__ = ("eng", "kind", "fn", "deps", "seq", "needs_inc", "sem_val",
                 "tag")

    def __init__(self, eng, kind, fn, tag):
        self.eng = eng
        self.kind = kind
        self.fn = fn
        self.deps = []
        self.seq = -1
        self.needs_inc = False
        self.sem_val = None
        self.tag = tag


class Buf:
    __slots__ = ("name", "last_write", "readers")

    def __init__(self, name):
        self.name = name
        self.last_write = None
        self.readers = []


class KB:
    def __init__(self, nc):
        self.nc = nc
        self.ops = {e: [] for e in ENGINES}
        self.final_waits = []

    def buf(self, name):
        return Buf(name)

    def _keep_dep(self, dep, op):
        if dep.kind == "l" or op.kind == "l":
            return True
        if dep.eng != op.eng:
            return True
        if dep.kind == "d" and op.kind == "d":
            return False
        if dep.kind == "c" and op.kind == "c":
            if dep.eng == "pe":
                return False
            return SAME_ENGINE_COMPUTE_EDGE
        return True

    def emit(self, eng, fn, reads=(), writes=(), dma=False, tag="",
             extra_deps=(), coll=False):
        op = Op(eng, "l" if coll else ("d" if dma else "c"), fn, tag)
        deps = set()
        for b in reads:
            if b.last_write is not None:
                deps.add(b.last_write)
        for b in writes:
            if b.last_write is not None:
                deps.add(b.last_write)
            deps.update(b.readers)
        deps.update(extra_deps)
        deps.discard(op)
        op.deps = [d for d in deps if self._keep_dep(d, op)]
        for b in reads:
            if op not in b.readers:
                b.readers.append(op)
        for b in writes:
            b.last_write = op
            b.readers = []
        self.ops[eng].append(op)
        op.seq = len(self.ops[eng]) - 1
        return op

    def require_done(self, op):
        self.final_waits.append(op)

    def finalize_and_emit(self):
        nc = self.nc
        for e in ENGINES:
            for op in self.ops[e]:
                for d in op.deps:
                    d.needs_inc = True
        for op in self.final_waits:
            op.needs_inc = True
        LIMIT = 56000
        n_epochs = {}
        n_coll = 0
        for e in ENGINES:
            for k, step in (("c", 1), ("d", 16)):
                ep = 0
                v = 0
                for op in self.ops[e]:
                    if op.kind == "l" and op.needs_inc and op.sem_val is None:
                        op.sem_val = ("L", n_coll, 1)
                        n_coll += 1
                        continue
                    if op.kind != k or not op.needs_inc:
                        continue
                    if v + step > LIMIT:
                        ep += 1
                        v = 0
                    v += step
                    op.sem_val = (ep, v)
                n_epochs[(e, k)] = ep + 1 if v > 0 else 0
        sems = {}
        for (e, k), n in n_epochs.items():
            for ep in range(n):
                sems[(e, k, ep)] = nc.alloc_semaphore(f"s_{e}_{k}{ep}")
        for i in range(n_coll):
            sems[("coll", i)] = nc.alloc_semaphore(f"s_coll{i}")
        highwater = {}

        def wait_dep(eng_name, engobj, dep):
            if dep.sem_val[0] == "L":
                key = ("coll", dep.sem_val[1])
                if not highwater.get((eng_name, key)):
                    engobj.wait_ge(sems[key], 1)
                    highwater[(eng_name, key)] = 1
                return
            ep, val = dep.sem_val
            key = (dep.eng, dep.kind, ep)
            hw = highwater.get((eng_name, key), 0)
            if val > hw:
                engobj.wait_ge(sems[key], val)
                highwater[(eng_name, key)] = val

        def emit_stream(eng_name, engobj):
            for op in self.ops[eng_name]:
                for dep in sorted(op.deps, key=lambda d: (d.eng, d.kind,
                                                          d.sem_val)):
                    wait_dep(eng_name, engobj, dep)
                inst = op.fn(engobj)
                if op.needs_inc:
                    if op.sem_val[0] == "L":
                        inst.then_inc(sems[("coll", op.sem_val[1])])
                    else:
                        ep, _ = op.sem_val
                        inst.then_inc(sems[(op.eng, op.kind, ep)],
                                      16 if op.kind == "d" else 1)
            if eng_name == "pool":
                for op in self.final_waits:
                    wait_dep(eng_name, engobj, op)

        with nc.Block() as blk:

            @blk.tensor
            def _(tensor):
                emit_stream("pe", tensor)

            @blk.scalar
            def _(scalar):
                emit_stream("act", scalar)

            @blk.vector
            def _(vector):
                emit_stream("dve", vector)

            @blk.gpsimd
            def _(gpsimd):
                emit_stream("pool", gpsimd)

            @blk.sync
            def _(sync):
                emit_stream("sync", sync)

        return {e: len(self.ops[e]) for e in ENGINES}


B, T, D = 8, 2048, 1024
NCH = 8
HEADS, DH = 16, 64
E_EXP, HFF = 8, 4096
LN_EPS = 1e-5


class C:
    def __init__(self, tag):
        self.nc = bass.Bass("TRN2", target_bir_lowering=False)
        self.kb = KB(self.nc)
        self.E = self.kb.emit
        self.inp = {}
        self.tag = tag
        nc = self.nc
        self.xc = [nc.alloc_sbuf_tensor(f"xc{i}", [128, 2048], F32)
                   for i in range(NCH)]
        self.xl = [nc.alloc_sbuf_tensor(f"xl{i}", [128, 2048], BF16)
                   for i in range(NCH)]
        self.sa = nc.alloc_sbuf_tensor("sa", [128, 16896], BF16)
        self.sb = nc.alloc_sbuf_tensor("sb", [128, 5632], BF16)
        self.ua = [nc.alloc_sbuf_tensor(f"ua{i}", [128, 2048], BF16)
                   for i in range(NCH)]
        self.sf = nc.alloc_sbuf_tensor("sf", [128, 2048], F32)
        self.ws = nc.alloc_sbuf_tensor("ws", [128, 8192], BF16)
        self.ct = nc.alloc_sbuf_tensor("ct", [128, 896], F32)
        self.ctb = nc.alloc_sbuf_tensor("ctb", [128, 2048], BF16)
        self.ps = nc.alloc_psum_tensor("ps", [128, 4096], F32)
        kb = self.kb
        self.xcB = [kb.buf(f"xc{i}") for i in range(NCH)]
        self.xlB = [kb.buf(f"xl{i}") for i in range(NCH)]
        self.saB = kb.buf("sa")
        self.sbB = kb.buf("sb")
        self.uaB = [kb.buf(f"ua{i}") for i in range(NCH)]
        self.sfB = kb.buf("sf")
        self.wsB = kb.buf("ws")
        self.ctB_ = kb.buf("ct")
        self.ctbB = kb.buf("ctb")
        self.psB = [kb.buf(f"psb{i}") for i in range(8)]
        self._const_cols = {}
        self._n_const = 0
        self._extra_sb = {}
        self._init_consts()

    def _init_consts(self):
        self.E("pool", lambda e: e.memset(self.ct[:, 64:65], 1.0),
               writes=[self.ctB_], tag="ones_col")
        self.E("pool", lambda e: e.memset(self.ct[0:1, 66:194], 1.0),
               writes=[self.ctB_], tag="ones_row")
        self.E("pool", lambda e: e.memset(self.ct[32:33, 66:194], 1.0),
               writes=[self.ctB_], tag="ones_row32")
        self.E("pool", lambda e: e.memset(self.ct[64:65, 66:194], 1.0),
               writes=[self.ctB_], tag="ones_row64")
        self.E("pool", lambda e: e.memset(self.ctb[0:1, 0:128], 1.0),
               writes=[self.ctbB], tag="ones_row16")
        self.E("pool", lambda e: e.memset(self.ctb[:, 128:129], 1.0),
               writes=[self.ctbB], tag="ones_col16")
        self.ones_col = self.ct[:, 64:65]
        self.ones_row = self.ct[0:1, 66:194]
        self.ones_row32 = self.ct[32:33, 66:194]
        self.ones_row64 = self.ct[64:65, 66:194]
        self.ones_row16 = self.ctb[0:1, 0:128]
        self.ones_col16 = self.ctb[:, 128:129]

    def alloc(self, name, shape, dt):
        t = self.nc.alloc_sbuf_tensor(f"{name}_{self.tag}", list(shape), dt)
        b = self.kb.buf(name)
        return t, b

    def pr(self, p0, p1, c0, c1):
        return self.ps[p0:p1, c0:c1]

    def pb(self, c0, c1):
        return [self.psB[i] for i in range(c0 // 512, (c1 + 511) // 512)]

    def din(self, name, arr, dt=F32, percore=False):
        self.inp[name] = arr
        shape = list(arr[0].shape if percore else arr.shape)
        return self.nc.dram_tensor(name, shape, dt, kind="ExternalInput").ap()

    def dout(self, name, shape, dt=F32):
        return self.nc.dram_tensor(name, list(shape), dt,
                                   kind="ExternalOutput").ap()

    def dint(self, name, shape, dt):
        return self.nc.dram_tensor(name, list(shape), dt, kind="Internal").ap()

    def const(self, val):
        if val not in self._const_cols:
            j = self._n_const
            self._n_const += 1
            assert j < 64
            self.kb.emit("pool",
                         lambda e, j=j, v=val: e.memset(self.ct[:, j:j + 1], v),
                         writes=[self.ctB_], tag=f"const{val}")
            self._const_cols[val] = j
        j = self._const_cols[val]
        return self.ct[:, j:j + 1]

    def finish(self):
        for e in ("sync", "pool", "act", "dve"):
            for op in self.kb.ops[e]:
                if op.kind == "d":
                    self.kb.require_done(op)
        return self.kb.finalize_and_emit()

    def run(self, trace=False):
        in_maps = []
        for core in range(B):
            m = {}
            for k, v in self.inp.items():
                m[k] = v[core] if isinstance(v, list) else v
            in_maps.append(m)
        return run_bass_kernel_spmd(self.nc, in_maps, core_ids=list(range(B)),
                                    trace=trace)


def load_x(c, x_dram):
    for i in range(NCH):
        c.E("sync", lambda e, i=i: e.dma_start(
            c.xc[i][:], x_dram[i * 128:(i + 1) * 128, :]),
            writes=[c.xcB[i]], dma=True, tag=f"ldx{i}")


def store_x(c, y_dram):
    for i in range(NCH):
        c.E("sync", lambda e, i=i: e.dma_start(
            y_dram[i * 128:(i + 1) * 128, :], c.xc[i][:]),
            reads=[c.xcB[i]], dma=True, tag=f"stx{i}")


def emit_ln(c, src, srcB, out, outB, eps=LN_EPS, src16=False, gb=None):
    E = c.E
    gA = c.pb(0, 2048)
    gB = c.pb(2048, 4096)
    ocol = c.ones_col16 if src16 else c.ones_col
    ocolB = c.ctbB if src16 else c.ctB_
    for i in range(NCH):
        E("act", lambda e, i=i: e.square(c.sf[:], src[i][:]),
          reads=[srcB[i]], writes=[c.sfB], tag="ln_sq")
        for j in range(4):
            E("pe", lambda e, i=i, j=j: e.matmul(
                c.pr(0, 1, j * 512, (j + 1) * 512), ocol,
                src[i][:, j * 512:(j + 1) * 512],
                start=(i == 0), stop=(i == NCH - 1)),
              reads=[srcB[i], ocolB], writes=gA, tag="ln_s")
            E("pe", lambda e, i=i, j=j: e.matmul(
                c.pr(32, 33, j * 512, (j + 1) * 512), c.ones_col,
                c.sf[:, j * 512:(j + 1) * 512],
                start=(i == 0), stop=(i == NCH - 1)),
              reads=[c.sfB, c.ctB_], writes=gA, tag="ln_s2")
    r_mean = c.sf[0:1, :]
    r_a = c.ws[0:1, 0:4096].bitcast(F32)
    r_b = c.ws[0:1, 4096:8192].bitcast(F32)
    E("dve", lambda e: e.tensor_scalar(r_mean, c.pr(0, 1, 0, 2048),
                                       1.0 / D, None, ALU.mult),
      reads=gA, writes=[c.sfB], tag="ln_mean")
    E("dve", lambda e: e.tensor_scalar(r_a, c.pr(32, 33, 0, 2048),
                                       1.0 / D, None, ALU.mult),
      reads=gA, writes=[c.wsB], tag="ln_ms")
    E("dve", lambda e: e.scalar_tensor_tensor(
        r_b, r_mean, -1.0, r_mean, ALU.mult, ALU.mult),
      reads=[c.sfB], writes=[c.wsB], tag="ln_nm2")
    E("dve", lambda e: e.tensor_tensor(r_a, r_a, r_b, ALU.add),
      reads=[c.wsB], writes=[c.wsB], tag="ln_var")
    eps_col = c.const(eps)
    E("act", lambda e: e.activation(r_b, r_a, AF.Ln,
                                    bias=eps_col[0:1, :]),
      reads=[c.wsB, c.ctB_], writes=[c.wsB], tag="ln_ln")
    E("act", lambda e: e.activation(r_a, r_b, AF.Exp, scale=-0.5),
      reads=[c.wsB], writes=[c.wsB], tag="ln_rstd")
    E("dve", lambda e: e.scalar_tensor_tensor(
        r_b, r_mean, -1.0, r_a, ALU.mult, ALU.mult),
      reads=[c.sfB, c.wsB], writes=[c.wsB], tag="ln_b")
    for j in range(4):
        E("pe", lambda e, j=j: e.matmul(
            c.pr(0, 128, j * 512, (j + 1) * 512), c.ones_row,
            c.ws[0:1, j * 1024:j * 1024 + 1024].bitcast(F32),
            start=True, stop=True),
          reads=[c.wsB, c.ctB_], writes=gA, tag="ln_bcA")
        E("pe", lambda e, j=j: e.matmul(
            c.pr(0, 128, 2048 + j * 512, 2048 + (j + 1) * 512), c.ones_row,
            c.ws[0:1, 4096 + j * 1024:4096 + j * 1024 + 1024].bitcast(F32),
            start=True, stop=True),
          reads=[c.wsB, c.ctB_], writes=gB, tag="ln_bcB")
    for i in range(NCH):
        E("dve", lambda e, i=i: e.tensor_tensor(
            c.sf[:], src[i][:], c.pr(0, 128, 0, 2048), ALU.mult),
          reads=[srcB[i]] + gA, writes=[c.sfB], tag="ln_n1")
        if gb is None:
            E("dve", lambda e, i=i: e.tensor_tensor(
                out[i][:], c.sf[:], c.pr(0, 128, 2048, 4096), ALU.add),
              reads=[c.sfB] + gB, writes=[outB[i]], tag="ln_n2")
        else:
            gt, gtB, go, bo_ = gb
            E("dve", lambda e, i=i: e.tensor_tensor(
                c.sf[:], c.sf[:], c.pr(0, 128, 2048, 4096), ALU.add),
              reads=[c.sfB] + gB, writes=[c.sfB], tag="ln_n2")
            E("act", lambda e, i=i, gt=gt, go=go: e.activation(
                c.sf[:], c.sf[:], AF.Copy, scale=gt[:, go + i:go + i + 1]),
              reads=[c.sfB, gtB], writes=[c.sfB], tag="ln_g")
            E("dve", lambda e, i=i, gt=gt, bo_=bo_: e.tensor_scalar(
                out[i][:], c.sf[:], gt[:, bo_ + i:bo_ + i + 1], None,
                ALU.add),
              reads=[c.sfB, gtB], writes=[outB[i]], tag="ln_b2")


def stage_cast(c, dram_ap, r0, rows, c0, cols, dest, destB, doff, tag,
               sf_off=0):
    E = c.E
    E("sync", lambda e: e.dma_start(
        c.sf[0:rows, sf_off:sf_off + cols],
        dram_ap[r0:r0 + rows, c0:c0 + cols]),
      writes=[c.sfB], dma=True, tag=f"{tag}_ld")
    E("pool", lambda e: e.tensor_copy(dest[0:rows, doff:doff + cols],
                                      c.sf[0:rows, sf_off:sf_off + cols]),
      reads=[c.sfB], writes=[destB], tag=f"{tag}_cast")


def emit_proj(c, wT_dram, src, srcB, nout, consume, tag, nin=NCH,
              wq_width=512):
    E = c.E
    n_waves = (nout + 3) // 4
    for w in range(n_waves):
        ocs = list(range(w * 4, min(w * 4 + 4, nout)))
        wwidth = len(ocs) * 128
        half = (w % 2) * 4096
        for ic in range(nin):
            stage_cast(c, wT_dram, ic * 128, 128, ocs[0] * 128, wwidth,
                       c.ws, c.wsB, half + ic * 512, f"{tag}w{w}i{ic}")
        for tq in range(4):
            pbase = 0 if tq % 2 == 0 else 2048
            guards = c.pb(pbase, pbase + wwidth)
            for ic in range(nin):
                for k, oc in enumerate(ocs):
                    E("pe", lambda e, ic=ic, k=k, tq=tq, half=half,
                        pbase=pbase: e.matmul(
                          c.pr(0, 128, pbase + k * 512, pbase + k * 512 + 512),
                          c.ws[:, half + ic * 512 + k * 128:
                               half + ic * 512 + k * 128 + 128],
                          src[ic][:, tq * 512:(tq + 1) * 512],
                          start=(ic == 0), stop=(ic == nin - 1)),
                      reads=[c.wsB, srcB[ic]], writes=guards,
                      tag=f"{tag}_mm")
            for k, oc in enumerate(ocs):
                consume(oc, tq * 512, 512, pbase + k * 512,
                        c.pb(pbase + k * 512, pbase + k * 512 + 512))


def emit_l1_ffn(c, W, it=0):
    E = c.E
    tg = f"f{it}"
    emit_ln(c, c.xc, c.xcB, c.xl, c.xlB)
    mu, muB = c.alloc(f"mu{it}", [128, 16], F32)
    om, omB = c.alloc(f"om{it}", [128, 16], F32)
    E("sync", lambda e: e.dma_start(mu[:], W["mu"]), writes=[muB], dma=True)
    E("dve", lambda e: e.tensor_scalar(om[:], mu[:], -1.0, 1.0, ALU.mult,
                                       ALU.add),
      reads=[muB], writes=[omB], tag="om")
    for i in range(NCH):
        E("dve", lambda e, i=i: e.tensor_scalar(
            c.ua[i][:], c.xl[i][:], mu[:, i:i + 1], None, ALU.mult),
          reads=[c.xlB[i], muB], writes=[c.uaB[i]], tag="xr1")
        E("dve", lambda e, i=i: e.scalar_tensor_tensor(
            c.ua[i][:, 1:2048], c.xl[i][:, 0:2047], om[:, i:i + 1],
            c.ua[i][:, 1:2048], ALU.mult, ALU.add),
          reads=[c.xlB[i], omB, c.uaB[i]], writes=[c.uaB[i]], tag="xr2")
        E("dve", lambda e, i=i: e.tensor_scalar(
            c.sa[:, i * 2048:(i + 1) * 2048], c.xl[i][:],
            mu[:, 8 + i:9 + i], None, ALU.mult),
          reads=[c.xlB[i], muB, c.saB], writes=[c.saB], tag="xk1")
        E("dve", lambda e, i=i: e.scalar_tensor_tensor(
            c.sa[:, i * 2048 + 1:(i + 1) * 2048],
            c.xl[i][:, 0:2047], om[:, 8 + i:9 + i],
            c.sa[:, i * 2048 + 1:(i + 1) * 2048], ALU.mult, ALU.add),
          reads=[c.xlB[i], omB, c.saB], writes=[c.saB], tag="xk2")
    sa_chunks = [c.sa[:, i * 2048:(i + 1) * 2048] for i in range(NCH)]
    saBl = [c.saB] * NCH

    def consume_k(oc, tq0, width, c0, guards):
        E("act", lambda e: e.activation(c.xl[oc][:, tq0:tq0 + width],
                                        c.pr(0, 128, c0, c0 + width), AF.Relu),
          reads=guards, writes=[c.xlB[oc]], tag="k_relu")

    emit_proj(c, W["wkT"], sa_chunks, saBl, NCH, consume_k, f"{tg}k")
    for i in range(NCH):
        E("pool", lambda e, i=i: e.tensor_tensor(c.xl[i][:], c.xl[i][:],
                                                 c.xl[i][:], ALU.mult),
          reads=[c.xlB[i]], writes=[c.xlB[i]], tag="k_sq")
    ua_l = [c.ua[i][:] for i in range(NCH)]

    def consume_r(oc, tq0, width, c0, guards):
        E("act", lambda e: e.activation(
            c.sa[:, oc * 2048 + tq0:oc * 2048 + tq0 + width],
            c.pr(0, 128, c0, c0 + width), AF.Sigmoid),
          reads=guards, writes=[c.saB], tag="r_sig")

    emit_proj(c, W["wrT"], ua_l, c.uaB, NCH, consume_r, f"{tg}r")
    xl_l = [c.xl[i][:] for i in range(NCH)]

    def consume_v(oc, tq0, width, c0, guards):
        E("dve", lambda e: e.tensor_tensor(
            c.sb[:, 0:width], c.pr(0, 128, c0, c0 + width),
            c.sa[:, oc * 2048 + tq0:oc * 2048 + tq0 + width], ALU.mult),
          reads=guards + [c.saB], writes=[c.sbB], tag="v_gate")
        E("dve", lambda e: e.tensor_tensor(
            c.xc[oc][:, tq0:tq0 + width], c.xc[oc][:, tq0:tq0 + width],
            c.sb[:, 0:width], ALU.add),
          reads=[c.sbB, c.xcB[oc]], writes=[c.xcB[oc]], tag="v_res")

    emit_proj(c, W["wvT"], xl_l, c.xlB, NCH, consume_v, f"{tg}v")


def prep_l1_ffn(inputs):
    f32 = np.float32
    mu = np.stack([inputs["L1_ffn_mu_r"], inputs["L1_ffn_mu_k"]])
    mu_pp = np.ascontiguousarray(
        mu.reshape(2, 8, 128).transpose(2, 0, 1).reshape(128, 16), f32)
    return {
        "wrT": np.ascontiguousarray(inputs["L1_ffn_W_r"].T, f32),
        "wkT": np.ascontiguousarray(inputs["L1_ffn_W_k"].T, f32),
        "wvT": np.ascontiguousarray(inputs["L1_ffn_W_v"].T, f32),
        "mu": mu_pp,
    }


def emit_l0_attn_b(c, W):
    E = c.E
    emit_ln(c, c.xc, c.xcB, c.xl, c.xlB)
    E("sync", lambda e: e.dma_start(c.ctb[:, 512:640], W["esel"]),
      writes=[c.ctbB], dma=True, tag="esel")
    E("sync", lambda e: e.dma_start(c.ctb[0:16, 640:1664], W["eselT"]),
      writes=[c.ctbB], dma=True, tag="eselT")
    bpr, bprB = c.alloc("bpr", [128, 8], F32)
    E("sync", lambda e: e.dma_start(bpr[:], W["bproj_pp"]), writes=[bprB],
      dma=True)
    STG = 2048
    TMP = 2560
    TMP2 = 3072
    KP = 3584
    def proj_chunk(oc, tq, pcol):
        for ic in range(NCH):
            off = (ic % 4) * 512
            E("pe", lambda e, ic=ic, oc=oc, tq=tq, pcol=pcol: e.matmul(
                c.pr(0, 128, pcol, pcol + 512),
                c.ws[:, (oc % 8) * 1024 + ic * 128:(oc % 8) * 1024 + ic * 128 + 128],
                c.xl[ic][:, tq * 512:(tq + 1) * 512],
                start=(ic == 0), stop=(ic == NCH - 1)),
              reads=[c.wsB, c.xlB[ic]], writes=c.pb(pcol, pcol + 512),
              tag="s3_mm")

    def stage_w(oc):
        for ic in range(NCH):
            stage_cast(c, W["wqkv"], ic * 128, 128, oc * 128, 128,
                       c.ws, c.wsB, (oc % 8) * 1024 + ic * 128,
                       f"s3w{oc}_{ic}", sf_off=(ic % 4) * 512)

    guards_vs = c.pb(2048, 4096)
    for j in range(NCH):
        oc = 16 + j
        stage_w(oc)
        for tq in range(4):
            pcol = (tq % 2) * 512
            proj_chunk(oc, tq, pcol)
            gv = c.pb(pcol, pcol + 512)
            E("act", lambda e, tq=tq, pcol=pcol: e.copy(
                c.sb[:, STG:STG + 512], c.pr(0, 128, pcol, pcol + 512)),
              reads=gv, writes=[c.sbB], tag="s3_vev")
            E("pe", lambda e, j=j, tq=tq: e.matmul(
                c.pr(0, 16, 2048 + tq * 512, 2048 + (tq + 1) * 512),
                c.ctb[:, 512 + j * 16:512 + (j + 1) * 16],
                c.sb[:, STG:STG + 512],
                start=(j == 0), stop=(j == NCH - 1)),
              reads=[c.ctbB, c.sbB], writes=guards_vs, tag="s3_vsum")
    E("act", lambda e: e.copy(c.sb[0:16, 0:2048], c.pr(0, 16, 2048, 4096)),
      reads=guards_vs, writes=[c.sbB], tag="s3_vsum_ev")

    def elu1(pcol, dst_off):
        g = c.pb(pcol, pcol + 512)
        E("dve", lambda e: e.tensor_scalar(c.sb[:, TMP:TMP + 512],
                                           c.pr(0, 128, pcol, pcol + 512),
                                           0.0, None, ALU.min),
          reads=g, writes=[c.sbB], tag="s3_min")
        E("act", lambda e: e.activation(c.sb[:, TMP2:TMP2 + 512],
                                        c.sb[:, TMP:TMP + 512], AF.Exp),
          reads=[c.sbB], writes=[c.sbB], tag="s3_exp")
        E("act", lambda e: e.activation(c.sb[:, TMP:TMP + 512],
                                        c.pr(0, 128, pcol, pcol + 512),
                                        AF.Relu),
          reads=g + [c.sbB], writes=[c.sbB], tag="s3_relu")
        E("dve", lambda e: e.tensor_tensor(c.sb[:, dst_off:dst_off + 512],
                                           c.sb[:, TMP:TMP + 512],
                                           c.sb[:, TMP2:TMP2 + 512], ALU.add),
          reads=[c.sbB], writes=[c.sbB], tag="s3_add1")

    for j in range(NCH):
        oc = 8 + j
        stage_w(oc)
        for tq in range(4):
            pcol = (tq % 2) * 512
            proj_chunk(oc, tq, pcol)
            elu1(pcol, KP)
            E("pe", lambda e, j=j, tq=tq: e.matmul(
                c.pr(0, 128, 2560, 3072),
                c.ctb[0:16, 640 + j * 128:640 + (j + 1) * 128],
                c.sb[0:16, tq * 512:(tq + 1) * 512],
                start=True, stop=True),
              reads=[c.ctbB, c.sbB], writes=c.pb(2560, 3072), tag="s3_vbc")
            E("dve", lambda e, tq=tq: e.scalar_tensor_tensor(
                c.sb[:, TMP:TMP + 512], c.sb[:, KP:KP + 512], 1.0,
                c.pr(0, 128, 2560, 3072), ALU.mult, ALU.mult,
                accum_out=c.ct[:, 200 + tq:201 + tq]),
              reads=[c.sbB] + c.pb(2560, 3072), writes=[c.sbB, c.ctB_],
              tag="s3_kv")
        E("dve", lambda e, j=j: e.reduce_sum(
            c.ct[:, 240 + j:241 + j], c.ct[:, 200:204], axis=AX.X),
          reads=[c.ctB_], writes=[c.ctB_], tag="s3_kvsum")
    for j in range(NCH):
        oc = j
        stage_w(oc)
        for tq in range(4):
            pcol = (tq % 2) * 512
            proj_chunk(oc, tq, pcol)
            elu1(pcol, KP)
            E("pe", lambda e, j=j, tq=tq: e.matmul(
                c.pr(0, 16, 2048 + tq * 512, 2048 + (tq + 1) * 512),
                c.ctb[:, 512 + j * 16:512 + (j + 1) * 16],
                c.sb[:, KP:KP + 512],
                start=(j == 0), stop=(j == NCH - 1)),
              reads=[c.ctbB, c.sbB], writes=guards_vs, tag="s3_qsum")
    E("act", lambda e: e.copy(c.sb[0:16, 0:2048], c.pr(0, 16, 2048, 4096)),
      reads=guards_vs, writes=[c.sbB], tag="s3_qsum_ev")
    for j in range(NCH):
        for tq in range(4):
            E("pe", lambda e, j=j, tq=tq: e.matmul(
                c.pr(0, 128, 2560, 3072),
                c.ctb[0:16, 640 + j * 128:640 + (j + 1) * 128],
                c.sb[0:16, tq * 512:(tq + 1) * 512],
                start=True, stop=True),
              reads=[c.ctbB, c.sbB], writes=c.pb(2560, 3072), tag="s3_qbc")
            E("act", lambda e, j=j, tq=tq: e.activation(
                c.ua[j][:, tq * 512:(tq + 1) * 512],
                c.pr(0, 128, 2560, 3072), AF.Copy,
                scale=c.ct[:, 240 + j:241 + j]),
              reads=c.pb(2560, 3072) + [c.ctB_], writes=[c.uaB[j]],
              tag="s3_out")
    ua_l = [c.ua[i][:] for i in range(NCH)]

    def consume_o(oc, tq0, width, c0, guards):
        E("dve", lambda e: e.scalar_tensor_tensor(
            c.xc[oc][:, tq0:tq0 + width], c.pr(0, 128, c0, c0 + width),
            bpr[:, oc:oc + 1], c.xc[oc][:, tq0:tq0 + width],
            ALU.add, ALU.add),
          reads=guards + [bprB, c.xcB[oc]], writes=[c.xcB[oc]], tag="s3_res")

    emit_proj(c, W["wproj"], ua_l, c.uaB, NCH, consume_o, "s3o")


def prep_l0_attn_b(inputs):
    f32 = np.float32
    esel = np.zeros((8, 128, 16), f32)
    eselT = np.zeros((8, 16, 128), f32)
    for j in range(8):
        for p in range(128):
            h = 2 * j + (p // 64)
            esel[j, p, h] = 1.0
            eselT[j, h, p] = 1.0
    return {
        "wqkv": np.ascontiguousarray(inputs["L0_attn_b_W_qkv"], f32),
        "wproj": np.ascontiguousarray(inputs["L0_attn_b_W_proj"], f32),
        "bproj_pp": np.ascontiguousarray(
            inputs["L0_attn_b_b_proj"].reshape(8, 128).T, f32),
        "esel": np.ascontiguousarray(
            esel.transpose(1, 0, 2).reshape(128, 128)).astype(BF),
        "eselT": np.ascontiguousarray(
            eselT.transpose(1, 0, 2).reshape(16, 1024)).astype(BF),
    }


def emit_mha(c, W, tag, masked=False):
    E = c.E
    CSH = 12.0
    csh_col = c.const(-CSH)
    c60_col = c.const(-60.0)
    TRI = W.get("tri_off", 2048 - 128)
    KKD, KKC = 536, 552
    for ic in range(NCH):
        stage_cast(c, W["wv2"], ic * 128, 128, 0, 1024, c.ws, c.wsB,
                   ic * 1024, f"{tag}wv{ic}", sf_off=(ic % 2) * 1024)
    if W.get("bv_row") is not None:
        E("sync", lambda e: e.dma_start(c.sf[0:1, 0:1024], W["bv_row"]),
          writes=[c.sfB], dma=True, tag="bv_ld")
        E("pool", lambda e: e.tensor_copy(c.sb[0:1, 0:1024],
                                          c.sf[0:1, 0:1024]),
          reads=[c.sfB], writes=[c.sbB], tag="bv_cast")
    for kt in range(16):
        pc = (kt % 2) * 1024
        g = c.pb(pc, pc + 1024)
        for ic in range(NCH):
            for half in range(2):
                E("pe", lambda e, ic=ic, half=half, kt=kt, pc=pc: e.matmul(
                    c.pr(0, 128, pc + half * 512, pc + half * 512 + 512),
                    c.xl[ic][:, kt * 128:(kt + 1) * 128],
                    c.ws[:, ic * 1024 + half * 512:
                         ic * 1024 + half * 512 + 512],
                    start=(ic == 0), stop=(ic == NCH - 1 and
                                           W.get("bv_row") is None)),
                  reads=[c.xlB[ic], c.wsB], writes=g, tag=f"{tag}_vmm")
        if W.get("bv_row") is not None:
            for half in range(2):
                E("pe", lambda e, half=half, pc=pc: e.matmul(
                    c.pr(0, 128, pc + half * 512, pc + half * 512 + 512),
                    c.ones_row16,
                    c.sb[0:1, half * 512:half * 512 + 512],
                    start=False, stop=True),
                  reads=[c.sbB], writes=g, tag=f"{tag}_vbias")
        base = kt * 1056
        E("pool", lambda e, base=base: e.memset(
            c.sa[:, base:base + 1056].rearrange(
                "p (h x) -> p h x", x=66)[:, :, 64:66], 1.0),
          writes=[c.saB], tag=f"{tag}_vones")
        E("act", lambda e, base=base, pc=pc: e.activation(
            c.sa[:, base:base + 1056].rearrange(
                "p (h x) -> p h x", x=66)[:, :, 0:64],
            c.pr(0, 128, pc, pc + 1024).rearrange(
                "p (h x) -> p h x", x=64),
            AF.Copy),
          reads=g + [c.saB], writes=[c.saB], tag=f"{tag}_vev")
    QO, KO, PT = 0, 2048, 4096
    for gi in range(8):
        for ic in range(NCH):
            stage_cast(c, W["wq2"], ic * 128, 128, gi * 128, 128,
                       c.ws, c.wsB, ic * 128, f"{tag}q{gi}_{ic}",
                       sf_off=(ic % 4) * 512)
            if not W.get("k_in_ua"):
                stage_cast(c, W["wk2"], ic * 128, 128, gi * 128, 128,
                           c.ws, c.wsB, 1024 + ic * 128, f"{tag}k{gi}_{ic}",
                           sf_off=(ic % 4) * 512)
        for which, wsoff, dst, bias in (
                ("q", 0, QO, W.get("bq_pp")),
                ("k", 1024, KO, W.get("bk_pp"))):
            if which == "k" and W.get("k_in_ua"):
                continue
            for tq in range(4):
                pc = 2048 + (tq % 2) * 512
                g = c.pb(pc, pc + 512)
                for ic in range(NCH):
                    E("pe", lambda e, ic=ic, tq=tq, pc=pc, wsoff=wsoff:
                        e.matmul(
                          c.pr(0, 128, pc, pc + 512),
                          c.ws[:, wsoff + ic * 128:wsoff + ic * 128 + 128],
                          c.xl[ic][:, tq * 512:(tq + 1) * 512],
                          start=(ic == 0), stop=(ic == NCH - 1)),
                      reads=[c.wsB, c.xlB[ic]], writes=g,
                      tag=f"{tag}_{which}mm")
                if bias is not None:
                    E("dve", lambda e, tq=tq, pc=pc, dst=dst, gi=gi, b=bias:
                        e.tensor_scalar(
                          c.sb[:, dst + tq * 512:dst + (tq + 1) * 512],
                          c.pr(0, 128, pc, pc + 512),
                          b[0][:, gi:gi + 1], None, ALU.add),
                      reads=g + [bias[1]], writes=[c.sbB],
                      tag=f"{tag}_{which}b")
                else:
                    E("act", lambda e, tq=tq, pc=pc, dst=dst: e.copy(
                        c.sb[:, dst + tq * 512:dst + (tq + 1) * 512],
                        c.pr(0, 128, pc, pc + 512)),
                      reads=g, writes=[c.sbB], tag=f"{tag}_{which}ev")
        for hh in range(2):
            h = gi * 2 + hh
            kb = hh * 64
            for kt in range(16):
                q0 = kt * 128
                blocks = []
                nb = min((q0 // 512 + 1) * 512, 2048)
                blocks.append((q0, nb))
                while nb < 2048:
                    blocks.append((nb, min(nb + 512, 2048)))
                    nb += 512
                for bi, (qa, qb) in enumerate(blocks):
                    pc = (kt % 2) * 512
                    g = c.pb(pc, pc + 512)
                    if W.get("k_in_ua"):
                        lk = c.ua[gi][kb:kb + 64, kt * 128:(kt + 1) * 128]
                        lkB = [c.uaB[gi]]
                    else:
                        lk = c.sb[kb:kb + 64,
                                  KO + kt * 128:KO + (kt + 1) * 128]
                        lkB = [c.sbB]
                    E("pe", lambda e, lk=lk, qa=qa, qb=qb, pc=pc, kb=kb:
                        e.matmul(c.pr(0, 128, pc, pc + qb - qa), lk,
                                 c.sb[kb:kb + 64, QO + qa:QO + qb],
                                 start=True, stop=True),
                      reads=lkB + [c.sbB], writes=g, tag=f"{tag}_smm")
                    po = PT + (bi % 2) * 512
                    if not masked:
                        E("act", lambda e, pc=pc, po=po, qa=qa, qb=qb:
                            e.activation(c.sb[:, po:po + qb - qa],
                                         c.pr(0, 128, pc, pc + qb - qa),
                                         AF.Exp, bias=csh_col,
                                         scale=0.125),
                          reads=g + [c.ctB_], writes=[c.sbB],
                          tag=f"{tag}_exp")
                    else:
                        E("act", lambda e, pc=pc, qa=qa, qb=qb, kt=kt:
                            e.activation(c.sf[:, 0:qb - qa],
                                         c.pr(0, 128, pc, pc + qb - qa),
                                         AF.Copy,
                                         scale=c.ct[:, KKD + kt:KKD + kt + 1]),
                          reads=g + [c.ctB_], writes=[c.sfB],
                          tag=f"{tag}_t1")
                        E("dve", lambda e, qa=qa, qb=qb, kt=kt:
                            e.tensor_scalar(c.sf[:, 512:512 + qb - qa],
                                            c.sf[:, 0:qb - qa],
                                            c.ct[:, KKC + kt:KKC + kt + 1],
                                            None, ALU.add),
                          reads=[c.sfB, c.ctB_], writes=[c.sfB],
                          tag=f"{tag}_t1b")
                        E("pool", lambda e, qa=qa, qb=qb:
                            e.tensor_tensor(c.sf[:, 0:qb - qa],
                                            c.sf[:, 512:512 + qb - qa],
                                            c.sb[:, 2048 + qa:2048 + qb],
                                            ALU.mult),
                          reads=[c.sfB, c.sbB], writes=[c.sfB],
                          tag=f"{tag}_t2")
                        E("act", lambda e, po=po, qa=qa, qb=qb:
                            e.activation(c.sb[:, po:po + qb - qa],
                                         c.sf[:, 0:qb - qa],
                                         AF.Exp, bias=c60_col),
                          reads=[c.sfB, c.ctB_], writes=[c.sbB],
                          tag=f"{tag}_exp")
                    if bi == 0:
                        E("pool", lambda e, po=po: e.tensor_tensor(
                            c.sb[:, po:po + 128], c.sb[:, po:po + 128],
                            c.ctb[:, TRI:TRI + 128], ALU.mult),
                          reads=[c.sbB, c.ctbB], writes=[c.sbB],
                          tag=f"{tag}_tri")
                    E("pe", lambda e, kt=kt, h=h, po=po, qa=qa, qb=qb:
                        e.matmul(
                          c.pr(0, 65, 2048 + qa, 2048 + qb),
                          c.sa[:, kt * 1056 + h * 66:
                               kt * 1056 + h * 66 + 65],
                          c.sb[:, po:po + qb - qa],
                          start=(kt == 0),
                          stop=(kt == min(15, (qb - 1) // 128))),
                      reads=[c.saB, c.sbB],
                      writes=c.pb(2048 + qa, 2048 + qb), tag=f"{tag}_av")
            gctx = c.pb(2048, 4096)
            E("dve", lambda e: e.reciprocal(c.sf[0:1, :],
                                            c.pr(64, 65, 2048, 4096)),
              reads=gctx, writes=[c.sfB], tag=f"{tag}_rcp")
            for j in range(4):
                E("pe", lambda e, j=j: e.matmul(
                    c.pr(0, 64, j * 512, (j + 1) * 512),
                    c.ones_row[0:1, 0:64],
                    c.sf[0:1, j * 512:(j + 1) * 512], start=True, stop=True),
                  reads=[c.sfB, c.ctB_],
                  writes=c.pb(j * 512, (j + 1) * 512), tag=f"{tag}_rbc")
            E("act", lambda e: e.copy(c.sf[0:64, :], c.pr(0, 64, 0, 2048)),
              reads=c.pb(0, 2048), writes=[c.sfB], tag=f"{tag}_rbe")
            E("dve", lambda e, gi=gi, hh=hh: e.tensor_tensor(
                c.ua[gi][hh * 64:(hh + 1) * 64, :],
                c.pr(0, 64, 2048, 4096), c.sf[0:64, :], ALU.mult),
              reads=gctx + [c.sfB], writes=[c.uaB[gi]], tag=f"{tag}_ctx")
    ua_l = [c.ua[i][:] for i in range(NCH)]
    bo = W.get("bo_pp")

    def consume_o(oc, tq0, width, c0, guards):
        if bo is not None:
            E("dve", lambda e: e.scalar_tensor_tensor(
                c.xc[oc][:, tq0:tq0 + width], c.pr(0, 128, c0, c0 + width),
                bo[0][:, oc:oc + 1], c.xc[oc][:, tq0:tq0 + width],
                ALU.add, ALU.add),
              reads=guards + [bo[1], c.xcB[oc]], writes=[c.xcB[oc]],
              tag=f"{tag}_res")
        else:
            E("dve", lambda e: e.tensor_tensor(
                c.xc[oc][:, tq0:tq0 + width],
                c.pr(0, 128, c0, c0 + width),
                c.xc[oc][:, tq0:tq0 + width], ALU.add),
              reads=guards + [c.xcB[oc]], writes=[c.xcB[oc]],
              tag=f"{tag}_res")

    emit_proj(c, W["wo"], ua_l, c.uaB, NCH, consume_o, f"{tag}o")


def emit_l0_attn_a(c, W):
    emit_ln(c, c.xc, c.xcB, c.xl, c.xlB)
    bq, bqB = c.alloc("bq", [128, 8], F32)
    bk, bkB = c.alloc("bk", [128, 8], F32)
    bo, boB = c.alloc("bo", [128, 8], F32)
    E = c.E
    E("sync", lambda e: e.dma_start(bq[:], W["bq_pp"]), writes=[bqB], dma=True)
    E("sync", lambda e: e.dma_start(bk[:], W["bk_pp"]), writes=[bkB], dma=True)
    E("sync", lambda e: e.dma_start(bo[:], W["bo_pp"]), writes=[boB], dma=True)
    E("sync", lambda e: e.dma_start(c.ctb[:, 2048 - 128:2048], W["tri"]),
      writes=[c.ctbB], dma=True, tag="tri")
    W2 = dict(W)
    W2["bq_pp"] = (bq, bqB)
    W2["bk_pp"] = (bk, bkB)
    W2["bo_pp"] = (bo, boB)
    emit_mha(c, W2, "s1", masked=False)


def prep_l0_attn_a(inputs):
    f32 = np.float32
    tri = np.triu(np.ones((128, 128), f32)).astype(BF)
    return {
        "wq2": np.ascontiguousarray(
            inputs["L0_attn_a_Wq"].transpose(1, 0, 2).reshape(1024, 1024),
            f32),
        "wk2": np.ascontiguousarray(
            inputs["L0_attn_a_Wk"].transpose(1, 0, 2).reshape(1024, 1024),
            f32),
        "wv2": np.ascontiguousarray(
            inputs["L0_attn_a_Wv"].transpose(1, 0, 2).reshape(1024, 1024),
            f32),
        "wo": np.ascontiguousarray(inputs["L0_attn_a_Wo"], f32),
        "bq_pp": np.ascontiguousarray(
            inputs["L0_attn_a_bq"].reshape(8, 128).T, f32),
        "bk_pp": np.ascontiguousarray(
            inputs["L0_attn_a_bk"].reshape(8, 128).T, f32),
        "bv_row": np.ascontiguousarray(
            inputs["L0_attn_a_bv"].reshape(1, 1024), f32),
        "bo_pp": np.ascontiguousarray(
            inputs["L0_attn_a_bo"].reshape(8, 128).T, f32),
        "tri": tri,
    }


def emit_l1_attn_a(c, W):
    E = c.E
    emit_ln(c, c.xc, c.xcB, c.xl, c.xlB)
    E("sync", lambda e: e.dma_start(c.ctb[:, 512:640], W["tri"]),
      writes=[c.ctbB], dma=True, tag="s5tri")
    bo, boB = c.alloc("s5bo", [128, 8], F32)
    E("sync", lambda e: e.dma_start(bo[:], W["bout_pp"]), writes=[boB],
      dma=True)
    br1, br1B = c.alloc("s5br1", [128, 8], F32)
    E("sync", lambda e: e.dma_start(br1[:], W["br1_pp"]), writes=[br1B],
      dma=True)
    br2, br2B = c.alloc("s5br2", [1, 8], F32)
    E("sync", lambda e: e.dma_start(br2[:], W["br2_pp"]), writes=[br2B],
      dma=True)
    wqkvT = W["wqkvT"]
    for gi in range(8):
        for ic in range(NCH):
            stage_cast(c, wqkvT, ic * 128, 128, 1024 + gi * 128, 128,
                       c.ws, c.wsB, ic * 128, f"s5K{gi}_{ic}",
                       sf_off=(ic % 4) * 512)
        for tq in range(4):
            pc = (tq % 2) * 512
            g = c.pb(pc, pc + 512)
            for ic in range(NCH):
                E("pe", lambda e, ic=ic, tq=tq, pc=pc: e.matmul(
                    c.pr(0, 128, pc, pc + 512),
                    c.ws[:, ic * 128:ic * 128 + 128],
                    c.xl[ic][:, tq * 512:(tq + 1) * 512],
                    start=(ic == 0), stop=(ic == NCH - 1)),
                  reads=[c.wsB, c.xlB[ic]], writes=g, tag="s5_kmm")
            E("act", lambda e, gi=gi, tq=tq, pc=pc: e.copy(
                c.ua[gi][:, tq * 512:(tq + 1) * 512],
                c.pr(0, 128, pc, pc + 512)),
              reads=g, writes=[c.uaB[gi]], tag="s5_kev")
        E("dve", lambda e, gi=gi: e.reduce_sum(
            c.ct[:, 264 + gi * 32:264 + (gi + 1) * 32],
            c.ua[gi][:].rearrange("p (b t) -> p b t", t=64), axis=AX.X),
          reads=[c.uaB[gi]], writes=[c.ctB_], tag="s5_blk")
    E("dve", lambda e: e.tensor_scalar(
        c.ct[:, 264:520], c.ct[:, 264:520], 1.0 / 64, None, ALU.mult),
      reads=[c.ctB_], writes=[c.ctB_], tag="s5_blkm")
    E("pool", lambda e: e.tensor_copy(c.ctb[:, 640:896], c.ct[:, 264:520]),
      reads=[c.ctB_], writes=[c.ctbB], tag="s5_blkc")
    for ic in range(NCH):
        stage_cast(c, W["wr1T"], ic * 128, 128, 0, 128,
                   c.ws, c.wsB, ic * 128, f"s5r1_{ic}",
                   sf_off=(ic % 4) * 512)
    stage_cast(c, W["wr2T8"], 0, 128, 0, 8, c.ws, c.wsB, 1024, "s5r2")
    gr = c.pb(1024, 1536)
    for ic in range(NCH):
        E("pe", lambda e, ic=ic: e.matmul(
            c.pr(0, 128, 1024, 1056),
            c.ws[:, ic * 128:ic * 128 + 128],
            c.ctb[:, 640 + ic * 32:640 + (ic + 1) * 32],
            start=(ic == 0), stop=(ic == NCH - 1)),
          reads=[c.wsB, c.ctbB], writes=gr, tag="s5_r1mm")
    E("dve", lambda e: e.tensor_scalar(
        c.sf[:, 0:32], c.pr(0, 128, 1024, 1056), br1[:, 0:1], None, ALU.add),
      reads=gr + [br1B], writes=[c.sfB], tag="s5_r1b")
    E("act", lambda e: e.activation(c.ctb[:, 896:928], c.sf[:, 0:32],
                                    AF.Relu),
      reads=[c.sfB], writes=[c.ctbB], tag="s5_r1relu")
    E("pe", lambda e: e.matmul(
        c.pr(0, 1, 1536, 1568), c.ws[:, 1024:1025], c.ctb[:, 896:928],
        start=True, stop=True),
      reads=[c.wsB, c.ctbB], writes=c.pb(1536, 2048), tag="s5_r2mm")
    E("dve", lambda e: e.tensor_scalar(
        c.sf[0:1, 0:32], c.pr(0, 1, 1536, 1568), br2[0:1, 0:1], None,
        ALU.add),
      reads=c.pb(1536, 2048) + [br2B], writes=[c.sfB], tag="s5_r2b")
    E("act", lambda e: e.sign(c.sf[0:1, 32:64], c.sf[0:1, 0:32]),
      reads=[c.sfB], writes=[c.sfB], tag="s5_sign")
    E("dve", lambda e: e.tensor_scalar(c.sf[0:1, 32:64], c.sf[0:1, 32:64],
                                       0.0, None, ALU.max),
      reads=[c.sfB], writes=[c.sfB], tag="s5_keep")
    cc_in = c.dint("s5cc_in", [1, 32], F32)
    cc_out = c.dint("s5cc_out", [8, 32], F32)
    ccinB = c.kb.buf("ccin")
    ccoutB = c.kb.buf("ccout")
    E("sync", lambda e: e.dma_start(cc_in, c.sf[0:1, 32:64]),
      reads=[c.sfB], writes=[ccinB], dma=True, tag="s5_ccw")
    E("pool", lambda e: e.collective_compute(
        "AllGather", ALU.bypass, [list(range(B))], [cc_in.opt()],
        [cc_out.opt()]),
      reads=[ccinB], writes=[ccoutB], coll=True, tag="s5_ag")
    E("sync", lambda e: e.dma_start(c.sf[0:1, 0:32], cc_out[0:1, :]),
      reads=[ccoutB], writes=[c.sfB], dma=True, tag="s5_ccr")
    k01c, k01cB = c.alloc("k01c", [32, 8], F32)
    k01b, k01bB = c.alloc("k01b", [32, 8], BF16)
    E("pool", lambda e: e.dma_start(k01c[0:32, 0:1], c.sf[0:1, 0:32],
                                    allow_slow_non_contiguous=True),
      reads=[c.sfB], writes=[k01cB], dma=True, tag="s5_k01t")
    E("pool", lambda e: e.tensor_copy(k01b[:, 0:1], k01c[:, 0:1]),
      reads=[k01cB], writes=[k01bB], tag="s5_k01c")
    E("sync", lambda e: e.dma_start(c.ws[0:32, 1024:3072], W["esel32"]),
      writes=[c.wsB], dma=True, tag="s5_e32")
    for j in range(4):
        E("pe", lambda e, j=j: e.matmul(
            c.pr(0, 1, j * 512, (j + 1) * 512), k01b[:, 0:1],
            c.ws[0:32, 1024 + j * 512:1024 + (j + 1) * 512],
            start=True, stop=True),
          reads=[k01bB, c.wsB], writes=c.pb(j * 512, (j + 1) * 512),
          tag="s5_kkrow")
    E("act", lambda e: e.copy(c.sf[0:1, :], c.pr(0, 1, 0, 2048)),
      reads=c.pb(0, 2048), writes=[c.sfB], tag="s5_kkev")
    for j in range(4):
        E("pe", lambda e, j=j: e.matmul(
            c.pr(0, 128, 2048 + j * 512, 2048 + (j + 1) * 512),
            c.ones_row, c.sf[0:1, j * 512:(j + 1) * 512],
            start=True, stop=True),
          reads=[c.sfB, c.ctB_], writes=c.pb(2048 + j * 512,
                                             2048 + (j + 1) * 512),
          tag="s5_kqbc")
    E("act", lambda e: e.copy(c.sb[:, 2048:4096], c.pr(0, 128, 2048, 4096)),
      reads=c.pb(2048, 4096), writes=[c.sbB], tag="s5_kqev")
    for kt in range(16):
        E("pool", lambda e, kt=kt: e.dma_start(
            c.ct[:, 520 + kt:521 + kt], c.sf[0:1, kt * 128:(kt + 1) * 128],
            allow_slow_non_contiguous=True),
          reads=[c.sfB], writes=[c.ctB_], dma=True, tag="s5_kkcol")
    E("dve", lambda e: e.tensor_scalar(c.ct[:, 536:552], c.ct[:, 520:536],
                                       1.0 / 32, None, ALU.mult),
      reads=[c.ctB_], writes=[c.ctB_], tag="s5_kkd")
    E("dve", lambda e: e.tensor_scalar(c.ct[:, 552:568], c.ct[:, 520:536],
                                       60.0, None, ALU.mult),
      reads=[c.ctB_], writes=[c.ctB_], tag="s5_kkc")
    W2 = dict(W)
    W2["wq2"] = wqkvT[0:1024, 0:1024]
    W2["wv2"] = wqkvT[0:1024, 2048:3072]
    W2["wo"] = W["woT"]
    W2["bo_pp"] = (bo, boB)
    W2["k_in_ua"] = True
    W2["tri_off"] = 512
    emit_mha(c, W2, "s5", masked=True)


def prep_l1_attn_a(inputs):
    f32 = np.float32
    tri = np.triu(np.ones((128, 128), f32)).astype(BF)
    esel32 = np.zeros((32, 2048), f32)
    for b_ in range(32):
        esel32[b_, b_ * 64:(b_ + 1) * 64] = 1.0
    wr2 = np.zeros((128, 8), f32)
    wr2[:, 0:1] = inputs["L1_attn_a_w_r2"].T
    br2 = np.zeros((1, 8), f32)
    br2[0, 0] = inputs["L1_attn_a_b_r2"][0]
    return {
        "wqkvT": np.ascontiguousarray(inputs["L1_attn_a_w_qkv"].T, f32),
        "wr1T": np.ascontiguousarray(inputs["L1_attn_a_w_r1"].T, f32),
        "wr2T8": wr2,
        "br1_pp": np.ascontiguousarray(
            np.repeat(inputs["L1_attn_a_b_r1"].reshape(128, 1), 8, axis=1),
            f32),
        "br2_pp": br2,
        "woT": np.ascontiguousarray(inputs["L1_attn_a_w_out"].T, f32),
        "bout_pp": np.ascontiguousarray(
            inputs["L1_attn_a_b_out"].reshape(8, 128).T, f32),
        "tri": tri,
        "esel32": esel32.astype(BF),
    }


def emit_l0_ffn(c, W, it=0):
    E = c.E
    tg = f"moe{it}"
    emit_ln(c, c.xc, c.xcB, c.xl, c.xlB)
    if W.get("spill"):
        for i in range(NCH):
            E("sync", lambda e, i=i: e.dma_start(
                W["x_dram"][i * 128:(i + 1) * 128, :], c.xc[i][:]),
              reads=[c.xcB[i]], dma=True, tag=f"{tg}_spill{i}")
    idf, idfB = c.alloc(f"idf{it}", [128, 128], F32)
    E("sync", lambda e: e.dma_start(idf[:], W["ident"]), writes=[idfB],
      dma=True)
    b1t, b1tB = c.alloc(f"b1t{it}", [128, 256], F32)
    b2t, b2tB = c.alloc(f"b2t{it}", [128, 64], F32)
    gbt, gbtB = c.alloc(f"gbt{it}", [128, 16], F32)
    bgt, bgtB = c.alloc(f"bgt{it}", [8, 8], F32)
    E("sync", lambda e: e.dma_start(b1t[:], W["b1_pt"]), writes=[b1tB],
      dma=True)
    E("sync", lambda e: e.dma_start(b2t[:], W["b2_pt"]), writes=[b2tB],
      dma=True)
    E("sync", lambda e: e.dma_start(gbt[:], W["gb_pp"]), writes=[gbtB],
      dma=True)
    E("sync", lambda e: e.dma_start(bgt[:], W["bg8"]), writes=[bgtB],
      dma=True)
    stage_cast(c, W["wg_pp"], 0, 128, 0, 64, c.ctb, c.ctbB, 512, f"{tg}wg")
    gL = c.pb(0, 2048)
    for ic in range(NCH):
        for j in range(4):
            E("pe", lambda e, ic=ic, j=j: e.matmul(
                c.pr(0, 8, j * 512, (j + 1) * 512),
                c.ctb[:, 512 + ic * 8:512 + (ic + 1) * 8],
                c.xl[ic][:, j * 512:(j + 1) * 512],
                start=(ic == 0), stop=(ic == NCH - 1)),
              reads=[c.ctbB, c.xlB[ic]], writes=gL, tag=f"{tg}_gmm")
    E("dve", lambda e: e.tensor_scalar(c.sf[0:8, :], c.pr(0, 8, 0, 2048),
                                       bgt[0:8, 0:1], None, ALU.add),
      reads=gL + [bgtB], writes=[c.sfB], tag=f"{tg}_bg")
    gT = c.pb(2048, 2560)
    for i in range(16):
        E("pe", lambda e, i=i: e.transpose(
            c.pr(0, 128, 2048 + (i % 16) * 8, 2048 + (i % 16) * 8 + 8),
            c.sf[0:8, i * 128:(i + 1) * 128], idf[0:8, 0:8]),
          reads=[c.sfB, idfB], writes=gT, tag=f"{tg}_ltr")
    E("act", lambda e: e.copy(c.ct[:, 264:392], c.pr(0, 128, 2048, 2176)),
      reads=gT, writes=[c.ctB_], tag=f"{tg}_ltr_ev")
    for i in range(16):
        L = c.ct[:, 264 + i * 8:272 + i * 8]
        CM = c.ct[:, 392 + i * 8:400 + i * 8]
        nm1 = c.ct[:, 521:522]
        m1 = c.ct[:, 520:521]
        m2 = c.ct[:, 522:523]
        den = c.ct[:, 523:524]
        sc1 = c.ct[:, 524:532]
        sc2 = c.ct[:, 532:540]
        E("dve", lambda e, L=L, nm1=nm1: e.reduce_max(nm1, L, axis=AX.X,
                                                      negate=True),
          reads=[c.ctB_], writes=[c.ctB_], tag=f"{tg}_m1")
        E("dve", lambda e, m1=m1, nm1=nm1: e.tensor_scalar(
            m1, nm1, -1.0, None, ALU.mult),
          reads=[c.ctB_], writes=[c.ctB_], tag=f"{tg}_m1b")
        E("dve", lambda e, L=L, m1=m1, sc1=sc1: e.tensor_scalar(
            sc1, L, m1, None, ALU.is_ge),
          reads=[c.ctB_], writes=[c.ctB_], tag=f"{tg}_ge1")
        E("dve", lambda e, L=L, sc1=sc1, sc2=sc2: e.scalar_tensor_tensor(
            sc2, sc1, -1e30, L, ALU.mult, ALU.add),
          reads=[c.ctB_], writes=[c.ctB_], tag=f"{tg}_lp")
        E("dve", lambda e, m2=m2, sc2=sc2: e.reduce_max(m2, sc2, axis=AX.X),
          reads=[c.ctB_], writes=[c.ctB_], tag=f"{tg}_m2")
        E("dve", lambda e, m2=m2, nm1=nm1, den=den: e.tensor_tensor(
            den, m2, nm1, ALU.add),
          reads=[c.ctB_], writes=[c.ctB_], tag=f"{tg}_dm")
        E("act", lambda e, den=den: e.activation(den, den, AF.Exp),
          reads=[c.ctB_], writes=[c.ctB_], tag=f"{tg}_de")
        E("dve", lambda e, den=den: e.tensor_scalar(den, den, 1.0, None,
                                                    ALU.add),
          reads=[c.ctB_], writes=[c.ctB_], tag=f"{tg}_d1")
        E("dve", lambda e, den=den: e.reciprocal(den, den),
          reads=[c.ctB_], writes=[c.ctB_], tag=f"{tg}_dr")
        E("dve", lambda e, L=L, m2=m2, sc1=sc1: e.tensor_scalar(
            sc1, L, m2, None, ALU.is_ge),
          reads=[c.ctB_], writes=[c.ctB_], tag=f"{tg}_ge2")
        E("act", lambda e, L=L, nm1=nm1, sc2=sc2: e.activation(
            sc2, L, AF.Exp, bias=nm1),
          reads=[c.ctB_], writes=[c.ctB_], tag=f"{tg}_ex")
        E("dve", lambda e, sc2=sc2, den=den, CM=CM: e.tensor_scalar(
            CM, sc2, den, None, ALU.mult),
          reads=[c.ctB_], writes=[c.ctB_], tag=f"{tg}_cm1")
        E("dve", lambda e, CM=CM, sc1=sc1: e.tensor_tensor(
            CM, CM, sc1, ALU.mult),
          reads=[c.ctB_], writes=[c.ctB_], tag=f"{tg}_cm2")
    gT2 = c.pb(2048, 4096)
    for i in range(16):
        E("pe", lambda e, i=i: e.transpose(
            c.pr(0, 8, 2048 + i * 128, 2048 + (i + 1) * 128),
            c.ct[:, 392 + i * 8:400 + i * 8], idf[:]),
          reads=[c.ctB_, idfB], writes=gT2, tag=f"{tg}_ctr")
    E("act", lambda e: e.copy(c.sf[0:8, :], c.pr(0, 8, 2048, 4096)),
      reads=gT2, writes=[c.sfB], tag=f"{tg}_cev")
    cmb_dram = c.dint(f"cmb{it}", [8, 2048], F32)
    cmbB = c.kb.buf(f"cmb{it}")
    E("sync", lambda e: e.dma_start(cmb_dram, c.sf[0:8, :]),
      reads=[c.sfB], writes=[cmbB], dma=True, tag=f"{tg}_cst")
    for i in range(NCH):
        E("pool", lambda e, i=i: e.tensor_copy(c.ua[i][:], c.xl[i][:]),
          reads=[c.xlB[i]], writes=[c.uaB[i]], tag=f"{tg}_ui")
    for ex in range(E_EXP):
        w1ap = W["w1"]
        w2ap = W["w2"]
        w1r = ex * 1024
        w2r = ex * 4096
        for j in range(4):
            E("pool", lambda e, ex=ex, j=j: e.dma_start(
                c.ct[0:1, 264:776],
                cmb_dram[ex:ex + 1, j * 512:(j + 1) * 512]),
              reads=[cmbB], writes=[c.ctB_], dma=True, tag=f"{tg}_crow")
            E("pe", lambda e, j=j: e.matmul(
                c.pr(0, 128, 1024, 1536), c.ones_row, c.ct[0:1, 264:776],
                start=True, stop=True),
              reads=[c.ctB_], writes=c.pb(1024, 1536), tag=f"{tg}_cbc")
            E("act", lambda e, j=j: e.copy(
                c.sb[:, j * 512:(j + 1) * 512],
                c.pr(0, 128, 1024, 1536)),
              reads=c.pb(1024, 1536), writes=[c.sbB], tag=f"{tg}_cbev")
        for hc in range(32):
            soff = (hc % 2) * 1024
            E("sync", lambda e, hc=hc, soff=soff, w2ap=w2ap, w2r=w2r:
                e.dma_start(
                  c.sf[:, soff:soff + 1024],
                  w2ap[w2r + hc * 128:w2r + (hc + 1) * 128, 0:1024]),
              writes=[c.sfB], dma=True, tag=f"{tg}_w2l")
            E("pool", lambda e, hc=hc, soff=soff: e.tensor_copy(
                c.xc[hc // 4][:].bitcast(BF16)[:, (hc % 4) * 1024:
                                               (hc % 4) * 1024 + 1024],
                c.sf[:, soff:soff + 1024]),
              reads=[c.sfB], writes=[c.xcB[hc // 4]], tag=f"{tg}_w2c")
        for tq in range(4):
            t0 = tq * 512
            for gph in range(8):
                half = (gph % 2) * 4096
                for ic in range(NCH):
                    E("sync", lambda e, ic=ic, gph=gph, half=half,
                        w1ap=w1ap, w1r=w1r: e.dma_start(
                          c.sf[:, (ic % 4) * 512:(ic % 4) * 512 + 512],
                          w1ap[w1r + ic * 128:w1r + (ic + 1) * 128,
                               gph * 512:(gph + 1) * 512]),
                      writes=[c.sfB], dma=True, tag=f"{tg}_w1l")
                    E("pool", lambda e, ic=ic, half=half: e.tensor_copy(
                        c.ws[:, half + ic * 512:half + (ic + 1) * 512],
                        c.sf[:, (ic % 4) * 512:(ic % 4) * 512 + 512]),
                      reads=[c.sfB], writes=[c.wsB], tag=f"{tg}_w1c")
                for hcl in range(4):
                    hc = gph * 4 + hcl
                    pc = (hc % 2) * 512
                    gH = c.pb(pc, pc + 512)
                    for ic in range(NCH):
                        E("pe", lambda e, ic=ic, hcl=hcl, half=half, pc=pc,
                            t0=t0: e.matmul(
                              c.pr(0, 128, pc, pc + 512),
                              c.ws[:, half + ic * 512 + hcl * 128:
                                   half + ic * 512 + hcl * 128 + 128],
                              c.xl[ic][:, t0:t0 + 512],
                              start=(ic == 0), stop=(ic == NCH - 1)),
                          reads=[c.wsB, c.xlB[ic]], writes=gH,
                          tag=f"{tg}_amm")
                    E("act", lambda e, hc=hc, pc=pc, ex=ex: e.activation(
                        c.sa[:, hc * 512:(hc + 1) * 512],
                        c.pr(0, 128, pc, pc + 512), AF.Gelu,
                        bias=b1t[:, ex * 32 + hc:ex * 32 + hc + 1]),
                      reads=c.pb(pc, pc + 512) + [b1tB], writes=[c.saB],
                      tag=f"{tg}_gelu")
            for w in range(2):
                gY = c.pb(2048, 4096)
                for hc in range(32):
                    for k in range(4):
                        dc = w * 4 + k
                        E("pe", lambda e, hc=hc, k=k, dc=dc: e.matmul(
                            c.pr(0, 128, 2048 + k * 512, 2048 + k * 512 + 512),
                            c.xc[hc // 4][:].bitcast(BF16)[
                                :, (hc % 4) * 1024 + dc * 128:
                                (hc % 4) * 1024 + dc * 128 + 128],
                            c.sa[:, hc * 512:(hc + 1) * 512],
                            start=(hc == 0), stop=(hc == 31)),
                          reads=[c.xcB[hc // 4], c.saB],
                          writes=c.pb(2048 + k * 512, 2048 + k * 512 + 512),
                          tag=f"{tg}_bmm")
                for k in range(4):
                    dc = w * 4 + k
                    gk = c.pb(2048 + k * 512, 2048 + k * 512 + 512)
                    E("dve", lambda e, k=k, dc=dc, ex=ex, t0=t0:
                        e.scalar_tensor_tensor(
                          c.sb[:, 2048 + k * 512:2048 + (k + 1) * 512],
                          c.pr(0, 128, 2048 + k * 512, 2048 + k * 512 + 512),
                          b2t[:, ex * 8 + dc:ex * 8 + dc + 1],
                          c.sb[:, t0:t0 + 512],
                          ALU.add, ALU.mult),
                      reads=gk + [b2tB, c.sbB], writes=[c.sbB],
                      tag=f"{tg}_yev")
                    E("pool", lambda e, k=k, dc=dc, t0=t0: e.tensor_tensor(
                        c.ua[dc][:, t0:t0 + 512], c.ua[dc][:, t0:t0 + 512],
                        c.sb[:, 2048 + k * 512:2048 + (k + 1) * 512],
                        ALU.add),
                      reads=[c.sbB, c.uaB[dc]], writes=[c.uaB[dc]],
                      tag=f"{tg}_uacc")
    gA = c.pb(0, 2048)
    gB = c.pb(2048, 4096)
    for i in range(NCH):
        E("pool", lambda e, i=i: e.tensor_tensor(
            c.sb[:, 0:2048], c.ua[i][:], c.ua[i][:], ALU.mult),
          reads=[c.uaB[i]], writes=[c.sbB], tag=f"{tg}_usq")
        for j in range(4):
            E("pe", lambda e, i=i, j=j: e.matmul(
                c.pr(0, 1, j * 512, (j + 1) * 512), c.ones_col16,
                c.ua[i][:, j * 512:(j + 1) * 512],
                start=(i == 0), stop=(i == NCH - 1)),
              reads=[c.uaB[i], c.ctbB], writes=gA, tag=f"{tg}_us")
            E("pe", lambda e, i=i, j=j: e.matmul(
                c.pr(32, 33, j * 512, (j + 1) * 512), c.ones_col16,
                c.sb[:, j * 512:(j + 1) * 512],
                start=(i == 0), stop=(i == NCH - 1)),
              reads=[c.sbB, c.ctbB], writes=gA, tag=f"{tg}_us2")
    r_mean = c.sf[0:1, :]
    r_a = c.ws[0:1, 0:4096].bitcast(F32)
    r_b = c.ws[0:1, 4096:8192].bitcast(F32)
    eps_col = c.const(LN_EPS)
    E("dve", lambda e: e.tensor_scalar(r_mean, c.pr(0, 1, 0, 2048),
                                       1.0 / D, None, ALU.mult),
      reads=gA, writes=[c.sfB], tag=f"{tg}_fmean")
    E("dve", lambda e: e.tensor_scalar(r_a, c.pr(32, 33, 0, 2048),
                                       1.0 / D, None, ALU.mult),
      reads=gA, writes=[c.wsB], tag=f"{tg}_fms")
    E("dve", lambda e: e.scalar_tensor_tensor(
        r_b, r_mean, -1.0, r_mean, ALU.mult, ALU.mult),
      reads=[c.sfB], writes=[c.wsB], tag=f"{tg}_fnm2")
    E("dve", lambda e: e.tensor_tensor(r_a, r_a, r_b, ALU.add),
      reads=[c.wsB], writes=[c.wsB], tag=f"{tg}_fvar")
    E("act", lambda e: e.activation(r_b, r_a, AF.Ln, bias=eps_col[0:1, :]),
      reads=[c.wsB, c.ctB_], writes=[c.wsB], tag=f"{tg}_fln")
    E("act", lambda e: e.activation(r_a, r_b, AF.Exp, scale=-0.5),
      reads=[c.wsB], writes=[c.wsB], tag=f"{tg}_frstd")
    E("dve", lambda e: e.scalar_tensor_tensor(
        r_b, r_mean, -1.0, r_a, ALU.mult, ALU.mult),
      reads=[c.sfB, c.wsB], writes=[c.wsB], tag=f"{tg}_fb")
    for j in range(4):
        E("pe", lambda e, j=j: e.matmul(
            c.pr(0, 128, j * 512, (j + 1) * 512), c.ones_row,
            c.ws[0:1, j * 1024:j * 1024 + 1024].bitcast(F32),
            start=True, stop=True),
          reads=[c.wsB, c.ctB_], writes=gA, tag=f"{tg}_fbcA")
        E("pe", lambda e, j=j: e.matmul(
            c.pr(0, 128, 2048 + j * 512, 2048 + (j + 1) * 512), c.ones_row,
            c.ws[0:1, 4096 + j * 1024:4096 + j * 1024 + 1024].bitcast(F32),
            start=True, stop=True),
          reads=[c.wsB, c.ctB_], writes=gB, tag=f"{tg}_fbcB")
    for i in range(NCH):
        E("sync", lambda e, i=i: e.dma_start(
            c.xc[i][:], W["x_dram"][i * 128:(i + 1) * 128, :]),
          writes=[c.xcB[i]], dma=True, tag=f"{tg}_xrl{i}")
        E("dve", lambda e, i=i: e.tensor_tensor(
            c.sf[:], c.ua[i][:], c.pr(0, 128, 0, 2048), ALU.mult),
          reads=[c.uaB[i]] + gA, writes=[c.sfB], tag=f"{tg}_fn1")
        E("dve", lambda e, i=i: e.tensor_tensor(
            c.sf[:], c.sf[:], c.pr(0, 128, 2048, 4096), ALU.add),
          reads=[c.sfB] + gB, writes=[c.sfB], tag=f"{tg}_fn2")
        E("act", lambda e, i=i: e.activation(
            c.sf[:], c.sf[:], AF.Copy, scale=gbt[:, i:i + 1]),
          reads=[c.sfB, gbtB], writes=[c.sfB], tag=f"{tg}_fg")
        E("dve", lambda e, i=i: e.scalar_tensor_tensor(
            c.xc[i][:], c.sf[:], gbt[:, 8 + i:9 + i], c.xc[i][:],
            ALU.add, ALU.add),
          reads=[c.sfB, gbtB, c.xcB[i]], writes=[c.xcB[i]],
          tag=f"{tg}_fres")


def prep_l0_ffn(inputs):
    f32 = np.float32
    bg8 = np.repeat(inputs["L0_ffn_bg"].reshape(8, 1), 8, axis=1)
    gb = np.stack([inputs["L0_ffn_gamma"], inputs["L0_ffn_beta"]])
    return {
        "wg_pp": np.ascontiguousarray(
            inputs["L0_ffn_Wg"].reshape(8, 128, 8).transpose(1, 0, 2)
            .reshape(128, 64), f32),
        "bg8": np.ascontiguousarray(bg8, f32),
        "w1": np.ascontiguousarray(inputs["L0_ffn_W1"].reshape(8192, 4096), f32),
        "w2": np.ascontiguousarray(inputs["L0_ffn_W2"].reshape(32768, 1024), f32),
        "b1_pt": np.ascontiguousarray(
            inputs["L0_ffn_b1"].reshape(8, 32, 128).transpose(2, 0, 1)
            .reshape(128, 256), f32),
        "b2_pt": np.ascontiguousarray(
            inputs["L0_ffn_b2"].reshape(8, 8, 128).transpose(2, 0, 1)
            .reshape(128, 64), f32),
        "gb_pp": np.ascontiguousarray(
            gb.reshape(2, 8, 128).transpose(2, 0, 1).reshape(128, 16), f32),
        "ident": np.eye(128, dtype=f32),
    }


def emit_s7_qkv(c, W, qkv_out, qkv_outB):
    E = c.E
    emit_ln(c, c.xc, c.xcB, c.xl, c.xlB)
    gbt, gbtB = c.alloc("s7gb", [128, 16], F32)
    E("sync", lambda e: e.dma_start(gbt[:], W["gb_pp"]), writes=[gbtB],
      dma=True)
    emit_ln(c, c.xl, c.xlB, c.ua, c.uaB, eps=1e-6, src16=True,
            gb=(gbt, gbtB, 0, 8))
    for t3 in range(3):
        for oc in range(NCH):
            for ic in range(NCH):
                stage_cast(c, W["qkvk"], t3 * 1024 + ic * 128, 128,
                           oc * 128, 128, c.ws, c.wsB,
                           ((oc % 4) * 8 + ic) * 128, f"s7w{t3}_{oc}_{ic}",
                           sf_off=(ic % 4) * 512)
            for tq in range(4):
                pc = (tq % 2) * 512
                g = c.pb(pc, pc + 512)
                for ic in range(NCH):
                    E("pe", lambda e, ic=ic, oc=oc, tq=tq, pc=pc: e.matmul(
                        c.pr(0, 128, pc, pc + 512),
                        c.ws[:, ((oc % 4) * 8 + ic) * 128:
                             ((oc % 4) * 8 + ic) * 128 + 128],
                        c.ua[ic][:, tq * 512:(tq + 1) * 512],
                        start=(ic == 0), stop=(ic == NCH - 1)),
                      reads=[c.wsB, c.uaB[ic]], writes=g, tag="s7_pmm")
                E("act", lambda e, tq=tq, pc=pc: e.copy(
                    c.xl[3][:, tq * 512:(tq + 1) * 512],
                    c.pr(0, 128, pc, pc + 512)),
                  reads=g, writes=[c.xlB[3]], tag="s7_kvev")
            E("sync", lambda e, t3=t3, oc=oc: e.dma_start(
                qkv_out[t3 * 1024 + oc * 128:t3 * 1024 + (oc + 1) * 128, :],
                c.xl[3][:]),
              reads=[c.xlB[3]], writes=[qkv_outB], dma=True, tag="s7_kvst")


def emit_s7_attn(c, W, q_in, kv_all, kv_allB):
    E = c.E
    msk, mskB = c.alloc("s7msk", [128, 8], F32)
    E("sync", lambda e: e.dma_start(msk[:], W["mask_pp"]), writes=[mskB],
      dma=True)
    E("sync", lambda e: e.dma_start(c.ctb[:, 512:528], W["esel8"]),
      writes=[c.ctbB], dma=True, tag="s7e8")
    E("sync", lambda e: e.dma_start(c.ctb[0:16, 528:1552], W["eselT"]),
      writes=[c.ctbB], dma=True, tag="s7eT")
    for oc in range(NCH):
        E("sync", lambda e, oc=oc: e.dma_start(
            c.sa[:, oc * 2048:(oc + 1) * 2048],
            q_in[oc * 128:(oc + 1) * 128, :]),
          writes=[c.saB], dma=True, tag="s7_qld")
    E("sync", lambda e: e.dma_start(c.ws[:], W["sel2"]),
      writes=[c.wsB], dma=True, tag="s7_sel2")
    gL = c.pb(0, 2048)
    for kb in range(B):
        for ic in range(NCH):
            koff = (ic % 2) * 2048
            E("sync", lambda e, kb=kb, ic=ic, koff=koff: e.dma_start(
                c.sb[:, koff:koff + 2048],
                kv_all[kb * 2048 + ic * 128:kb * 2048 + (ic + 1) * 128, :]),
              writes=[c.sbB], dma=True, tag="s7_kld")
            E("dve", lambda e, ic=ic, koff=koff: e.tensor_tensor(
                c.xl[0][:], c.sa[:, ic * 2048:(ic + 1) * 2048],
                c.sb[:, koff:koff + 2048], ALU.mult),
              reads=[c.sbB, c.saB], writes=[c.xlB[0]], tag="s7_prod")
            for j in range(4):
                E("pe", lambda e, kb=kb, ic=ic, j=j: e.matmul(
                    c.pr(0, 128, j * 512, (j + 1) * 512),
                    c.ws[:, (ic * 8 + kb) * 128:(ic * 8 + kb) * 128 + 128],
                    c.xl[0][:, j * 512:(j + 1) * 512],
                    start=(ic == 0 and kb == 0),
                    stop=(ic == NCH - 1 and kb == B - 1)),
                  reads=[c.wsB, c.xlB[0]], writes=gL, tag="s7_lmm")
    E("act", lambda e: e.activation(c.xl[1][:], c.pr(0, 128, 0, 2048),
                                    AF.Exp, bias=msk[:, 0:1], scale=0.125),
      reads=gL + [mskB], writes=[c.xlB[1]], tag="s7_exp")
    for j in range(4):
        E("pe", lambda e, j=j: e.matmul(
            c.pr(0, 16, j * 512, (j + 1) * 512), c.ctb[:, 512:528],
            c.xl[1][:, j * 512:(j + 1) * 512], start=True, stop=True),
          reads=[c.ctbB, c.xlB[1]], writes=gL, tag="s7_rs")

    def _rcp(e):
        with c.nc.allow_low_precision(reason="softmax normalizer in bf16"):
            return e.reciprocal(c.xl[2][0:16, :], c.pr(0, 16, 0, 2048))
    E("dve", _rcp, reads=gL, writes=[c.xlB[2]], tag="s7_rcp")
    E("sync", lambda e: e.dma_start(c.ws[:], W["sel2T"]),
      writes=[c.wsB], dma=True, tag="s7_sel2T")
    for ic in range(NCH):
        for kb in range(B):
            koff = (kb % 2) * 2048
            E("sync", lambda e, kb=kb, ic=ic, koff=koff: e.dma_start(
                c.sb[:, koff:koff + 2048],
                kv_all[kb * 2048 + 1024 + ic * 128:
                       kb * 2048 + 1024 + (ic + 1) * 128, :]),
              writes=[c.sbB], dma=True, tag="s7_vld")
            for j in range(4):
                E("pe", lambda e, kb=kb, ic=ic, j=j: e.matmul(
                    c.pr(0, 128, 2048 + j * 512, 2048 + (j + 1) * 512),
                    c.ws[:, (ic * 8 + kb) * 128:(ic * 8 + kb) * 128 + 128],
                    c.xl[1][:, j * 512:(j + 1) * 512],
                    start=True, stop=True),
                  reads=[c.wsB, c.xlB[1]],
                  writes=c.pb(2048 + j * 512, 2048 + (j + 1) * 512),
                  tag="s7_pbc")
            if kb == 0:
                for j in range(4):
                    E("dve", lambda e, j=j, koff=koff: e.tensor_tensor(
                        c.sf[:, j * 512:(j + 1) * 512],
                        c.sb[:, koff + j * 512:koff + (j + 1) * 512],
                        c.pr(0, 128, 2048 + j * 512, 2048 + (j + 1) * 512),
                        ALU.mult),
                      reads=[c.sbB] + c.pb(2048 + j * 512,
                                           2048 + (j + 1) * 512),
                      writes=[c.sfB], tag="s7_cx0")
            else:
                for j in range(4):
                    E("dve", lambda e, j=j, koff=koff: e.tensor_tensor(
                        c.sb[:, 4096 + (j % 3) * 512:
                             4096 + (j % 3) * 512 + 512],
                        c.sb[:, koff + j * 512:koff + (j + 1) * 512],
                        c.pr(0, 128, 2048 + j * 512, 2048 + (j + 1) * 512),
                        ALU.mult),
                      reads=[c.sbB] + c.pb(2048 + j * 512,
                                           2048 + (j + 1) * 512),
                      writes=[c.sbB], tag="s7_cxm")
                    E("pool", lambda e, j=j: e.tensor_tensor(
                        c.sf[:, j * 512:(j + 1) * 512],
                        c.sf[:, j * 512:(j + 1) * 512],
                        c.sb[:, 4096 + (j % 3) * 512:
                             4096 + (j % 3) * 512 + 512], ALU.add),
                      reads=[c.sbB, c.sfB], writes=[c.sfB], tag="s7_cxa")
        for j in range(4):
            E("pe", lambda e, ic=ic, j=j: e.matmul(
                c.pr(0, 128, 2048 + j * 512, 2048 + (j + 1) * 512),
                c.ctb[0:16, 528 + ic * 128:528 + (ic + 1) * 128],
                c.xl[2][0:16, j * 512:(j + 1) * 512],
                start=True, stop=True),
              reads=[c.ctbB, c.xlB[2]],
              writes=c.pb(2048 + j * 512, 2048 + (j + 1) * 512),
              tag="s7_rbc")
            E("dve", lambda e, ic=ic, j=j: e.tensor_tensor(
                c.ua[ic][:, j * 512:(j + 1) * 512],
                c.sf[:, j * 512:(j + 1) * 512],
                c.pr(0, 128, 2048 + j * 512, 2048 + (j + 1) * 512),
                ALU.mult),
              reads=[c.sfB] + c.pb(2048 + j * 512, 2048 + (j + 1) * 512),
              writes=[c.uaB[ic]], tag="s7_nrm")
    ua_l = [c.ua[i][:] for i in range(NCH)]

    def consume_o(oc, tq0, width, c0, guards):
        E("dve", lambda e: e.tensor_tensor(
            c.xc[oc][:, tq0:tq0 + width], c.pr(0, 128, c0, c0 + width),
            c.xc[oc][:, tq0:tq0 + width], ALU.add),
          reads=guards + [c.xcB[oc]], writes=[c.xcB[oc]], tag="s7_res")

    emit_proj(c, W["wout"], ua_l, c.uaB, NCH, consume_o, "s7o")


def prep_l1_attn_b(inputs):
    f32 = np.float32
    esel8 = np.zeros((128, 16), f32)
    eselT = np.zeros((8, 16, 128), f32)
    sel2 = np.zeros((128, 64 * 128), f32)
    sel2T = np.zeros((128, 64 * 128), f32)
    for ic in range(8):
        for p in range(128):
            h = 2 * ic + p // 64
            for kb in range(8):
                sel2[p, (ic * 8 + kb) * 128 + h * 8 + kb] = 1.0
                sel2T[h * 8 + kb, (ic * 8 + kb) * 128 + p] = 1.0
            eselT[ic, h % 16, p] = 0.0
    for pp in range(128):
        esel8[pp, pp // 8] = 1.0
    for ic in range(8):
        for p in range(128):
            h = 2 * ic + p // 64
            eselT[ic, h, p] = 1.0
    gb = np.stack([inputs["L1_attn_b_ln_scale"], inputs["L1_attn_b_ln_bias"]])
    masks = []
    for core in range(8):
        m = np.zeros((128, 8), f32)
        for p in range(128):
            if p % 8 > core:
                m[p, 0] = -1e10
        masks.append(np.ascontiguousarray(m))
    return {
        "qkvk": np.ascontiguousarray(
            inputs["L1_attn_b_qkv_kernel"].transpose(1, 0, 2)
            .reshape(3072, 1024), f32),
        "wout": np.ascontiguousarray(inputs["L1_attn_b_out_kernel"], f32),
        "gb_pp": np.ascontiguousarray(
            gb.reshape(2, 8, 128).transpose(2, 0, 1).reshape(128, 16), f32),
        "esel8": esel8.astype(BF),
        "eselT": np.ascontiguousarray(
            eselT.transpose(1, 0, 2).reshape(16, 1024)).astype(BF),
        "sel2": sel2.astype(BF),
        "sel2T": sel2T.astype(BF),
        "mask_pp": masks,
    }


def run_launch(subs, x_fm, inputs, extra=None):
    c = C("L" + "_".join(s for s in subs))
    x_in = c.din("x_in", [np.ascontiguousarray(x_fm[i]) for i in range(B)],
                 percore=True)
    x_out = c.dout("x_out", (1024, 2048), F32)
    load_x(c, x_in)

    def din_w(pfx, W):
        out = {}
        for k, v in W.items():
            dt = BF16 if v.dtype == BF else F32
            out[k] = c.din(f"{pfx}_{k}", v, dt=dt)
        return out

    for s in subs:
        if s in ("s6", "s8"):
            emit_l1_ffn(c, din_w(s, prep_l1_ffn(inputs)),
                        it=0 if s == "s6" else 1)
        elif s == "s3":
            emit_l0_attn_b(c, din_w(s, prep_l0_attn_b(inputs)))
        elif s == "s1":
            emit_l0_attn_a(c, din_w(s, prep_l0_attn_a(inputs)))
        elif s == "s5":
            emit_l1_attn_a(c, din_w(s, prep_l1_attn_a(inputs)))
        elif s == "s7a":
            Wh = prep_l1_attn_b(inputs)
            Wh.pop("mask_pp")
            W = din_w(s, {"qkvk": Wh["qkvk"], "gb_pp": Wh["gb_pp"]})
            qkv_out = c.dout("qkv_out", (3 * 1024, 2048), BF16)
            qkvB = c.kb.buf("qkv_out")
            emit_s7_qkv(c, W, qkv_out, qkvB)
        elif s == "s7b":
            Wh = prep_l1_attn_b(inputs)
            masks = Wh.pop("mask_pp")
            W = din_w(s, {k: Wh[k] for k in
                          ("wout", "esel8", "eselT", "sel2", "sel2T")})
            W["mask_pp"] = c.din("s7_mask_pp", masks, percore=True)
            q_in = c.din("s7_q", extra["q"], dt=BF16, percore=True)
            kv_all = c.din("s7_kv", extra["kv"], dt=BF16)
            emit_s7_attn(c, W, q_in, kv_all, c.kb.buf("kv_all"))
        elif s in ("s2", "s4"):
            W = din_w(s, prep_l0_ffn(inputs))
            W["x_dram"] = x_in
            W["spill"] = False
            emit_l0_ffn(c, W, it=0 if s == "s2" else 1)
        else:
            raise NotImplementedError(s)
    store_x(c, x_out)
    stats = c.finish()
    res = c.run()
    out = np.stack([res.results[i]["x_out"] for i in range(B)])
    ex = {}
    if "s7a" in subs:
        ex["qkv"] = [np.asarray(res.results[i]["qkv_out"]) for i in range(B)]
    return out, stats, ex


LAUNCH_PLAN = [["s1"], ["s2"], ["s3"], ["s4"], ["s5"], ["s6"],
               ["s7a"], ["s7b"], ["s8"]]


def kernel(**inputs):
    x = inputs["L0_attn_a_x"]
    x_fm = np.ascontiguousarray(x.transpose(0, 2, 1)).astype(np.float32)
    extra = None
    for subs in LAUNCH_PLAN:
        x_fm, _, ex = run_launch(subs, x_fm, inputs, extra=extra)
        if "qkv" in ex:
            qkv = ex["qkv"]
            extra = {
                "q": [np.ascontiguousarray(qkv[b][0:1024]) for b in range(B)],
                "kv": np.ascontiguousarray(
                    np.concatenate([qkv[b][1024:3072] for b in range(B)],
                                   axis=0)),
            }
    return np.ascontiguousarray(x_fm.transpose(0, 2, 1))


# version 04
# speedup vs baseline: 1.0042x; 1.0042x over previous
import numpy as np
import ml_dtypes
import concourse.bass as bass
import concourse.mybir as mybir
from concourse.bass_utils import run_bass_kernel_spmd

F32 = mybir.dt.float32
BF16 = mybir.dt.bfloat16
AF = mybir.ActivationFunctionType
ALU = mybir.AluOpType
AX = mybir.AxisListType
BF = ml_dtypes.bfloat16

ENGINES = ("pe", "act", "dve", "pool", "sync")
SAME_ENGINE_COMPUTE_EDGE = True


class Op:
    __slots__ = ("eng", "kind", "fn", "deps", "seq", "needs_inc", "sem_val",
                 "tag")

    def __init__(self, eng, kind, fn, tag):
        self.eng = eng
        self.kind = kind
        self.fn = fn
        self.deps = []
        self.seq = -1
        self.needs_inc = False
        self.sem_val = None
        self.tag = tag


class Buf:
    __slots__ = ("name", "last_write", "readers")

    def __init__(self, name):
        self.name = name
        self.last_write = None
        self.readers = []


class KB:
    def __init__(self, nc):
        self.nc = nc
        self.ops = {e: [] for e in ENGINES}
        self.final_waits = []

    def buf(self, name):
        return Buf(name)

    def _keep_dep(self, dep, op):
        if dep.kind == "l" or op.kind == "l":
            return True
        if dep.eng != op.eng:
            return True
        if dep.kind == "d" and op.kind == "d":
            return False
        if dep.kind == "c" and op.kind == "c":
            if dep.eng == "pe":
                return False
            return SAME_ENGINE_COMPUTE_EDGE
        return True

    def emit(self, eng, fn, reads=(), writes=(), dma=False, tag="",
             extra_deps=(), coll=False, hard_deps=()):
        op = Op(eng, "l" if coll else ("d" if dma else "c"), fn, tag)
        deps = set()
        for b in reads:
            if b.last_write is not None:
                deps.add(b.last_write)
        for b in writes:
            if b.last_write is not None:
                deps.add(b.last_write)
            deps.update(b.readers)
        deps.update(extra_deps)
        deps.discard(op)
        op.deps = [d for d in deps if self._keep_dep(d, op)]
        for d in hard_deps:
            if d is not op and d not in op.deps:
                op.deps.append(d)
        for b in reads:
            if op not in b.readers:
                b.readers.append(op)
        for b in writes:
            b.last_write = op
            b.readers = []
        self.ops[eng].append(op)
        op.seq = len(self.ops[eng]) - 1
        return op

    def require_done(self, op):
        self.final_waits.append(op)

    def finalize_and_emit(self):
        nc = self.nc
        for e in ENGINES:
            for op in self.ops[e]:
                for d in op.deps:
                    d.needs_inc = True
        for op in self.final_waits:
            op.needs_inc = True
        LIMIT = 56000
        n_epochs = {}
        n_coll = 0
        for e in ENGINES:
            for k, step in (("c", 1), ("d", 16)):
                ep = 0
                v = 0
                for op in self.ops[e]:
                    if op.kind == "l" and op.needs_inc and op.sem_val is None:
                        op.sem_val = ("L", n_coll, 1)
                        n_coll += 1
                        continue
                    if op.kind != k or not op.needs_inc:
                        continue
                    if v + step > LIMIT:
                        ep += 1
                        v = 0
                    v += step
                    op.sem_val = (ep, v)
                n_epochs[(e, k)] = ep + 1 if v > 0 else 0
        sems = {}
        for (e, k), n in n_epochs.items():
            for ep in range(n):
                sems[(e, k, ep)] = nc.alloc_semaphore(f"s_{e}_{k}{ep}")
        for i in range(n_coll):
            sems[("coll", i)] = nc.alloc_semaphore(f"s_coll{i}")
        highwater = {}

        def wait_dep(eng_name, engobj, dep):
            if dep.sem_val[0] == "L":
                key = ("coll", dep.sem_val[1])
                if not highwater.get((eng_name, key)):
                    engobj.wait_ge(sems[key], 1)
                    highwater[(eng_name, key)] = 1
                return
            ep, val = dep.sem_val
            key = (dep.eng, dep.kind, ep)
            hw = highwater.get((eng_name, key), 0)
            if val > hw:
                engobj.wait_ge(sems[key], val)
                highwater[(eng_name, key)] = val

        def emit_stream(eng_name, engobj):
            for op in self.ops[eng_name]:
                for dep in sorted(op.deps, key=lambda d: (d.eng, d.kind,
                                                          d.sem_val)):
                    wait_dep(eng_name, engobj, dep)
                inst = op.fn(engobj)
                if op.needs_inc:
                    if op.sem_val[0] == "L":
                        inst.then_inc(sems[("coll", op.sem_val[1])])
                    else:
                        ep, _ = op.sem_val
                        inst.then_inc(sems[(op.eng, op.kind, ep)],
                                      16 if op.kind == "d" else 1)
            if eng_name == "pool":
                for op in self.final_waits:
                    wait_dep(eng_name, engobj, op)

        with nc.Block() as blk:

            @blk.tensor
            def _(tensor):
                emit_stream("pe", tensor)

            @blk.scalar
            def _(scalar):
                emit_stream("act", scalar)

            @blk.vector
            def _(vector):
                emit_stream("dve", vector)

            @blk.gpsimd
            def _(gpsimd):
                emit_stream("pool", gpsimd)

            @blk.sync
            def _(sync):
                emit_stream("sync", sync)

        return {e: len(self.ops[e]) for e in ENGINES}


B, T, D = 8, 2048, 1024
NCH = 8
HEADS, DH = 16, 64
E_EXP, HFF = 8, 4096
LN_EPS = 1e-5


class C:
    def __init__(self, tag):
        self.nc = bass.Bass("TRN2", target_bir_lowering=False)
        self.kb = KB(self.nc)
        self.E = self.kb.emit
        self.inp = {}
        self.tag = tag
        nc = self.nc
        self.xc = [nc.alloc_sbuf_tensor(f"xc{i}", [128, 2048], F32)
                   for i in range(NCH)]
        self.xl = [nc.alloc_sbuf_tensor(f"xl{i}", [128, 2048], BF16)
                   for i in range(NCH)]
        self.sa = nc.alloc_sbuf_tensor("sa", [128, 16896], BF16)
        self.sb = nc.alloc_sbuf_tensor("sb", [128, 5632], BF16)
        self.ua = [nc.alloc_sbuf_tensor(f"ua{i}", [128, 2048], BF16)
                   for i in range(NCH)]
        self.sf = nc.alloc_sbuf_tensor("sf", [128, 2048], F32)
        self.ws = nc.alloc_sbuf_tensor("ws", [128, 8192], BF16)
        self.ct = nc.alloc_sbuf_tensor("ct", [128, 896], F32)
        self.ctb = nc.alloc_sbuf_tensor("ctb", [128, 2048], BF16)
        self.ps = nc.alloc_psum_tensor("ps", [128, 4096], F32)
        kb = self.kb
        self.xcB = [kb.buf(f"xc{i}") for i in range(NCH)]
        self.xlB = [kb.buf(f"xl{i}") for i in range(NCH)]
        self.saB = kb.buf("sa")
        self.sbB = kb.buf("sb")
        self.uaB = [kb.buf(f"ua{i}") for i in range(NCH)]
        self.sfB = kb.buf("sf")
        self.wsB = kb.buf("ws")
        self.ctB_ = kb.buf("ct")
        self.ctbB = kb.buf("ctb")
        self.psB = [kb.buf(f"psb{i}") for i in range(8)]
        self._const_cols = {}
        self._n_const = 0
        self._extra_sb = {}
        self._init_consts()

    def _init_consts(self):
        self.E("pool", lambda e: e.memset(self.ct[:, 64:65], 1.0),
               writes=[self.ctB_], tag="ones_col")
        self.E("pool", lambda e: e.memset(self.ct[0:1, 66:194], 1.0),
               writes=[self.ctB_], tag="ones_row")
        self.E("pool", lambda e: e.memset(self.ct[32:33, 66:194], 1.0),
               writes=[self.ctB_], tag="ones_row32")
        self.E("pool", lambda e: e.memset(self.ct[64:65, 66:194], 1.0),
               writes=[self.ctB_], tag="ones_row64")
        self.E("pool", lambda e: e.memset(self.ctb[0:1, 0:128], 1.0),
               writes=[self.ctbB], tag="ones_row16")
        self.E("pool", lambda e: e.memset(self.ctb[:, 128:129], 1.0),
               writes=[self.ctbB], tag="ones_col16")
        self.ones_col = self.ct[:, 64:65]
        self.ones_row = self.ct[0:1, 66:194]
        self.ones_row32 = self.ct[32:33, 66:194]
        self.ones_row64 = self.ct[64:65, 66:194]
        self.ones_row16 = self.ctb[0:1, 0:128]
        self.ones_col16 = self.ctb[:, 128:129]

    def alloc(self, name, shape, dt):
        if name in self._extra_sb:
            return self._extra_sb[name]
        t = self.nc.alloc_sbuf_tensor(f"{name}_{self.tag}", list(shape), dt)
        b = self.kb.buf(name)
        self._extra_sb[name] = (t, b)
        return t, b

    def pr(self, p0, p1, c0, c1):
        return self.ps[p0:p1, c0:c1]

    def pb(self, c0, c1):
        return [self.psB[i] for i in range(c0 // 512, (c1 + 511) // 512)]

    def din(self, name, arr, dt=F32, percore=False):
        self.inp[name] = arr
        shape = list(arr[0].shape if percore else arr.shape)
        return self.nc.dram_tensor(name, shape, dt, kind="ExternalInput").ap()

    def dout(self, name, shape, dt=F32):
        return self.nc.dram_tensor(name, list(shape), dt,
                                   kind="ExternalOutput").ap()

    def dint(self, name, shape, dt):
        return self.nc.dram_tensor(name, list(shape), dt, kind="Internal").ap()

    def const(self, val):
        if val not in self._const_cols:
            j = self._n_const
            self._n_const += 1
            assert j < 64
            self.kb.emit("pool",
                         lambda e, j=j, v=val: e.memset(self.ct[:, j:j + 1], v),
                         writes=[self.ctB_], tag=f"const{val}")
            self._const_cols[val] = j
        j = self._const_cols[val]
        return self.ct[:, j:j + 1]

    def finish(self):
        for e in ("sync", "pool", "act", "dve"):
            for op in self.kb.ops[e]:
                if op.kind == "d":
                    self.kb.require_done(op)
        return self.kb.finalize_and_emit()

    def run(self, trace=False):
        in_maps = []
        for core in range(B):
            m = {}
            for k, v in self.inp.items():
                m[k] = v[core] if isinstance(v, list) else v
            in_maps.append(m)
        return run_bass_kernel_spmd(self.nc, in_maps, core_ids=list(range(B)),
                                    trace=trace)


def load_x(c, x_dram):
    for i in range(NCH):
        c.E("sync", lambda e, i=i: e.dma_start(
            c.xc[i][:], x_dram[i * 128:(i + 1) * 128, :]),
            writes=[c.xcB[i]], dma=True, tag=f"ldx{i}")


def store_x(c, y_dram):
    for i in range(NCH):
        c.E("sync", lambda e, i=i: e.dma_start(
            y_dram[i * 128:(i + 1) * 128, :], c.xc[i][:]),
            reads=[c.xcB[i]], dma=True, tag=f"stx{i}")


def emit_ln(c, src, srcB, out, outB, eps=LN_EPS, src16=False, gb=None):
    E = c.E
    gA = c.pb(0, 2048)
    gB = c.pb(2048, 4096)
    ocol = c.ones_col16 if src16 else c.ones_col
    ocolB = c.ctbB if src16 else c.ctB_
    for i in range(NCH):
        E("act", lambda e, i=i: e.square(c.sf[:], src[i][:]),
          reads=[srcB[i]], writes=[c.sfB], tag="ln_sq")
        for j in range(4):
            E("pe", lambda e, i=i, j=j: e.matmul(
                c.pr(0, 1, j * 512, (j + 1) * 512), ocol,
                src[i][:, j * 512:(j + 1) * 512],
                start=(i == 0), stop=(i == NCH - 1)),
              reads=[srcB[i], ocolB], writes=gA, tag="ln_s")
            E("pe", lambda e, i=i, j=j: e.matmul(
                c.pr(32, 33, j * 512, (j + 1) * 512), c.ones_col,
                c.sf[:, j * 512:(j + 1) * 512],
                start=(i == 0), stop=(i == NCH - 1)),
              reads=[c.sfB, c.ctB_], writes=gA, tag="ln_s2")
    r_mean = c.sf[0:1, :]
    r_a = c.ws[0:1, 0:4096].bitcast(F32)
    r_b = c.ws[0:1, 4096:8192].bitcast(F32)
    E("dve", lambda e: e.tensor_scalar(r_mean, c.pr(0, 1, 0, 2048),
                                       1.0 / D, None, ALU.mult),
      reads=gA, writes=[c.sfB], tag="ln_mean")
    E("dve", lambda e: e.tensor_scalar(r_a, c.pr(32, 33, 0, 2048),
                                       1.0 / D, None, ALU.mult),
      reads=gA, writes=[c.wsB], tag="ln_ms")
    E("dve", lambda e: e.scalar_tensor_tensor(
        r_b, r_mean, -1.0, r_mean, ALU.mult, ALU.mult),
      reads=[c.sfB], writes=[c.wsB], tag="ln_nm2")
    E("dve", lambda e: e.tensor_tensor(r_a, r_a, r_b, ALU.add),
      reads=[c.wsB], writes=[c.wsB], tag="ln_var")
    eps_col = c.const(eps)
    E("act", lambda e: e.activation(r_b, r_a, AF.Ln,
                                    bias=eps_col[0:1, :]),
      reads=[c.wsB, c.ctB_], writes=[c.wsB], tag="ln_ln")
    E("act", lambda e: e.activation(r_a, r_b, AF.Exp, scale=-0.5),
      reads=[c.wsB], writes=[c.wsB], tag="ln_rstd")
    E("dve", lambda e: e.scalar_tensor_tensor(
        r_b, r_mean, -1.0, r_a, ALU.mult, ALU.mult),
      reads=[c.sfB, c.wsB], writes=[c.wsB], tag="ln_b")
    for j in range(4):
        E("pe", lambda e, j=j: e.matmul(
            c.pr(0, 128, j * 512, (j + 1) * 512), c.ones_row,
            c.ws[0:1, j * 1024:j * 1024 + 1024].bitcast(F32),
            start=True, stop=True),
          reads=[c.wsB, c.ctB_], writes=gA, tag="ln_bcA")
        E("pe", lambda e, j=j: e.matmul(
            c.pr(0, 128, 2048 + j * 512, 2048 + (j + 1) * 512), c.ones_row,
            c.ws[0:1, 4096 + j * 1024:4096 + j * 1024 + 1024].bitcast(F32),
            start=True, stop=True),
          reads=[c.wsB, c.ctB_], writes=gB, tag="ln_bcB")
    for i in range(NCH):
        E("dve", lambda e, i=i: e.tensor_tensor(
            c.sf[:], src[i][:], c.pr(0, 128, 0, 2048), ALU.mult),
          reads=[srcB[i]] + gA, writes=[c.sfB], tag="ln_n1")
        if gb is None:
            E("dve", lambda e, i=i: e.tensor_tensor(
                out[i][:], c.sf[:], c.pr(0, 128, 2048, 4096), ALU.add),
              reads=[c.sfB] + gB, writes=[outB[i]], tag="ln_n2")
        else:
            gt, gtB, go, bo_ = gb
            E("dve", lambda e, i=i: e.tensor_tensor(
                c.sf[:], c.sf[:], c.pr(0, 128, 2048, 4096), ALU.add),
              reads=[c.sfB] + gB, writes=[c.sfB], tag="ln_n2")
            E("act", lambda e, i=i, gt=gt, go=go: e.activation(
                c.sf[:], c.sf[:], AF.Copy, scale=gt[:, go + i:go + i + 1]),
              reads=[c.sfB, gtB], writes=[c.sfB], tag="ln_g")
            E("dve", lambda e, i=i, gt=gt, bo_=bo_: e.tensor_scalar(
                out[i][:], c.sf[:], gt[:, bo_ + i:bo_ + i + 1], None,
                ALU.add),
              reads=[c.sfB, gtB], writes=[outB[i]], tag="ln_b2")


def stage_cast(c, dram_ap, r0, rows, c0, cols, dest, destB, doff, tag,
               sf_off=0):
    E = c.E
    E("sync", lambda e: e.dma_start(
        c.sf[0:rows, sf_off:sf_off + cols],
        dram_ap[r0:r0 + rows, c0:c0 + cols]),
      writes=[c.sfB], dma=True, tag=f"{tag}_ld")
    E("pool", lambda e: e.tensor_copy(dest[0:rows, doff:doff + cols],
                                      c.sf[0:rows, sf_off:sf_off + cols]),
      reads=[c.sfB], writes=[destB], tag=f"{tag}_cast")


def emit_proj(c, wT_dram, src, srcB, nout, consume, tag, nin=NCH,
              wq_width=512):
    E = c.E
    n_waves = (nout + 3) // 4
    for w in range(n_waves):
        ocs = list(range(w * 4, min(w * 4 + 4, nout)))
        wwidth = len(ocs) * 128
        half = (w % 2) * 4096
        for ic in range(nin):
            stage_cast(c, wT_dram, ic * 128, 128, ocs[0] * 128, wwidth,
                       c.ws, c.wsB, half + ic * 512, f"{tag}w{w}i{ic}")
        for tq in range(4):
            pbase = 0 if tq % 2 == 0 else 2048
            guards = c.pb(pbase, pbase + wwidth)
            for ic in range(nin):
                for k, oc in enumerate(ocs):
                    E("pe", lambda e, ic=ic, k=k, tq=tq, half=half,
                        pbase=pbase: e.matmul(
                          c.pr(0, 128, pbase + k * 512, pbase + k * 512 + 512),
                          c.ws[:, half + ic * 512 + k * 128:
                               half + ic * 512 + k * 128 + 128],
                          src[ic][:, tq * 512:(tq + 1) * 512],
                          start=(ic == 0), stop=(ic == nin - 1)),
                      reads=[c.wsB, srcB[ic]], writes=guards,
                      tag=f"{tag}_mm")
            for k, oc in enumerate(ocs):
                consume(oc, tq * 512, 512, pbase + k * 512,
                        c.pb(pbase + k * 512, pbase + k * 512 + 512))


def emit_l1_ffn(c, W, it=0):
    E = c.E
    tg = f"f{it}"
    emit_ln(c, c.xc, c.xcB, c.xl, c.xlB)
    mu, muB = c.alloc("mu6", [128, 16], F32)
    om, omB = c.alloc("om6", [128, 16], F32)
    E("sync", lambda e: e.dma_start(mu[:], W["mu"]), writes=[muB], dma=True)
    E("dve", lambda e: e.tensor_scalar(om[:], mu[:], -1.0, 1.0, ALU.mult,
                                       ALU.add),
      reads=[muB], writes=[omB], tag="om")
    for i in range(NCH):
        E("dve", lambda e, i=i: e.tensor_scalar(
            c.ua[i][:], c.xl[i][:], mu[:, i:i + 1], None, ALU.mult),
          reads=[c.xlB[i], muB], writes=[c.uaB[i]], tag="xr1")
        E("dve", lambda e, i=i: e.scalar_tensor_tensor(
            c.ua[i][:, 1:2048], c.xl[i][:, 0:2047], om[:, i:i + 1],
            c.ua[i][:, 1:2048], ALU.mult, ALU.add),
          reads=[c.xlB[i], omB, c.uaB[i]], writes=[c.uaB[i]], tag="xr2")
        E("dve", lambda e, i=i: e.tensor_scalar(
            c.sa[:, i * 2048:(i + 1) * 2048], c.xl[i][:],
            mu[:, 8 + i:9 + i], None, ALU.mult),
          reads=[c.xlB[i], muB, c.saB], writes=[c.saB], tag="xk1")
        E("dve", lambda e, i=i: e.scalar_tensor_tensor(
            c.sa[:, i * 2048 + 1:(i + 1) * 2048],
            c.xl[i][:, 0:2047], om[:, 8 + i:9 + i],
            c.sa[:, i * 2048 + 1:(i + 1) * 2048], ALU.mult, ALU.add),
          reads=[c.xlB[i], omB, c.saB], writes=[c.saB], tag="xk2")
    sa_chunks = [c.sa[:, i * 2048:(i + 1) * 2048] for i in range(NCH)]
    saBl = [c.saB] * NCH

    def consume_k(oc, tq0, width, c0, guards):
        E("act", lambda e: e.activation(c.xl[oc][:, tq0:tq0 + width],
                                        c.pr(0, 128, c0, c0 + width), AF.Relu),
          reads=guards, writes=[c.xlB[oc]], tag="k_relu")

    emit_proj(c, W["wkT"], sa_chunks, saBl, NCH, consume_k, f"{tg}k")
    for i in range(NCH):
        E("pool", lambda e, i=i: e.tensor_tensor(c.xl[i][:], c.xl[i][:],
                                                 c.xl[i][:], ALU.mult),
          reads=[c.xlB[i]], writes=[c.xlB[i]], tag="k_sq")
    ua_l = [c.ua[i][:] for i in range(NCH)]

    def consume_r(oc, tq0, width, c0, guards):
        E("act", lambda e: e.activation(
            c.sa[:, oc * 2048 + tq0:oc * 2048 + tq0 + width],
            c.pr(0, 128, c0, c0 + width), AF.Sigmoid),
          reads=guards, writes=[c.saB], tag="r_sig")

    emit_proj(c, W["wrT"], ua_l, c.uaB, NCH, consume_r, f"{tg}r")
    xl_l = [c.xl[i][:] for i in range(NCH)]

    def consume_v(oc, tq0, width, c0, guards):
        E("dve", lambda e: e.tensor_tensor(
            c.sb[:, 0:width], c.pr(0, 128, c0, c0 + width),
            c.sa[:, oc * 2048 + tq0:oc * 2048 + tq0 + width], ALU.mult),
          reads=guards + [c.saB], writes=[c.sbB], tag="v_gate")
        E("dve", lambda e: e.tensor_tensor(
            c.xc[oc][:, tq0:tq0 + width], c.xc[oc][:, tq0:tq0 + width],
            c.sb[:, 0:width], ALU.add),
          reads=[c.sbB, c.xcB[oc]], writes=[c.xcB[oc]], tag="v_res")

    emit_proj(c, W["wvT"], xl_l, c.xlB, NCH, consume_v, f"{tg}v")


def prep_l1_ffn(inputs):
    f32 = np.float32
    mu = np.stack([inputs["L1_ffn_mu_r"], inputs["L1_ffn_mu_k"]])
    mu_pp = np.ascontiguousarray(
        mu.reshape(2, 8, 128).transpose(2, 0, 1).reshape(128, 16), f32)
    return {
        "wrT": np.ascontiguousarray(inputs["L1_ffn_W_r"].T, f32),
        "wkT": np.ascontiguousarray(inputs["L1_ffn_W_k"].T, f32),
        "wvT": np.ascontiguousarray(inputs["L1_ffn_W_v"].T, f32),
        "mu": mu_pp,
    }


def emit_l0_attn_b(c, W):
    E = c.E
    emit_ln(c, c.xc, c.xcB, c.xl, c.xlB)
    E("sync", lambda e: e.dma_start(c.ctb[:, 512:640], W["esel"]),
      writes=[c.ctbB], dma=True, tag="esel")
    E("sync", lambda e: e.dma_start(c.ctb[0:16, 640:1664], W["eselT"]),
      writes=[c.ctbB], dma=True, tag="eselT")
    bpr, bprB = c.alloc("bpr", [128, 8], F32)
    E("sync", lambda e: e.dma_start(bpr[:], W["bproj_pp"]), writes=[bprB],
      dma=True)
    STG = 2048
    TMP = 2560
    TMP2 = 3072
    KP = 3584
    def proj_chunk(oc, tq, pcol):
        for ic in range(NCH):
            off = (ic % 4) * 512
            E("pe", lambda e, ic=ic, oc=oc, tq=tq, pcol=pcol: e.matmul(
                c.pr(0, 128, pcol, pcol + 512),
                c.ws[:, (oc % 8) * 1024 + ic * 128:(oc % 8) * 1024 + ic * 128 + 128],
                c.xl[ic][:, tq * 512:(tq + 1) * 512],
                start=(ic == 0), stop=(ic == NCH - 1)),
              reads=[c.wsB, c.xlB[ic]], writes=c.pb(pcol, pcol + 512),
              tag="s3_mm")

    def stage_w(oc):
        for ic in range(NCH):
            stage_cast(c, W["wqkv"], ic * 128, 128, oc * 128, 128,
                       c.ws, c.wsB, (oc % 8) * 1024 + ic * 128,
                       f"s3w{oc}_{ic}", sf_off=(ic % 4) * 512)

    guards_vs = c.pb(2048, 4096)
    for j in range(NCH):
        oc = 16 + j
        stage_w(oc)
        for tq in range(4):
            pcol = (tq % 2) * 512
            proj_chunk(oc, tq, pcol)
            gv = c.pb(pcol, pcol + 512)
            E("act", lambda e, tq=tq, pcol=pcol: e.copy(
                c.sb[:, STG:STG + 512], c.pr(0, 128, pcol, pcol + 512)),
              reads=gv, writes=[c.sbB], tag="s3_vev")
            E("pe", lambda e, j=j, tq=tq: e.matmul(
                c.pr(0, 16, 2048 + tq * 512, 2048 + (tq + 1) * 512),
                c.ctb[:, 512 + j * 16:512 + (j + 1) * 16],
                c.sb[:, STG:STG + 512],
                start=(j == 0), stop=(j == NCH - 1)),
              reads=[c.ctbB, c.sbB], writes=guards_vs, tag="s3_vsum")
    E("act", lambda e: e.copy(c.sb[0:16, 0:2048], c.pr(0, 16, 2048, 4096)),
      reads=guards_vs, writes=[c.sbB], tag="s3_vsum_ev")

    def elu1(pcol, dst_off):
        g = c.pb(pcol, pcol + 512)
        E("dve", lambda e: e.tensor_scalar(c.sb[:, TMP:TMP + 512],
                                           c.pr(0, 128, pcol, pcol + 512),
                                           0.0, None, ALU.min),
          reads=g, writes=[c.sbB], tag="s3_min")
        E("act", lambda e: e.activation(c.sb[:, TMP2:TMP2 + 512],
                                        c.sb[:, TMP:TMP + 512], AF.Exp),
          reads=[c.sbB], writes=[c.sbB], tag="s3_exp")
        E("act", lambda e: e.activation(c.sb[:, TMP:TMP + 512],
                                        c.pr(0, 128, pcol, pcol + 512),
                                        AF.Relu),
          reads=g + [c.sbB], writes=[c.sbB], tag="s3_relu")
        E("dve", lambda e: e.tensor_tensor(c.sb[:, dst_off:dst_off + 512],
                                           c.sb[:, TMP:TMP + 512],
                                           c.sb[:, TMP2:TMP2 + 512], ALU.add),
          reads=[c.sbB], writes=[c.sbB], tag="s3_add1")

    for j in range(NCH):
        oc = 8 + j
        stage_w(oc)
        for tq in range(4):
            pcol = (tq % 2) * 512
            proj_chunk(oc, tq, pcol)
            elu1(pcol, KP)
            E("pe", lambda e, j=j, tq=tq: e.matmul(
                c.pr(0, 128, 2560, 3072),
                c.ctb[0:16, 640 + j * 128:640 + (j + 1) * 128],
                c.sb[0:16, tq * 512:(tq + 1) * 512],
                start=True, stop=True),
              reads=[c.ctbB, c.sbB], writes=c.pb(2560, 3072), tag="s3_vbc")
            E("dve", lambda e, tq=tq: e.scalar_tensor_tensor(
                c.sb[:, TMP:TMP + 512], c.sb[:, KP:KP + 512], 1.0,
                c.pr(0, 128, 2560, 3072), ALU.mult, ALU.mult,
                accum_out=c.ct[:, 200 + tq:201 + tq]),
              reads=[c.sbB] + c.pb(2560, 3072), writes=[c.sbB, c.ctB_],
              tag="s3_kv")
        E("dve", lambda e, j=j: e.reduce_sum(
            c.ct[:, 240 + j:241 + j], c.ct[:, 200:204], axis=AX.X),
          reads=[c.ctB_], writes=[c.ctB_], tag="s3_kvsum")
    for j in range(NCH):
        oc = j
        stage_w(oc)
        for tq in range(4):
            pcol = (tq % 2) * 512
            proj_chunk(oc, tq, pcol)
            elu1(pcol, KP)
            E("pe", lambda e, j=j, tq=tq: e.matmul(
                c.pr(0, 16, 2048 + tq * 512, 2048 + (tq + 1) * 512),
                c.ctb[:, 512 + j * 16:512 + (j + 1) * 16],
                c.sb[:, KP:KP + 512],
                start=(j == 0), stop=(j == NCH - 1)),
              reads=[c.ctbB, c.sbB], writes=guards_vs, tag="s3_qsum")
    E("act", lambda e: e.copy(c.sb[0:16, 0:2048], c.pr(0, 16, 2048, 4096)),
      reads=guards_vs, writes=[c.sbB], tag="s3_qsum_ev")
    for j in range(NCH):
        for tq in range(4):
            E("pe", lambda e, j=j, tq=tq: e.matmul(
                c.pr(0, 128, 2560, 3072),
                c.ctb[0:16, 640 + j * 128:640 + (j + 1) * 128],
                c.sb[0:16, tq * 512:(tq + 1) * 512],
                start=True, stop=True),
              reads=[c.ctbB, c.sbB], writes=c.pb(2560, 3072), tag="s3_qbc")
            E("act", lambda e, j=j, tq=tq: e.activation(
                c.ua[j][:, tq * 512:(tq + 1) * 512],
                c.pr(0, 128, 2560, 3072), AF.Copy,
                scale=c.ct[:, 240 + j:241 + j]),
              reads=c.pb(2560, 3072) + [c.ctB_], writes=[c.uaB[j]],
              tag="s3_out")
    ua_l = [c.ua[i][:] for i in range(NCH)]

    def consume_o(oc, tq0, width, c0, guards):
        E("dve", lambda e: e.scalar_tensor_tensor(
            c.xc[oc][:, tq0:tq0 + width], c.pr(0, 128, c0, c0 + width),
            bpr[:, oc:oc + 1], c.xc[oc][:, tq0:tq0 + width],
            ALU.add, ALU.add),
          reads=guards + [bprB, c.xcB[oc]], writes=[c.xcB[oc]], tag="s3_res")

    emit_proj(c, W["wproj"], ua_l, c.uaB, NCH, consume_o, "s3o")


def prep_l0_attn_b(inputs):
    f32 = np.float32
    esel = np.zeros((8, 128, 16), f32)
    eselT = np.zeros((8, 16, 128), f32)
    for j in range(8):
        for p in range(128):
            h = 2 * j + (p // 64)
            esel[j, p, h] = 1.0
            eselT[j, h, p] = 1.0
    return {
        "wqkv": np.ascontiguousarray(inputs["L0_attn_b_W_qkv"], f32),
        "wproj": np.ascontiguousarray(inputs["L0_attn_b_W_proj"], f32),
        "bproj_pp": np.ascontiguousarray(
            inputs["L0_attn_b_b_proj"].reshape(8, 128).T, f32),
        "esel": np.ascontiguousarray(
            esel.transpose(1, 0, 2).reshape(128, 128)).astype(BF),
        "eselT": np.ascontiguousarray(
            eselT.transpose(1, 0, 2).reshape(16, 1024)).astype(BF),
    }


def emit_mha(c, W, tag, masked=False):
    E = c.E
    CSH = 12.0
    csh_col = c.const(-CSH)
    c60_col = c.const(-60.0)
    TRI = W.get("tri_off", 2048 - 128)
    KKD, KKC = 536, 552
    for ic in range(NCH):
        stage_cast(c, W["wv2"], ic * 128, 128, 0, 1024, c.ws, c.wsB,
                   ic * 1024, f"{tag}wv{ic}", sf_off=(ic % 2) * 1024)
    if W.get("bv_row") is not None:
        E("sync", lambda e: e.dma_start(c.sf[0:1, 0:1024], W["bv_row"]),
          writes=[c.sfB], dma=True, tag="bv_ld")
        E("pool", lambda e: e.tensor_copy(c.sb[0:1, 0:1024],
                                          c.sf[0:1, 0:1024]),
          reads=[c.sfB], writes=[c.sbB], tag="bv_cast")
    for kt in range(16):
        pc = (kt % 2) * 1024
        g = c.pb(pc, pc + 1024)
        for ic in range(NCH):
            for half in range(2):
                E("pe", lambda e, ic=ic, half=half, kt=kt, pc=pc: e.matmul(
                    c.pr(0, 128, pc + half * 512, pc + half * 512 + 512),
                    c.xl[ic][:, kt * 128:(kt + 1) * 128],
                    c.ws[:, ic * 1024 + half * 512:
                         ic * 1024 + half * 512 + 512],
                    start=(ic == 0), stop=(ic == NCH - 1 and
                                           W.get("bv_row") is None)),
                  reads=[c.xlB[ic], c.wsB], writes=g, tag=f"{tag}_vmm")
        if W.get("bv_row") is not None:
            for half in range(2):
                E("pe", lambda e, half=half, pc=pc: e.matmul(
                    c.pr(0, 128, pc + half * 512, pc + half * 512 + 512),
                    c.ones_row16,
                    c.sb[0:1, half * 512:half * 512 + 512],
                    start=False, stop=True),
                  reads=[c.sbB], writes=g, tag=f"{tag}_vbias")
        base = kt * 1056
        E("pool", lambda e, base=base: e.memset(
            c.sa[:, base:base + 1056].rearrange(
                "p (h x) -> p h x", x=66)[:, :, 64:66], 1.0),
          writes=[c.saB], tag=f"{tag}_vones")
        E("act", lambda e, base=base, pc=pc: e.activation(
            c.sa[:, base:base + 1056].rearrange(
                "p (h x) -> p h x", x=66)[:, :, 0:64],
            c.pr(0, 128, pc, pc + 1024).rearrange(
                "p (h x) -> p h x", x=64),
            AF.Copy),
          reads=g + [c.saB], writes=[c.saB], tag=f"{tag}_vev")
    QO, KO, PT = 0, 2048, 4096
    for gi in range(8):
        for ic in range(NCH):
            stage_cast(c, W["wq2"], ic * 128, 128, gi * 128, 128,
                       c.ws, c.wsB, ic * 128, f"{tag}q{gi}_{ic}",
                       sf_off=(ic % 4) * 512)
            if not W.get("k_in_ua"):
                stage_cast(c, W["wk2"], ic * 128, 128, gi * 128, 128,
                           c.ws, c.wsB, 1024 + ic * 128, f"{tag}k{gi}_{ic}",
                           sf_off=(ic % 4) * 512)
        for which, wsoff, dst, bias in (
                ("q", 0, QO, W.get("bq_pp")),
                ("k", 1024, KO, W.get("bk_pp"))):
            if which == "k" and W.get("k_in_ua"):
                continue
            for tq in range(4):
                pc = 2048 + (tq % 2) * 512
                g = c.pb(pc, pc + 512)
                for ic in range(NCH):
                    E("pe", lambda e, ic=ic, tq=tq, pc=pc, wsoff=wsoff:
                        e.matmul(
                          c.pr(0, 128, pc, pc + 512),
                          c.ws[:, wsoff + ic * 128:wsoff + ic * 128 + 128],
                          c.xl[ic][:, tq * 512:(tq + 1) * 512],
                          start=(ic == 0), stop=(ic == NCH - 1)),
                      reads=[c.wsB, c.xlB[ic]], writes=g,
                      tag=f"{tag}_{which}mm")
                if bias is not None:
                    E("dve", lambda e, tq=tq, pc=pc, dst=dst, gi=gi, b=bias:
                        e.tensor_scalar(
                          c.sb[:, dst + tq * 512:dst + (tq + 1) * 512],
                          c.pr(0, 128, pc, pc + 512),
                          b[0][:, gi:gi + 1], None, ALU.add),
                      reads=g + [bias[1]], writes=[c.sbB],
                      tag=f"{tag}_{which}b")
                else:
                    E("act", lambda e, tq=tq, pc=pc, dst=dst: e.copy(
                        c.sb[:, dst + tq * 512:dst + (tq + 1) * 512],
                        c.pr(0, 128, pc, pc + 512)),
                      reads=g, writes=[c.sbB], tag=f"{tag}_{which}ev")
        for hh in range(2):
            h = gi * 2 + hh
            kb = hh * 64
            for kt in range(16):
                q0 = kt * 128
                blocks = []
                nb = min((q0 // 512 + 1) * 512, 2048)
                blocks.append((q0, nb))
                while nb < 2048:
                    blocks.append((nb, min(nb + 512, 2048)))
                    nb += 512
                for bi, (qa, qb) in enumerate(blocks):
                    pc = (kt % 2) * 512
                    g = c.pb(pc, pc + 512)
                    if W.get("k_in_ua"):
                        lk = c.ua[gi][kb:kb + 64, kt * 128:(kt + 1) * 128]
                        lkB = [c.uaB[gi]]
                    else:
                        lk = c.sb[kb:kb + 64,
                                  KO + kt * 128:KO + (kt + 1) * 128]
                        lkB = [c.sbB]
                    E("pe", lambda e, lk=lk, qa=qa, qb=qb, pc=pc, kb=kb:
                        e.matmul(c.pr(0, 128, pc, pc + qb - qa), lk,
                                 c.sb[kb:kb + 64, QO + qa:QO + qb],
                                 start=True, stop=True),
                      reads=lkB + [c.sbB], writes=g, tag=f"{tag}_smm")
                    po = PT + (bi % 2) * 512
                    if not masked:
                        E("act", lambda e, pc=pc, po=po, qa=qa, qb=qb:
                            e.activation(c.sb[:, po:po + qb - qa],
                                         c.pr(0, 128, pc, pc + qb - qa),
                                         AF.Exp, bias=csh_col,
                                         scale=0.125),
                          reads=g + [c.ctB_], writes=[c.sbB],
                          tag=f"{tag}_exp")
                    else:
                        E("act", lambda e, pc=pc, qa=qa, qb=qb, kt=kt:
                            e.activation(c.sf[:, 0:qb - qa],
                                         c.pr(0, 128, pc, pc + qb - qa),
                                         AF.Copy,
                                         scale=c.ct[:, KKD + kt:KKD + kt + 1]),
                          reads=g + [c.ctB_], writes=[c.sfB],
                          tag=f"{tag}_t1")
                        E("dve", lambda e, qa=qa, qb=qb, kt=kt:
                            e.tensor_scalar(c.sf[:, 512:512 + qb - qa],
                                            c.sf[:, 0:qb - qa],
                                            c.ct[:, KKC + kt:KKC + kt + 1],
                                            None, ALU.add),
                          reads=[c.sfB, c.ctB_], writes=[c.sfB],
                          tag=f"{tag}_t1b")
                        E("pool", lambda e, qa=qa, qb=qb:
                            e.tensor_tensor(c.sf[:, 0:qb - qa],
                                            c.sf[:, 512:512 + qb - qa],
                                            c.sb[:, 2048 + qa:2048 + qb],
                                            ALU.mult),
                          reads=[c.sfB, c.sbB], writes=[c.sfB],
                          tag=f"{tag}_t2")
                        E("act", lambda e, po=po, qa=qa, qb=qb:
                            e.activation(c.sb[:, po:po + qb - qa],
                                         c.sf[:, 0:qb - qa],
                                         AF.Exp, bias=c60_col),
                          reads=[c.sfB, c.ctB_], writes=[c.sbB],
                          tag=f"{tag}_exp")
                    if bi == 0:
                        E("pool", lambda e, po=po: e.tensor_tensor(
                            c.sb[:, po:po + 128], c.sb[:, po:po + 128],
                            c.ctb[:, TRI:TRI + 128], ALU.mult),
                          reads=[c.sbB, c.ctbB], writes=[c.sbB],
                          tag=f"{tag}_tri")
                    E("pe", lambda e, kt=kt, h=h, po=po, qa=qa, qb=qb:
                        e.matmul(
                          c.pr(0, 65, 2048 + qa, 2048 + qb),
                          c.sa[:, kt * 1056 + h * 66:
                               kt * 1056 + h * 66 + 65],
                          c.sb[:, po:po + qb - qa],
                          start=(kt == 0),
                          stop=(kt == min(15, (qb - 1) // 128))),
                      reads=[c.saB, c.sbB],
                      writes=c.pb(2048 + qa, 2048 + qb), tag=f"{tag}_av")
            gctx = c.pb(2048, 4096)
            E("dve", lambda e: e.reciprocal(c.sf[0:1, :],
                                            c.pr(64, 65, 2048, 4096)),
              reads=gctx, writes=[c.sfB], tag=f"{tag}_rcp")
            for j in range(4):
                E("pe", lambda e, j=j: e.matmul(
                    c.pr(0, 64, j * 512, (j + 1) * 512),
                    c.ones_row[0:1, 0:64],
                    c.sf[0:1, j * 512:(j + 1) * 512], start=True, stop=True),
                  reads=[c.sfB, c.ctB_],
                  writes=c.pb(j * 512, (j + 1) * 512), tag=f"{tag}_rbc")
            E("act", lambda e: e.copy(c.sf[0:64, :], c.pr(0, 64, 0, 2048)),
              reads=c.pb(0, 2048), writes=[c.sfB], tag=f"{tag}_rbe")
            E("dve", lambda e, gi=gi, hh=hh: e.tensor_tensor(
                c.ua[gi][hh * 64:(hh + 1) * 64, :],
                c.pr(0, 64, 2048, 4096), c.sf[0:64, :], ALU.mult),
              reads=gctx + [c.sfB], writes=[c.uaB[gi]], tag=f"{tag}_ctx")
    ua_l = [c.ua[i][:] for i in range(NCH)]
    bo = W.get("bo_pp")

    def consume_o(oc, tq0, width, c0, guards):
        if bo is not None:
            E("dve", lambda e: e.scalar_tensor_tensor(
                c.xc[oc][:, tq0:tq0 + width], c.pr(0, 128, c0, c0 + width),
                bo[0][:, oc:oc + 1], c.xc[oc][:, tq0:tq0 + width],
                ALU.add, ALU.add),
              reads=guards + [bo[1], c.xcB[oc]], writes=[c.xcB[oc]],
              tag=f"{tag}_res")
        else:
            E("dve", lambda e: e.tensor_tensor(
                c.xc[oc][:, tq0:tq0 + width],
                c.pr(0, 128, c0, c0 + width),
                c.xc[oc][:, tq0:tq0 + width], ALU.add),
              reads=guards + [c.xcB[oc]], writes=[c.xcB[oc]],
              tag=f"{tag}_res")

    emit_proj(c, W["wo"], ua_l, c.uaB, NCH, consume_o, f"{tag}o")


def emit_l0_attn_a(c, W):
    emit_ln(c, c.xc, c.xcB, c.xl, c.xlB)
    bq, bqB = c.alloc("bq", [128, 8], F32)
    bk, bkB = c.alloc("bk", [128, 8], F32)
    bo, boB = c.alloc("bo", [128, 8], F32)
    E = c.E
    E("sync", lambda e: e.dma_start(bq[:], W["bq_pp"]), writes=[bqB], dma=True)
    E("sync", lambda e: e.dma_start(bk[:], W["bk_pp"]), writes=[bkB], dma=True)
    E("sync", lambda e: e.dma_start(bo[:], W["bo_pp"]), writes=[boB], dma=True)
    E("sync", lambda e: e.dma_start(c.ctb[:, 2048 - 128:2048], W["tri"]),
      writes=[c.ctbB], dma=True, tag="tri")
    W2 = dict(W)
    W2["bq_pp"] = (bq, bqB)
    W2["bk_pp"] = (bk, bkB)
    W2["bo_pp"] = (bo, boB)
    emit_mha(c, W2, "s1", masked=False)


def prep_l0_attn_a(inputs):
    f32 = np.float32
    tri = np.triu(np.ones((128, 128), f32)).astype(BF)
    return {
        "wq2": np.ascontiguousarray(
            inputs["L0_attn_a_Wq"].transpose(1, 0, 2).reshape(1024, 1024),
            f32),
        "wk2": np.ascontiguousarray(
            inputs["L0_attn_a_Wk"].transpose(1, 0, 2).reshape(1024, 1024),
            f32),
        "wv2": np.ascontiguousarray(
            inputs["L0_attn_a_Wv"].transpose(1, 0, 2).reshape(1024, 1024),
            f32),
        "wo": np.ascontiguousarray(inputs["L0_attn_a_Wo"], f32),
        "bq_pp": np.ascontiguousarray(
            inputs["L0_attn_a_bq"].reshape(8, 128).T, f32),
        "bk_pp": np.ascontiguousarray(
            inputs["L0_attn_a_bk"].reshape(8, 128).T, f32),
        "bv_row": np.ascontiguousarray(
            inputs["L0_attn_a_bv"].reshape(1, 1024), f32),
        "bo_pp": np.ascontiguousarray(
            inputs["L0_attn_a_bo"].reshape(8, 128).T, f32),
        "tri": tri,
    }


def emit_l1_attn_a(c, W):
    E = c.E
    emit_ln(c, c.xc, c.xcB, c.xl, c.xlB)
    E("sync", lambda e: e.dma_start(c.ctb[:, 512:640], W["tri"]),
      writes=[c.ctbB], dma=True, tag="s5tri")
    bo, boB = c.alloc("s5bo", [128, 8], F32)
    E("sync", lambda e: e.dma_start(bo[:], W["bout_pp"]), writes=[boB],
      dma=True)
    br1, br1B = c.alloc("s5br1", [128, 8], F32)
    E("sync", lambda e: e.dma_start(br1[:], W["br1_pp"]), writes=[br1B],
      dma=True)
    br2, br2B = c.alloc("s5br2", [1, 8], F32)
    E("sync", lambda e: e.dma_start(br2[:], W["br2_pp"]), writes=[br2B],
      dma=True)
    wqkvT = W["wqkvT"]
    for gi in range(8):
        for ic in range(NCH):
            stage_cast(c, wqkvT, ic * 128, 128, 1024 + gi * 128, 128,
                       c.ws, c.wsB, ic * 128, f"s5K{gi}_{ic}",
                       sf_off=(ic % 4) * 512)
        for tq in range(4):
            pc = (tq % 2) * 512
            g = c.pb(pc, pc + 512)
            for ic in range(NCH):
                E("pe", lambda e, ic=ic, tq=tq, pc=pc: e.matmul(
                    c.pr(0, 128, pc, pc + 512),
                    c.ws[:, ic * 128:ic * 128 + 128],
                    c.xl[ic][:, tq * 512:(tq + 1) * 512],
                    start=(ic == 0), stop=(ic == NCH - 1)),
                  reads=[c.wsB, c.xlB[ic]], writes=g, tag="s5_kmm")
            E("act", lambda e, gi=gi, tq=tq, pc=pc: e.copy(
                c.ua[gi][:, tq * 512:(tq + 1) * 512],
                c.pr(0, 128, pc, pc + 512)),
              reads=g, writes=[c.uaB[gi]], tag="s5_kev")
        E("dve", lambda e, gi=gi: e.reduce_sum(
            c.ct[:, 264 + gi * 32:264 + (gi + 1) * 32],
            c.ua[gi][:].rearrange("p (b t) -> p b t", t=64), axis=AX.X),
          reads=[c.uaB[gi]], writes=[c.ctB_], tag="s5_blk")
    E("dve", lambda e: e.tensor_scalar(
        c.ct[:, 264:520], c.ct[:, 264:520], 1.0 / 64, None, ALU.mult),
      reads=[c.ctB_], writes=[c.ctB_], tag="s5_blkm")
    E("pool", lambda e: e.tensor_copy(c.ctb[:, 640:896], c.ct[:, 264:520]),
      reads=[c.ctB_], writes=[c.ctbB], tag="s5_blkc")
    for ic in range(NCH):
        stage_cast(c, W["wr1T"], ic * 128, 128, 0, 128,
                   c.ws, c.wsB, ic * 128, f"s5r1_{ic}",
                   sf_off=(ic % 4) * 512)
    stage_cast(c, W["wr2T8"], 0, 128, 0, 8, c.ws, c.wsB, 1024, "s5r2")
    gr = c.pb(1024, 1536)
    for ic in range(NCH):
        E("pe", lambda e, ic=ic: e.matmul(
            c.pr(0, 128, 1024, 1056),
            c.ws[:, ic * 128:ic * 128 + 128],
            c.ctb[:, 640 + ic * 32:640 + (ic + 1) * 32],
            start=(ic == 0), stop=(ic == NCH - 1)),
          reads=[c.wsB, c.ctbB], writes=gr, tag="s5_r1mm")
    E("dve", lambda e: e.tensor_scalar(
        c.sf[:, 0:32], c.pr(0, 128, 1024, 1056), br1[:, 0:1], None, ALU.add),
      reads=gr + [br1B], writes=[c.sfB], tag="s5_r1b")
    E("act", lambda e: e.activation(c.ctb[:, 896:928], c.sf[:, 0:32],
                                    AF.Relu),
      reads=[c.sfB], writes=[c.ctbB], tag="s5_r1relu")
    E("pe", lambda e: e.matmul(
        c.pr(0, 1, 1536, 1568), c.ws[:, 1024:1025], c.ctb[:, 896:928],
        start=True, stop=True),
      reads=[c.wsB, c.ctbB], writes=c.pb(1536, 2048), tag="s5_r2mm")
    E("dve", lambda e: e.tensor_scalar(
        c.sf[0:1, 0:32], c.pr(0, 1, 1536, 1568), br2[0:1, 0:1], None,
        ALU.add),
      reads=c.pb(1536, 2048) + [br2B], writes=[c.sfB], tag="s5_r2b")
    E("act", lambda e: e.sign(c.sf[0:1, 32:64], c.sf[0:1, 0:32]),
      reads=[c.sfB], writes=[c.sfB], tag="s5_sign")
    E("dve", lambda e: e.tensor_scalar(c.sf[0:1, 32:64], c.sf[0:1, 32:64],
                                       0.0, None, ALU.max),
      reads=[c.sfB], writes=[c.sfB], tag="s5_keep")
    cc_in = c.dint("s5cc_in", [1, 32], F32)
    cc_out = c.dint("s5cc_out", [8, 32], F32)
    ccinB = c.kb.buf("ccin")
    ccoutB = c.kb.buf("ccout")
    E("sync", lambda e: e.dma_start(cc_in, c.sf[0:1, 32:64]),
      reads=[c.sfB], writes=[ccinB], dma=True, tag="s5_ccw")
    E("pool", lambda e: e.collective_compute(
        "AllGather", ALU.bypass, [list(range(B))], [cc_in.opt()],
        [cc_out.opt()]),
      reads=[ccinB], writes=[ccoutB], coll=True, tag="s5_ag")
    E("sync", lambda e: e.dma_start(c.sf[0:1, 0:32], cc_out[0:1, :]),
      reads=[ccoutB], writes=[c.sfB], dma=True, tag="s5_ccr")
    k01c, k01cB = c.alloc("k01c", [32, 8], F32)
    k01b, k01bB = c.alloc("k01b", [32, 8], BF16)
    E("pool", lambda e: e.dma_start(k01c[0:32, 0:1], c.sf[0:1, 0:32],
                                    allow_slow_non_contiguous=True),
      reads=[c.sfB], writes=[k01cB], dma=True, tag="s5_k01t")
    E("pool", lambda e: e.tensor_copy(k01b[:, 0:1], k01c[:, 0:1]),
      reads=[k01cB], writes=[k01bB], tag="s5_k01c")
    E("sync", lambda e: e.dma_start(c.ws[0:32, 1024:3072], W["esel32"]),
      writes=[c.wsB], dma=True, tag="s5_e32")
    for j in range(4):
        E("pe", lambda e, j=j: e.matmul(
            c.pr(0, 1, j * 512, (j + 1) * 512), k01b[:, 0:1],
            c.ws[0:32, 1024 + j * 512:1024 + (j + 1) * 512],
            start=True, stop=True),
          reads=[k01bB, c.wsB], writes=c.pb(j * 512, (j + 1) * 512),
          tag="s5_kkrow")
    E("act", lambda e: e.copy(c.sf[0:1, :], c.pr(0, 1, 0, 2048)),
      reads=c.pb(0, 2048), writes=[c.sfB], tag="s5_kkev")
    for j in range(4):
        E("pe", lambda e, j=j: e.matmul(
            c.pr(0, 128, 2048 + j * 512, 2048 + (j + 1) * 512),
            c.ones_row, c.sf[0:1, j * 512:(j + 1) * 512],
            start=True, stop=True),
          reads=[c.sfB, c.ctB_], writes=c.pb(2048 + j * 512,
                                             2048 + (j + 1) * 512),
          tag="s5_kqbc")
    E("act", lambda e: e.copy(c.sb[:, 2048:4096], c.pr(0, 128, 2048, 4096)),
      reads=c.pb(2048, 4096), writes=[c.sbB], tag="s5_kqev")
    for kt in range(16):
        E("pool", lambda e, kt=kt: e.dma_start(
            c.ct[:, 520 + kt:521 + kt], c.sf[0:1, kt * 128:(kt + 1) * 128],
            allow_slow_non_contiguous=True),
          reads=[c.sfB], writes=[c.ctB_], dma=True, tag="s5_kkcol")
    E("dve", lambda e: e.tensor_scalar(c.ct[:, 536:552], c.ct[:, 520:536],
                                       1.0 / 32, None, ALU.mult),
      reads=[c.ctB_], writes=[c.ctB_], tag="s5_kkd")
    E("dve", lambda e: e.tensor_scalar(c.ct[:, 552:568], c.ct[:, 520:536],
                                       60.0, None, ALU.mult),
      reads=[c.ctB_], writes=[c.ctB_], tag="s5_kkc")
    W2 = dict(W)
    W2["wq2"] = wqkvT[0:1024, 0:1024]
    W2["wv2"] = wqkvT[0:1024, 2048:3072]
    W2["wo"] = W["woT"]
    W2["bo_pp"] = (bo, boB)
    W2["k_in_ua"] = True
    W2["tri_off"] = 512
    emit_mha(c, W2, "s5", masked=True)


def prep_l1_attn_a(inputs):
    f32 = np.float32
    tri = np.triu(np.ones((128, 128), f32)).astype(BF)
    esel32 = np.zeros((32, 2048), f32)
    for b_ in range(32):
        esel32[b_, b_ * 64:(b_ + 1) * 64] = 1.0
    wr2 = np.zeros((128, 8), f32)
    wr2[:, 0:1] = inputs["L1_attn_a_w_r2"].T
    br2 = np.zeros((1, 8), f32)
    br2[0, 0] = inputs["L1_attn_a_b_r2"][0]
    return {
        "wqkvT": np.ascontiguousarray(inputs["L1_attn_a_w_qkv"].T, f32),
        "wr1T": np.ascontiguousarray(inputs["L1_attn_a_w_r1"].T, f32),
        "wr2T8": wr2,
        "br1_pp": np.ascontiguousarray(
            np.repeat(inputs["L1_attn_a_b_r1"].reshape(128, 1), 8, axis=1),
            f32),
        "br2_pp": br2,
        "woT": np.ascontiguousarray(inputs["L1_attn_a_w_out"].T, f32),
        "bout_pp": np.ascontiguousarray(
            inputs["L1_attn_a_b_out"].reshape(8, 128).T, f32),
        "tri": tri,
        "esel32": esel32.astype(BF),
    }


def emit_l0_ffn(c, W, it=0):
    E = c.E
    tg = f"moe{it}"
    emit_ln(c, c.xc, c.xcB, c.xl, c.xlB)
    if W.get("spill"):
        for i in range(NCH):
            E("sync", lambda e, i=i: e.dma_start(
                W["x_dram"][i * 128:(i + 1) * 128, :], c.xc[i][:]),
              reads=[c.xcB[i]], dma=True, tag=f"{tg}_spill{i}")
    idf, idfB = c.alloc("idf", [128, 128], F32)
    E("sync", lambda e: e.dma_start(idf[:], W["ident"]), writes=[idfB],
      dma=True)
    b1t, b1tB = c.alloc("b1t", [128, 256], F32)
    b2t, b2tB = c.alloc("b2t", [128, 64], F32)
    gbt, gbtB = c.alloc("gbt2", [128, 16], F32)
    bgt, bgtB = c.alloc("bgt", [8, 8], F32)
    E("sync", lambda e: e.dma_start(b1t[:], W["b1_pt"]), writes=[b1tB],
      dma=True)
    E("sync", lambda e: e.dma_start(b2t[:], W["b2_pt"]), writes=[b2tB],
      dma=True)
    E("sync", lambda e: e.dma_start(gbt[:], W["gb_pp"]), writes=[gbtB],
      dma=True)
    E("sync", lambda e: e.dma_start(bgt[:], W["bg8"]), writes=[bgtB],
      dma=True)
    stage_cast(c, W["wg_pp"], 0, 128, 0, 64, c.ctb, c.ctbB, 512, f"{tg}wg")
    gL = c.pb(0, 2048)
    for ic in range(NCH):
        for j in range(4):
            E("pe", lambda e, ic=ic, j=j: e.matmul(
                c.pr(0, 8, j * 512, (j + 1) * 512),
                c.ctb[:, 512 + ic * 8:512 + (ic + 1) * 8],
                c.xl[ic][:, j * 512:(j + 1) * 512],
                start=(ic == 0), stop=(ic == NCH - 1)),
              reads=[c.ctbB, c.xlB[ic]], writes=gL, tag=f"{tg}_gmm")
    E("dve", lambda e: e.tensor_scalar(c.sf[0:8, :], c.pr(0, 8, 0, 2048),
                                       bgt[0:8, 0:1], None, ALU.add),
      reads=gL + [bgtB], writes=[c.sfB], tag=f"{tg}_bg")
    gT = c.pb(2048, 2560)
    for i in range(16):
        E("pe", lambda e, i=i: e.transpose(
            c.pr(0, 128, 2048 + (i % 16) * 8, 2048 + (i % 16) * 8 + 8),
            c.sf[0:8, i * 128:(i + 1) * 128], idf[0:8, 0:8]),
          reads=[c.sfB, idfB], writes=gT, tag=f"{tg}_ltr")
    E("act", lambda e: e.copy(c.ct[:, 264:392], c.pr(0, 128, 2048, 2176)),
      reads=gT, writes=[c.ctB_], tag=f"{tg}_ltr_ev")
    for i in range(16):
        L = c.ct[:, 264 + i * 8:272 + i * 8]
        CM = c.ct[:, 392 + i * 8:400 + i * 8]
        nm1 = c.ct[:, 521:522]
        m1 = c.ct[:, 520:521]
        m2 = c.ct[:, 522:523]
        den = c.ct[:, 523:524]
        sc1 = c.ct[:, 524:532]
        sc2 = c.ct[:, 532:540]
        E("dve", lambda e, L=L, nm1=nm1: e.reduce_max(nm1, L, axis=AX.X,
                                                      negate=True),
          reads=[c.ctB_], writes=[c.ctB_], tag=f"{tg}_m1")
        E("dve", lambda e, m1=m1, nm1=nm1: e.tensor_scalar(
            m1, nm1, -1.0, None, ALU.mult),
          reads=[c.ctB_], writes=[c.ctB_], tag=f"{tg}_m1b")
        E("dve", lambda e, L=L, m1=m1, sc1=sc1: e.tensor_scalar(
            sc1, L, m1, None, ALU.is_ge),
          reads=[c.ctB_], writes=[c.ctB_], tag=f"{tg}_ge1")
        E("dve", lambda e, L=L, sc1=sc1, sc2=sc2: e.scalar_tensor_tensor(
            sc2, sc1, -1e30, L, ALU.mult, ALU.add),
          reads=[c.ctB_], writes=[c.ctB_], tag=f"{tg}_lp")
        E("dve", lambda e, m2=m2, sc2=sc2: e.reduce_max(m2, sc2, axis=AX.X),
          reads=[c.ctB_], writes=[c.ctB_], tag=f"{tg}_m2")
        E("dve", lambda e, m2=m2, nm1=nm1, den=den: e.tensor_tensor(
            den, m2, nm1, ALU.add),
          reads=[c.ctB_], writes=[c.ctB_], tag=f"{tg}_dm")
        E("act", lambda e, den=den: e.activation(den, den, AF.Exp),
          reads=[c.ctB_], writes=[c.ctB_], tag=f"{tg}_de")
        E("dve", lambda e, den=den: e.tensor_scalar(den, den, 1.0, None,
                                                    ALU.add),
          reads=[c.ctB_], writes=[c.ctB_], tag=f"{tg}_d1")
        E("dve", lambda e, den=den: e.reciprocal(den, den),
          reads=[c.ctB_], writes=[c.ctB_], tag=f"{tg}_dr")
        E("dve", lambda e, L=L, m2=m2, sc1=sc1: e.tensor_scalar(
            sc1, L, m2, None, ALU.is_ge),
          reads=[c.ctB_], writes=[c.ctB_], tag=f"{tg}_ge2")
        E("act", lambda e, L=L, nm1=nm1, sc2=sc2: e.activation(
            sc2, L, AF.Exp, bias=nm1),
          reads=[c.ctB_], writes=[c.ctB_], tag=f"{tg}_ex")
        E("dve", lambda e, sc2=sc2, den=den, CM=CM: e.tensor_scalar(
            CM, sc2, den, None, ALU.mult),
          reads=[c.ctB_], writes=[c.ctB_], tag=f"{tg}_cm1")
        E("dve", lambda e, CM=CM, sc1=sc1: e.tensor_tensor(
            CM, CM, sc1, ALU.mult),
          reads=[c.ctB_], writes=[c.ctB_], tag=f"{tg}_cm2")
    gT2 = c.pb(2048, 4096)
    for i in range(16):
        E("pe", lambda e, i=i: e.transpose(
            c.pr(0, 8, 2048 + i * 128, 2048 + (i + 1) * 128),
            c.ct[:, 392 + i * 8:400 + i * 8], idf[:]),
          reads=[c.ctB_, idfB], writes=gT2, tag=f"{tg}_ctr")
    E("act", lambda e: e.copy(c.sf[0:8, :], c.pr(0, 8, 2048, 4096)),
      reads=gT2, writes=[c.sfB], tag=f"{tg}_cev")
    cmb_dram = c.dint(f"cmb{it}", [8, 2048], F32)
    cmbB = c.kb.buf(f"cmb{it}")
    E("sync", lambda e: e.dma_start(cmb_dram, c.sf[0:8, :]),
      reads=[c.sfB], writes=[cmbB], dma=True, tag=f"{tg}_cst")
    for i in range(NCH):
        E("pool", lambda e, i=i: e.tensor_copy(c.ua[i][:], c.xl[i][:]),
          reads=[c.xlB[i]], writes=[c.uaB[i]], tag=f"{tg}_ui")
    for ex in range(E_EXP):
        w1ap = W["w1"]
        w2ap = W["w2"]
        w1r = ex * 1024
        w2r = ex * 4096
        for j in range(4):
            E("pool", lambda e, ex=ex, j=j: e.dma_start(
                c.ct[0:1, 264:776],
                cmb_dram[ex:ex + 1, j * 512:(j + 1) * 512]),
              reads=[cmbB], writes=[c.ctB_], dma=True, tag=f"{tg}_crow")
            E("pe", lambda e, j=j: e.matmul(
                c.pr(0, 128, 1024, 1536), c.ones_row, c.ct[0:1, 264:776],
                start=True, stop=True),
              reads=[c.ctB_], writes=c.pb(1024, 1536), tag=f"{tg}_cbc")
            E("act", lambda e, j=j: e.copy(
                c.sb[:, j * 512:(j + 1) * 512],
                c.pr(0, 128, 1024, 1536)),
              reads=c.pb(1024, 1536), writes=[c.sbB], tag=f"{tg}_cbev")
        for hc in range(32):
            soff = (hc % 2) * 1024
            E("sync", lambda e, hc=hc, soff=soff, w2ap=w2ap, w2r=w2r:
                e.dma_start(
                  c.sf[:, soff:soff + 1024],
                  w2ap[w2r + hc * 128:w2r + (hc + 1) * 128, 0:1024]),
              writes=[c.sfB], dma=True, tag=f"{tg}_w2l")
            E("pool", lambda e, hc=hc, soff=soff: e.tensor_copy(
                c.xc[hc // 4][:].bitcast(BF16)[:, (hc % 4) * 1024:
                                               (hc % 4) * 1024 + 1024],
                c.sf[:, soff:soff + 1024]),
              reads=[c.sfB], writes=[c.xcB[hc // 4]], tag=f"{tg}_w2c")
        for tq in range(4):
            t0 = tq * 512
            for gph in range(8):
                half = (gph % 2) * 4096
                for ic in range(NCH):
                    E("sync", lambda e, ic=ic, gph=gph, half=half,
                        w1ap=w1ap, w1r=w1r: e.dma_start(
                          c.sf[:, (ic % 4) * 512:(ic % 4) * 512 + 512],
                          w1ap[w1r + ic * 128:w1r + (ic + 1) * 128,
                               gph * 512:(gph + 1) * 512]),
                      writes=[c.sfB], dma=True, tag=f"{tg}_w1l")
                    E("pool", lambda e, ic=ic, half=half: e.tensor_copy(
                        c.ws[:, half + ic * 512:half + (ic + 1) * 512],
                        c.sf[:, (ic % 4) * 512:(ic % 4) * 512 + 512]),
                      reads=[c.sfB], writes=[c.wsB], tag=f"{tg}_w1c")
                for hcl in range(4):
                    hc = gph * 4 + hcl
                    pc = (hc % 2) * 512
                    gH = c.pb(pc, pc + 512)
                    for ic in range(NCH):
                        E("pe", lambda e, ic=ic, hcl=hcl, half=half, pc=pc,
                            t0=t0: e.matmul(
                              c.pr(0, 128, pc, pc + 512),
                              c.ws[:, half + ic * 512 + hcl * 128:
                                   half + ic * 512 + hcl * 128 + 128],
                              c.xl[ic][:, t0:t0 + 512],
                              start=(ic == 0), stop=(ic == NCH - 1)),
                          reads=[c.wsB, c.xlB[ic]], writes=gH,
                          tag=f"{tg}_amm")
                    E("act", lambda e, hc=hc, pc=pc, ex=ex: e.activation(
                        c.sa[:, hc * 512:(hc + 1) * 512],
                        c.pr(0, 128, pc, pc + 512), AF.Gelu,
                        bias=b1t[:, ex * 32 + hc:ex * 32 + hc + 1]),
                      reads=c.pb(pc, pc + 512) + [b1tB], writes=[c.saB],
                      tag=f"{tg}_gelu")
            for w in range(2):
                gY = c.pb(2048, 4096)
                for hc in range(32):
                    for k in range(4):
                        dc = w * 4 + k
                        E("pe", lambda e, hc=hc, k=k, dc=dc: e.matmul(
                            c.pr(0, 128, 2048 + k * 512, 2048 + k * 512 + 512),
                            c.xc[hc // 4][:].bitcast(BF16)[
                                :, (hc % 4) * 1024 + dc * 128:
                                (hc % 4) * 1024 + dc * 128 + 128],
                            c.sa[:, hc * 512:(hc + 1) * 512],
                            start=(hc == 0), stop=(hc == 31)),
                          reads=[c.xcB[hc // 4], c.saB],
                          writes=c.pb(2048 + k * 512, 2048 + k * 512 + 512),
                          tag=f"{tg}_bmm")
                for k in range(4):
                    dc = w * 4 + k
                    gk = c.pb(2048 + k * 512, 2048 + k * 512 + 512)
                    E("dve", lambda e, k=k, dc=dc, ex=ex, t0=t0:
                        e.scalar_tensor_tensor(
                          c.sb[:, 2048 + k * 512:2048 + (k + 1) * 512],
                          c.pr(0, 128, 2048 + k * 512, 2048 + k * 512 + 512),
                          b2t[:, ex * 8 + dc:ex * 8 + dc + 1],
                          c.sb[:, t0:t0 + 512],
                          ALU.add, ALU.mult),
                      reads=gk + [b2tB, c.sbB], writes=[c.sbB],
                      tag=f"{tg}_yev")
                    E("pool", lambda e, k=k, dc=dc, t0=t0: e.tensor_tensor(
                        c.ua[dc][:, t0:t0 + 512], c.ua[dc][:, t0:t0 + 512],
                        c.sb[:, 2048 + k * 512:2048 + (k + 1) * 512],
                        ALU.add),
                      reads=[c.sbB, c.uaB[dc]], writes=[c.uaB[dc]],
                      tag=f"{tg}_uacc")
    gA = c.pb(0, 2048)
    gB = c.pb(2048, 4096)
    for i in range(NCH):
        E("pool", lambda e, i=i: e.tensor_tensor(
            c.sb[:, 0:2048], c.ua[i][:], c.ua[i][:], ALU.mult),
          reads=[c.uaB[i]], writes=[c.sbB], tag=f"{tg}_usq")
        for j in range(4):
            E("pe", lambda e, i=i, j=j: e.matmul(
                c.pr(0, 1, j * 512, (j + 1) * 512), c.ones_col16,
                c.ua[i][:, j * 512:(j + 1) * 512],
                start=(i == 0), stop=(i == NCH - 1)),
              reads=[c.uaB[i], c.ctbB], writes=gA, tag=f"{tg}_us")
            E("pe", lambda e, i=i, j=j: e.matmul(
                c.pr(32, 33, j * 512, (j + 1) * 512), c.ones_col16,
                c.sb[:, j * 512:(j + 1) * 512],
                start=(i == 0), stop=(i == NCH - 1)),
              reads=[c.sbB, c.ctbB], writes=gA, tag=f"{tg}_us2")
    r_mean = c.sf[0:1, :]
    r_a = c.ws[0:1, 0:4096].bitcast(F32)
    r_b = c.ws[0:1, 4096:8192].bitcast(F32)
    eps_col = c.const(LN_EPS)
    E("dve", lambda e: e.tensor_scalar(r_mean, c.pr(0, 1, 0, 2048),
                                       1.0 / D, None, ALU.mult),
      reads=gA, writes=[c.sfB], tag=f"{tg}_fmean")
    E("dve", lambda e: e.tensor_scalar(r_a, c.pr(32, 33, 0, 2048),
                                       1.0 / D, None, ALU.mult),
      reads=gA, writes=[c.wsB], tag=f"{tg}_fms")
    E("dve", lambda e: e.scalar_tensor_tensor(
        r_b, r_mean, -1.0, r_mean, ALU.mult, ALU.mult),
      reads=[c.sfB], writes=[c.wsB], tag=f"{tg}_fnm2")
    E("dve", lambda e: e.tensor_tensor(r_a, r_a, r_b, ALU.add),
      reads=[c.wsB], writes=[c.wsB], tag=f"{tg}_fvar")
    E("act", lambda e: e.activation(r_b, r_a, AF.Ln, bias=eps_col[0:1, :]),
      reads=[c.wsB, c.ctB_], writes=[c.wsB], tag=f"{tg}_fln")
    E("act", lambda e: e.activation(r_a, r_b, AF.Exp, scale=-0.5),
      reads=[c.wsB], writes=[c.wsB], tag=f"{tg}_frstd")
    E("dve", lambda e: e.scalar_tensor_tensor(
        r_b, r_mean, -1.0, r_a, ALU.mult, ALU.mult),
      reads=[c.sfB, c.wsB], writes=[c.wsB], tag=f"{tg}_fb")
    for j in range(4):
        E("pe", lambda e, j=j: e.matmul(
            c.pr(0, 128, j * 512, (j + 1) * 512), c.ones_row,
            c.ws[0:1, j * 1024:j * 1024 + 1024].bitcast(F32),
            start=True, stop=True),
          reads=[c.wsB, c.ctB_], writes=gA, tag=f"{tg}_fbcA")
        E("pe", lambda e, j=j: e.matmul(
            c.pr(0, 128, 2048 + j * 512, 2048 + (j + 1) * 512), c.ones_row,
            c.ws[0:1, 4096 + j * 1024:4096 + j * 1024 + 1024].bitcast(F32),
            start=True, stop=True),
          reads=[c.wsB, c.ctB_], writes=gB, tag=f"{tg}_fbcB")
    for i in range(NCH):
        E("sync", lambda e, i=i: e.dma_start(
            c.xc[i][:], W["x_dram"][i * 128:(i + 1) * 128, :]),
          writes=[c.xcB[i]], dma=True, tag=f"{tg}_xrl{i}")
        E("dve", lambda e, i=i: e.tensor_tensor(
            c.sf[:], c.ua[i][:], c.pr(0, 128, 0, 2048), ALU.mult),
          reads=[c.uaB[i]] + gA, writes=[c.sfB], tag=f"{tg}_fn1")
        E("dve", lambda e, i=i: e.tensor_tensor(
            c.sf[:], c.sf[:], c.pr(0, 128, 2048, 4096), ALU.add),
          reads=[c.sfB] + gB, writes=[c.sfB], tag=f"{tg}_fn2")
        E("act", lambda e, i=i: e.activation(
            c.sf[:], c.sf[:], AF.Copy, scale=gbt[:, i:i + 1]),
          reads=[c.sfB, gbtB], writes=[c.sfB], tag=f"{tg}_fg")
        E("dve", lambda e, i=i: e.scalar_tensor_tensor(
            c.xc[i][:], c.sf[:], gbt[:, 8 + i:9 + i], c.xc[i][:],
            ALU.add, ALU.add),
          reads=[c.sfB, gbtB, c.xcB[i]], writes=[c.xcB[i]],
          tag=f"{tg}_fres")


def prep_l0_ffn(inputs):
    f32 = np.float32
    bg8 = np.repeat(inputs["L0_ffn_bg"].reshape(8, 1), 8, axis=1)
    gb = np.stack([inputs["L0_ffn_gamma"], inputs["L0_ffn_beta"]])
    return {
        "wg_pp": np.ascontiguousarray(
            inputs["L0_ffn_Wg"].reshape(8, 128, 8).transpose(1, 0, 2)
            .reshape(128, 64), f32),
        "bg8": np.ascontiguousarray(bg8, f32),
        "w1": np.ascontiguousarray(inputs["L0_ffn_W1"].reshape(8192, 4096), f32),
        "w2": np.ascontiguousarray(inputs["L0_ffn_W2"].reshape(32768, 1024), f32),
        "b1_pt": np.ascontiguousarray(
            inputs["L0_ffn_b1"].reshape(8, 32, 128).transpose(2, 0, 1)
            .reshape(128, 256), f32),
        "b2_pt": np.ascontiguousarray(
            inputs["L0_ffn_b2"].reshape(8, 8, 128).transpose(2, 0, 1)
            .reshape(128, 64), f32),
        "gb_pp": np.ascontiguousarray(
            gb.reshape(2, 8, 128).transpose(2, 0, 1).reshape(128, 16), f32),
        "ident": np.eye(128, dtype=f32),
    }


def emit_s7_qkv(c, W, qkv_out, qkv_outB):
    E = c.E
    emit_ln(c, c.xc, c.xcB, c.xl, c.xlB)
    gbt, gbtB = c.alloc("s7gb", [128, 16], F32)
    E("sync", lambda e: e.dma_start(gbt[:], W["gb_pp"]), writes=[gbtB],
      dma=True)
    emit_ln(c, c.xl, c.xlB, c.ua, c.uaB, eps=1e-6, src16=True,
            gb=(gbt, gbtB, 0, 8))
    for t3 in range(3):
        for oc in range(NCH):
            for ic in range(NCH):
                stage_cast(c, W["qkvk"], t3 * 1024 + ic * 128, 128,
                           oc * 128, 128, c.ws, c.wsB,
                           ((oc % 4) * 8 + ic) * 128, f"s7w{t3}_{oc}_{ic}",
                           sf_off=(ic % 4) * 512)
            for tq in range(4):
                pc = (tq % 2) * 512
                g = c.pb(pc, pc + 512)
                for ic in range(NCH):
                    E("pe", lambda e, ic=ic, oc=oc, tq=tq, pc=pc: e.matmul(
                        c.pr(0, 128, pc, pc + 512),
                        c.ws[:, ((oc % 4) * 8 + ic) * 128:
                             ((oc % 4) * 8 + ic) * 128 + 128],
                        c.ua[ic][:, tq * 512:(tq + 1) * 512],
                        start=(ic == 0), stop=(ic == NCH - 1)),
                      reads=[c.wsB, c.uaB[ic]], writes=g, tag="s7_pmm")
                E("act", lambda e, tq=tq, pc=pc: e.copy(
                    c.xl[3][:, tq * 512:(tq + 1) * 512],
                    c.pr(0, 128, pc, pc + 512)),
                  reads=g, writes=[c.xlB[3]], tag="s7_kvev")
            E("sync", lambda e, t3=t3, oc=oc: e.dma_start(
                qkv_out[t3 * 1024 + oc * 128:t3 * 1024 + (oc + 1) * 128, :],
                c.xl[3][:]),
              reads=[c.xlB[3]], writes=[qkv_outB], dma=True, tag="s7_kvst")


def emit_s7_attn(c, W, q_in, kv_all, kv_allB):
    E = c.E
    msk, mskB = c.alloc("s7msk", [128, 8], F32)
    E("sync", lambda e: e.dma_start(msk[:], W["mask_pp"]), writes=[mskB],
      dma=True)
    E("sync", lambda e: e.dma_start(c.ctb[:, 512:528], W["esel8"]),
      writes=[c.ctbB], dma=True, tag="s7e8")
    E("sync", lambda e: e.dma_start(c.ctb[0:16, 528:1552], W["eselT"]),
      writes=[c.ctbB], dma=True, tag="s7eT")
    for oc in range(NCH):
        E("sync", lambda e, oc=oc: e.dma_start(
            c.sa[:, oc * 2048:(oc + 1) * 2048],
            q_in[oc * 128:(oc + 1) * 128, :]),
          writes=[c.saB], dma=True, tag="s7_qld")
    E("sync", lambda e: e.dma_start(c.ws[:], W["sel2"]),
      writes=[c.wsB], dma=True, tag="s7_sel2")
    gL = c.pb(0, 2048)
    for kb in range(B):
        for ic in range(NCH):
            koff = (ic % 2) * 2048
            E("sync", lambda e, kb=kb, ic=ic, koff=koff: e.dma_start(
                c.sb[:, koff:koff + 2048],
                kv_all[kb * 2048 + ic * 128:kb * 2048 + (ic + 1) * 128, :]),
              writes=[c.sbB], dma=True, tag="s7_kld")
            E("dve", lambda e, ic=ic, koff=koff: e.tensor_tensor(
                c.xl[0][:], c.sa[:, ic * 2048:(ic + 1) * 2048],
                c.sb[:, koff:koff + 2048], ALU.mult),
              reads=[c.sbB, c.saB], writes=[c.xlB[0]], tag="s7_prod")
            for j in range(4):
                E("pe", lambda e, kb=kb, ic=ic, j=j: e.matmul(
                    c.pr(0, 128, j * 512, (j + 1) * 512),
                    c.ws[:, (ic * 8 + kb) * 128:(ic * 8 + kb) * 128 + 128],
                    c.xl[0][:, j * 512:(j + 1) * 512],
                    start=(ic == 0 and kb == 0),
                    stop=(ic == NCH - 1 and kb == B - 1)),
                  reads=[c.wsB, c.xlB[0]], writes=gL, tag="s7_lmm")
    E("act", lambda e: e.activation(c.xl[1][:], c.pr(0, 128, 0, 2048),
                                    AF.Exp, bias=msk[:, 0:1], scale=0.125),
      reads=gL + [mskB], writes=[c.xlB[1]], tag="s7_exp")
    for j in range(4):
        E("pe", lambda e, j=j: e.matmul(
            c.pr(0, 16, j * 512, (j + 1) * 512), c.ctb[:, 512:528],
            c.xl[1][:, j * 512:(j + 1) * 512], start=True, stop=True),
          reads=[c.ctbB, c.xlB[1]], writes=gL, tag="s7_rs")

    def _rcp(e):
        with c.nc.allow_low_precision(reason="softmax normalizer in bf16"):
            return e.reciprocal(c.xl[2][0:16, :], c.pr(0, 16, 0, 2048))
    E("dve", _rcp, reads=gL, writes=[c.xlB[2]], tag="s7_rcp")
    E("sync", lambda e: e.dma_start(c.ws[:], W["sel2T"]),
      writes=[c.wsB], dma=True, tag="s7_sel2T")
    for ic in range(NCH):
        for kb in range(B):
            koff = (kb % 2) * 2048
            E("sync", lambda e, kb=kb, ic=ic, koff=koff: e.dma_start(
                c.sb[:, koff:koff + 2048],
                kv_all[kb * 2048 + 1024 + ic * 128:
                       kb * 2048 + 1024 + (ic + 1) * 128, :]),
              writes=[c.sbB], dma=True, tag="s7_vld")
            for j in range(4):
                E("pe", lambda e, kb=kb, ic=ic, j=j: e.matmul(
                    c.pr(0, 128, 2048 + j * 512, 2048 + (j + 1) * 512),
                    c.ws[:, (ic * 8 + kb) * 128:(ic * 8 + kb) * 128 + 128],
                    c.xl[1][:, j * 512:(j + 1) * 512],
                    start=True, stop=True),
                  reads=[c.wsB, c.xlB[1]],
                  writes=c.pb(2048 + j * 512, 2048 + (j + 1) * 512),
                  tag="s7_pbc")
            if kb == 0:
                for j in range(4):
                    E("dve", lambda e, j=j, koff=koff: e.tensor_tensor(
                        c.sf[:, j * 512:(j + 1) * 512],
                        c.sb[:, koff + j * 512:koff + (j + 1) * 512],
                        c.pr(0, 128, 2048 + j * 512, 2048 + (j + 1) * 512),
                        ALU.mult),
                      reads=[c.sbB] + c.pb(2048 + j * 512,
                                           2048 + (j + 1) * 512),
                      writes=[c.sfB], tag="s7_cx0")
            else:
                for j in range(4):
                    E("dve", lambda e, j=j, koff=koff: e.tensor_tensor(
                        c.sb[:, 4096 + (j % 3) * 512:
                             4096 + (j % 3) * 512 + 512],
                        c.sb[:, koff + j * 512:koff + (j + 1) * 512],
                        c.pr(0, 128, 2048 + j * 512, 2048 + (j + 1) * 512),
                        ALU.mult),
                      reads=[c.sbB] + c.pb(2048 + j * 512,
                                           2048 + (j + 1) * 512),
                      writes=[c.sbB], tag="s7_cxm")
                    E("pool", lambda e, j=j: e.tensor_tensor(
                        c.sf[:, j * 512:(j + 1) * 512],
                        c.sf[:, j * 512:(j + 1) * 512],
                        c.sb[:, 4096 + (j % 3) * 512:
                             4096 + (j % 3) * 512 + 512], ALU.add),
                      reads=[c.sbB, c.sfB], writes=[c.sfB], tag="s7_cxa")
        for j in range(4):
            E("pe", lambda e, ic=ic, j=j: e.matmul(
                c.pr(0, 128, 2048 + j * 512, 2048 + (j + 1) * 512),
                c.ctb[0:16, 528 + ic * 128:528 + (ic + 1) * 128],
                c.xl[2][0:16, j * 512:(j + 1) * 512],
                start=True, stop=True),
              reads=[c.ctbB, c.xlB[2]],
              writes=c.pb(2048 + j * 512, 2048 + (j + 1) * 512),
              tag="s7_rbc")
            E("dve", lambda e, ic=ic, j=j: e.tensor_tensor(
                c.ua[ic][:, j * 512:(j + 1) * 512],
                c.sf[:, j * 512:(j + 1) * 512],
                c.pr(0, 128, 2048 + j * 512, 2048 + (j + 1) * 512),
                ALU.mult),
              reads=[c.sfB] + c.pb(2048 + j * 512, 2048 + (j + 1) * 512),
              writes=[c.uaB[ic]], tag="s7_nrm")
    ua_l = [c.ua[i][:] for i in range(NCH)]

    def consume_o(oc, tq0, width, c0, guards):
        E("dve", lambda e: e.tensor_tensor(
            c.xc[oc][:, tq0:tq0 + width], c.pr(0, 128, c0, c0 + width),
            c.xc[oc][:, tq0:tq0 + width], ALU.add),
          reads=guards + [c.xcB[oc]], writes=[c.xcB[oc]], tag="s7_res")

    emit_proj(c, W["wout"], ua_l, c.uaB, NCH, consume_o, "s7o")


def prep_l1_attn_b(inputs):
    f32 = np.float32
    esel8 = np.zeros((128, 16), f32)
    eselT = np.zeros((8, 16, 128), f32)
    sel2 = np.zeros((128, 64 * 128), f32)
    sel2T = np.zeros((128, 64 * 128), f32)
    for ic in range(8):
        for p in range(128):
            h = 2 * ic + p // 64
            for kb in range(8):
                sel2[p, (ic * 8 + kb) * 128 + h * 8 + kb] = 1.0
                sel2T[h * 8 + kb, (ic * 8 + kb) * 128 + p] = 1.0
            eselT[ic, h % 16, p] = 0.0
    for pp in range(128):
        esel8[pp, pp // 8] = 1.0
    for ic in range(8):
        for p in range(128):
            h = 2 * ic + p // 64
            eselT[ic, h, p] = 1.0
    gb = np.stack([inputs["L1_attn_b_ln_scale"], inputs["L1_attn_b_ln_bias"]])
    masks = []
    for core in range(8):
        m = np.zeros((128, 8), f32)
        for p in range(128):
            if p % 8 > core:
                m[p, 0] = -1e10
        masks.append(np.ascontiguousarray(m))
    return {
        "qkvk": np.ascontiguousarray(
            inputs["L1_attn_b_qkv_kernel"].transpose(1, 0, 2)
            .reshape(3072, 1024), f32),
        "wout": np.ascontiguousarray(inputs["L1_attn_b_out_kernel"], f32),
        "gb_pp": np.ascontiguousarray(
            gb.reshape(2, 8, 128).transpose(2, 0, 1).reshape(128, 16), f32),
        "esel8": esel8.astype(BF),
        "eselT": np.ascontiguousarray(
            eselT.transpose(1, 0, 2).reshape(16, 1024)).astype(BF),
        "sel2": sel2.astype(BF),
        "sel2T": sel2T.astype(BF),
        "mask_pp": masks,
    }


def run_launch(subs, x_fm, inputs, extra=None):
    c = C("L" + "_".join(s for s in subs))
    x_in = c.din("x_in", [np.ascontiguousarray(x_fm[i]) for i in range(B)],
                 percore=True)
    x_out = c.dout("x_out", (1024, 2048), F32)
    load_x(c, x_in)

    def din_w(pfx, W):
        out = {}
        for k, v in W.items():
            dt = BF16 if v.dtype == BF else F32
            out[k] = c.din(f"{pfx}_{k}", v, dt=dt)
        return out

    for s in subs:
        if s in ("s6", "s8"):
            emit_l1_ffn(c, din_w(s, prep_l1_ffn(inputs)),
                        it=0 if s == "s6" else 1)
        elif s == "s3":
            emit_l0_attn_b(c, din_w(s, prep_l0_attn_b(inputs)))
        elif s == "s1":
            emit_l0_attn_a(c, din_w(s, prep_l0_attn_a(inputs)))
        elif s == "s5":
            emit_l1_attn_a(c, din_w(s, prep_l1_attn_a(inputs)))
        elif s == "s7a":
            Wh = prep_l1_attn_b(inputs)
            Wh.pop("mask_pp")
            W = din_w(s, {"qkvk": Wh["qkvk"], "gb_pp": Wh["gb_pp"]})
            qkv_out = c.dout("qkv_out", (3 * 1024, 2048), BF16)
            qkvB = c.kb.buf("qkv_out")
            emit_s7_qkv(c, W, qkv_out, qkvB)
        elif s == "s7b":
            Wh = prep_l1_attn_b(inputs)
            masks = Wh.pop("mask_pp")
            W = din_w(s, {k: Wh[k] for k in
                          ("wout", "esel8", "eselT", "sel2", "sel2T")})
            W["mask_pp"] = c.din("s7_mask_pp", masks, percore=True)
            q_in = c.din("s7_q", extra["q"], dt=BF16, percore=True)
            kv_all = c.din("s7_kv", extra["kv"], dt=BF16)
            emit_s7_attn(c, W, q_in, kv_all, c.kb.buf("kv_all"))
        elif s in ("s2", "s4"):
            W = din_w(s, prep_l0_ffn(inputs))
            W["x_dram"] = x_in
            W["spill"] = False
            emit_l0_ffn(c, W, it=0 if s == "s2" else 1)
        else:
            raise NotImplementedError(s)
    store_x(c, x_out)
    stats = c.finish()
    res = c.run()
    out = np.stack([res.results[i]["x_out"] for i in range(B)])
    ex = {}
    if "s7a" in subs:
        ex["qkv"] = [np.asarray(res.results[i]["qkv_out"]) for i in range(B)]
    return out, stats, ex


LAUNCH_PLAN = [["s1"], ["s2"], ["s3"], ["s4"], ["s5"], ["s6"],
               ["s7a"], ["s7b"], ["s8"]]
FUSED2 = True


def run_fused_A(x_fm, inputs):
    c = C("fA")
    x_in = c.din("x_in", [np.ascontiguousarray(x_fm[i]) for i in range(B)],
                 percore=True)
    x_out = c.dout("x_out", (1024, 2048), F32)
    qkv_out = c.dout("qkv_out", (3 * 1024, 2048), BF16)
    qkvB = c.kb.buf("qkv_out")
    xspill = c.dint("xspill", (1024, 2048), F32)

    def din_w(pfx, W):
        out = {}
        for k, v in W.items():
            dt = BF16 if v.dtype == BF else F32
            out[k] = c.din(f"{pfx}_{k}", v, dt=dt)
        return out

    load_x(c, x_in)
    emit_l0_attn_a(c, din_w("s1", prep_l0_attn_a(inputs)))
    Wf = din_w("s2", prep_l0_ffn(inputs))
    Wf["x_dram"] = xspill
    Wf["spill"] = True
    emit_l0_ffn(c, Wf, it=0)
    emit_l0_attn_b(c, din_w("s3", prep_l0_attn_b(inputs)))
    emit_l0_ffn(c, Wf, it=1)
    emit_l1_attn_a(c, din_w("s5", prep_l1_attn_a(inputs)))
    W6 = din_w("s6", prep_l1_ffn(inputs))
    emit_l1_ffn(c, W6, it=0)
    W7 = prep_l1_attn_b(inputs)
    W7.pop("mask_pp")
    W7a = din_w("s7a", {"qkvk": W7["qkvk"], "gb_pp": W7["gb_pp"]})
    emit_s7_qkv(c, W7a, qkv_out, qkvB)
    store_x(c, x_out)
    stats = c.finish()
    res = c.run()
    out = np.stack([res.results[i]["x_out"] for i in range(B)])
    qkv = [np.asarray(res.results[i]["qkv_out"]) for i in range(B)]
    return out, stats, qkv


def run_fused_B(x_fm, inputs, extra):
    c = C("fB")
    x_in = c.din("x_in", [np.ascontiguousarray(x_fm[i]) for i in range(B)],
                 percore=True)
    x_out = c.dout("x_out", (1024, 2048), F32)

    def din_w(pfx, W):
        out = {}
        for k, v in W.items():
            dt = BF16 if v.dtype == BF else F32
            out[k] = c.din(f"{pfx}_{k}", v, dt=dt)
        return out

    load_x(c, x_in)
    Wh = prep_l1_attn_b(inputs)
    masks = Wh.pop("mask_pp")
    W = din_w("s7b", {k: Wh[k] for k in
                      ("wout", "esel8", "eselT", "sel2", "sel2T")})
    W["mask_pp"] = c.din("s7_mask_pp", masks, percore=True)
    q_in = c.din("s7_q", extra["q"], dt=BF16, percore=True)
    kv_all = c.din("s7_kv", extra["kv"], dt=BF16)
    emit_s7_attn(c, W, q_in, kv_all, c.kb.buf("kv_all"))
    emit_l1_ffn(c, din_w("s8", prep_l1_ffn(inputs)), it=1)
    store_x(c, x_out)
    stats = c.finish()
    res = c.run()
    return np.stack([res.results[i]["x_out"] for i in range(B)]), stats


def kernel(**inputs):
    x = inputs["L0_attn_a_x"]
    x_fm = np.ascontiguousarray(x.transpose(0, 2, 1)).astype(np.float32)
    if FUSED2:
        x_fm, _, qkv = run_fused_A(x_fm, inputs)
        extra = {
            "q": [np.ascontiguousarray(qkv[b][0:1024]) for b in range(B)],
            "kv": np.ascontiguousarray(
                np.concatenate([qkv[b][1024:3072] for b in range(B)],
                               axis=0)),
        }
        x_fm, _ = run_fused_B(x_fm, inputs, extra)
        return np.ascontiguousarray(x_fm.transpose(0, 2, 1))
    extra = None
    for subs in LAUNCH_PLAN:
        x_fm, _, ex = run_launch(subs, x_fm, inputs, extra=extra)
        if "qkv" in ex:
            qkv = ex["qkv"]
            extra = {
                "q": [np.ascontiguousarray(qkv[b][0:1024]) for b in range(B)],
                "kv": np.ascontiguousarray(
                    np.concatenate([qkv[b][1024:3072] for b in range(B)],
                                   axis=0)),
            }
    return np.ascontiguousarray(x_fm.transpose(0, 2, 1))


# version 06
# speedup vs baseline: 1.8416x; 1.8338x over previous
import numpy as np
import ml_dtypes
import concourse.bass as bass
import concourse.mybir as mybir
from concourse.bass_utils import run_bass_kernel_spmd

F32 = mybir.dt.float32
BF16 = mybir.dt.bfloat16
AF = mybir.ActivationFunctionType
ALU = mybir.AluOpType
AX = mybir.AxisListType
BF = ml_dtypes.bfloat16

ENGINES = ("pe", "act", "dve", "pool", "sync")
SAME_ENGINE_COMPUTE_EDGE = True


class Op:
    __slots__ = ("eng", "kind", "fn", "deps", "seq", "needs_inc", "sem_val",
                 "tag")

    def __init__(self, eng, kind, fn, tag):
        self.eng = eng
        self.kind = kind
        self.fn = fn
        self.deps = []
        self.seq = -1
        self.needs_inc = False
        self.sem_val = None
        self.tag = tag


class Buf:
    __slots__ = ("name", "last_write", "readers")

    def __init__(self, name):
        self.name = name
        self.last_write = None
        self.readers = []


class KB:
    def __init__(self, nc):
        self.nc = nc
        self.ops = {e: [] for e in ENGINES}
        self.final_waits = []

    def buf(self, name):
        return Buf(name)

    def _keep_dep(self, dep, op):
        if dep.kind == "l" or op.kind == "l":
            return True
        if dep.eng != op.eng:
            return True
        if dep.kind == "d" and op.kind == "d":
            return False
        if dep.kind == "c" and op.kind == "c":
            if dep.eng == "pe":
                return False
            return SAME_ENGINE_COMPUTE_EDGE
        return True

    def emit(self, eng, fn, reads=(), writes=(), dma=False, tag="",
             extra_deps=(), coll=False, hard_deps=()):
        op = Op(eng, "l" if coll else ("d" if dma else "c"), fn, tag)
        deps = set()
        for b in reads:
            if b.last_write is not None:
                deps.add(b.last_write)
        for b in writes:
            if b.last_write is not None:
                deps.add(b.last_write)
            deps.update(b.readers)
        deps.update(extra_deps)
        deps.discard(op)
        op.deps = [d for d in deps if self._keep_dep(d, op)]
        for d in hard_deps:
            if d is not op and d not in op.deps:
                op.deps.append(d)
        for b in reads:
            if op not in b.readers:
                b.readers.append(op)
        for b in writes:
            b.last_write = op
            b.readers = []
        self.ops[eng].append(op)
        op.seq = len(self.ops[eng]) - 1
        return op

    def require_done(self, op):
        self.final_waits.append(op)

    def finalize_and_emit(self):
        nc = self.nc
        for e in ENGINES:
            for op in self.ops[e]:
                for d in op.deps:
                    d.needs_inc = True
        for op in self.final_waits:
            op.needs_inc = True
        LIMIT = 56000
        n_epochs = {}
        n_coll = 0
        for e in ENGINES:
            for k, step in (("c", 1), ("d", 16)):
                ep = 0
                v = 0
                for op in self.ops[e]:
                    if op.kind == "l" and op.needs_inc and op.sem_val is None:
                        op.sem_val = ("L", n_coll, 1)
                        n_coll += 1
                        continue
                    if op.kind != k or not op.needs_inc:
                        continue
                    if v + step > LIMIT:
                        ep += 1
                        v = 0
                    v += step
                    op.sem_val = (ep, v)
                n_epochs[(e, k)] = ep + 1 if v > 0 else 0
        sems = {}
        for (e, k), n in n_epochs.items():
            for ep in range(n):
                sems[(e, k, ep)] = nc.alloc_semaphore(f"s_{e}_{k}{ep}")
        for i in range(n_coll):
            sems[("coll", i)] = nc.alloc_semaphore(f"s_coll{i}")
        highwater = {}

        def wait_dep(eng_name, engobj, dep):
            if dep.sem_val[0] == "L":
                key = ("coll", dep.sem_val[1])
                if not highwater.get((eng_name, key)):
                    engobj.wait_ge(sems[key], 1)
                    highwater[(eng_name, key)] = 1
                return
            ep, val = dep.sem_val
            key = (dep.eng, dep.kind, ep)
            hw = highwater.get((eng_name, key), 0)
            if val > hw:
                engobj.wait_ge(sems[key], val)
                highwater[(eng_name, key)] = val

        def emit_stream(eng_name, engobj):
            for op in self.ops[eng_name]:
                for dep in sorted(op.deps, key=lambda d: (d.eng, d.kind,
                                                          d.sem_val)):
                    wait_dep(eng_name, engobj, dep)
                inst = op.fn(engobj)
                if op.needs_inc:
                    if op.sem_val[0] == "L":
                        inst.then_inc(sems[("coll", op.sem_val[1])])
                    else:
                        ep, _ = op.sem_val
                        inst.then_inc(sems[(op.eng, op.kind, ep)],
                                      16 if op.kind == "d" else 1)
            if eng_name == "pool":
                for op in self.final_waits:
                    wait_dep(eng_name, engobj, op)

        with nc.Block() as blk:

            @blk.tensor
            def _(tensor):
                emit_stream("pe", tensor)

            @blk.scalar
            def _(scalar):
                emit_stream("act", scalar)

            @blk.vector
            def _(vector):
                emit_stream("dve", vector)

            @blk.gpsimd
            def _(gpsimd):
                emit_stream("pool", gpsimd)

            @blk.sync
            def _(sync):
                emit_stream("sync", sync)

        return {e: len(self.ops[e]) for e in ENGINES}


B, T, D = 8, 2048, 1024
NCH = 8
HEADS, DH = 16, 64
E_EXP, HFF = 8, 4096
LN_EPS = 1e-5


class C:
    def __init__(self, tag):
        self.nc = bass.Bass("TRN2", target_bir_lowering=False)
        self.kb = KB(self.nc)
        self.E = self.kb.emit
        self.inp = {}
        self.tag = tag
        nc = self.nc
        self.xc = [nc.alloc_sbuf_tensor(f"xc{i}", [128, 2048], F32)
                   for i in range(NCH)]
        self.xl = [nc.alloc_sbuf_tensor(f"xl{i}", [128, 2048], BF16)
                   for i in range(NCH)]
        self.sa = nc.alloc_sbuf_tensor("sa", [128, 16896], BF16)
        self.sb = nc.alloc_sbuf_tensor("sb", [128, 5632], BF16)
        self.ua = [nc.alloc_sbuf_tensor(f"ua{i}", [128, 2048], BF16)
                   for i in range(NCH)]
        self.sf = nc.alloc_sbuf_tensor("sf", [128, 2048], F32)
        self.ws = nc.alloc_sbuf_tensor("ws", [128, 8192], BF16)
        self.ct = nc.alloc_sbuf_tensor("ct", [128, 896], F32)
        self.ctb = nc.alloc_sbuf_tensor("ctb", [128, 2048], BF16)
        self.ps = nc.alloc_psum_tensor("ps", [128, 4096], F32)
        kb = self.kb
        self.xcB = [kb.buf(f"xc{i}") for i in range(NCH)]
        self.xlB = [kb.buf(f"xl{i}") for i in range(NCH)]
        self.saB = kb.buf("sa")
        self.sbB = kb.buf("sb")
        self.uaB = [kb.buf(f"ua{i}") for i in range(NCH)]
        self.sfB = kb.buf("sf")
        self.wsB = kb.buf("ws")
        self.wsHB = [kb.buf("wsh0"), kb.buf("wsh1")]
        self.ctB_ = kb.buf("ct")
        self.ctbB = kb.buf("ctb")
        self.psB = [kb.buf(f"psb{i}") for i in range(8)]
        self._const_cols = {}
        self._n_const = 0
        self._extra_sb = {}
        self._init_consts()

    def _init_consts(self):
        self.E("pool", lambda e: e.memset(self.ct[:, 64:65], 1.0),
               writes=[self.ctB_], tag="ones_col")
        self.E("pool", lambda e: e.memset(self.ct[0:1, 66:194], 1.0),
               writes=[self.ctB_], tag="ones_row")
        self.E("pool", lambda e: e.memset(self.ct[32:33, 66:194], 1.0),
               writes=[self.ctB_], tag="ones_row32")
        self.E("pool", lambda e: e.memset(self.ct[64:65, 66:194], 1.0),
               writes=[self.ctB_], tag="ones_row64")
        self.E("pool", lambda e: e.memset(self.ctb[0:1, 0:128], 1.0),
               writes=[self.ctbB], tag="ones_row16")
        self.E("pool", lambda e: e.memset(self.ctb[:, 128:129], 1.0),
               writes=[self.ctbB], tag="ones_col16")
        self.ones_col = self.ct[:, 64:65]
        self.ones_row = self.ct[0:1, 66:194]
        self.ones_row32 = self.ct[32:33, 66:194]
        self.ones_row64 = self.ct[64:65, 66:194]
        self.ones_row16 = self.ctb[0:1, 0:128]
        self.ones_col16 = self.ctb[:, 128:129]

    def alloc(self, name, shape, dt):
        if name in self._extra_sb:
            return self._extra_sb[name]
        t = self.nc.alloc_sbuf_tensor(f"{name}_{self.tag}", list(shape), dt)
        b = self.kb.buf(name)
        self._extra_sb[name] = (t, b)
        return t, b

    def pr(self, p0, p1, c0, c1):
        return self.ps[p0:p1, c0:c1]

    def pb(self, c0, c1):
        return [self.psB[i] for i in range(c0 // 512, (c1 + 511) // 512)]

    def din(self, name, arr, dt=F32, percore=False):
        self.inp[name] = arr
        shape = list(arr[0].shape if percore else arr.shape)
        return self.nc.dram_tensor(name, shape, dt, kind="ExternalInput").ap()

    def dout(self, name, shape, dt=F32):
        return self.nc.dram_tensor(name, list(shape), dt,
                                   kind="ExternalOutput").ap()

    def dint(self, name, shape, dt):
        return self.nc.dram_tensor(name, list(shape), dt, kind="Internal").ap()

    def const(self, val):
        if val not in self._const_cols:
            j = self._n_const
            self._n_const += 1
            assert j < 64
            self.kb.emit("pool",
                         lambda e, j=j, v=val: e.memset(self.ct[:, j:j + 1], v),
                         writes=[self.ctB_], tag=f"const{val}")
            self._const_cols[val] = j
        j = self._const_cols[val]
        return self.ct[:, j:j + 1]

    def finish(self):
        for e in ("sync", "pool", "act", "dve"):
            for op in self.kb.ops[e]:
                if op.kind == "d":
                    self.kb.require_done(op)
        return self.kb.finalize_and_emit()

    def run(self, trace=False):
        in_maps = []
        for core in range(B):
            m = {}
            for k, v in self.inp.items():
                m[k] = v[core] if isinstance(v, list) else v
            in_maps.append(m)
        return run_bass_kernel_spmd(self.nc, in_maps, core_ids=list(range(B)),
                                    trace=trace)


def load_x(c, x_dram):
    for i in range(NCH):
        c.E("sync", lambda e, i=i: e.dma_start(
            c.xc[i][:], x_dram[i * 128:(i + 1) * 128, :]),
            writes=[c.xcB[i]], dma=True, tag=f"ldx{i}")


def store_x(c, y_dram):
    for i in range(NCH):
        c.E("sync", lambda e, i=i: e.dma_start(
            y_dram[i * 128:(i + 1) * 128, :], c.xc[i][:]),
            reads=[c.xcB[i]], dma=True, tag=f"stx{i}")


def emit_ln(c, src, srcB, out, outB, eps=LN_EPS, src16=False, gb=None):
    E = c.E
    gA = c.pb(0, 2048)
    gB = c.pb(2048, 4096)
    ocol = c.ones_col16 if src16 else c.ones_col
    ocolB = c.ctbB if src16 else c.ctB_
    for i in range(NCH):
        E("act", lambda e, i=i: e.square(c.sf[:], src[i][:]),
          reads=[srcB[i]], writes=[c.sfB], tag="ln_sq")
        for j in range(4):
            E("pe", lambda e, i=i, j=j: e.matmul(
                c.pr(0, 1, j * 512, (j + 1) * 512), ocol,
                src[i][:, j * 512:(j + 1) * 512],
                start=(i == 0), stop=(i == NCH - 1)),
              reads=[srcB[i], ocolB], writes=gA, tag="ln_s")
            E("pe", lambda e, i=i, j=j: e.matmul(
                c.pr(32, 33, j * 512, (j + 1) * 512), c.ones_col,
                c.sf[:, j * 512:(j + 1) * 512],
                start=(i == 0), stop=(i == NCH - 1)),
              reads=[c.sfB, c.ctB_], writes=gA, tag="ln_s2")
    r_mean = c.sf[0:1, :]
    r_a = c.ws[0:1, 0:4096].bitcast(F32)
    r_b = c.ws[0:1, 4096:8192].bitcast(F32)
    E("dve", lambda e: e.tensor_scalar(r_mean, c.pr(0, 1, 0, 2048),
                                       1.0 / D, None, ALU.mult),
      reads=gA, writes=[c.sfB], tag="ln_mean")
    E("dve", lambda e: e.tensor_scalar(r_a, c.pr(32, 33, 0, 2048),
                                       1.0 / D, None, ALU.mult),
      reads=gA, writes=[c.wsB], tag="ln_ms")
    E("dve", lambda e: e.scalar_tensor_tensor(
        r_b, r_mean, -1.0, r_mean, ALU.mult, ALU.mult),
      reads=[c.sfB], writes=[c.wsB], tag="ln_nm2")
    E("dve", lambda e: e.tensor_tensor(r_a, r_a, r_b, ALU.add),
      reads=[c.wsB], writes=[c.wsB], tag="ln_var")
    eps_col = c.const(eps)
    E("act", lambda e: e.activation(r_b, r_a, AF.Ln,
                                    bias=eps_col[0:1, :]),
      reads=[c.wsB, c.ctB_], writes=[c.wsB], tag="ln_ln")
    E("act", lambda e: e.activation(r_a, r_b, AF.Exp, scale=-0.5),
      reads=[c.wsB], writes=[c.wsB], tag="ln_rstd")
    E("dve", lambda e: e.scalar_tensor_tensor(
        r_b, r_mean, -1.0, r_a, ALU.mult, ALU.mult),
      reads=[c.sfB, c.wsB], writes=[c.wsB], tag="ln_b")
    for j in range(4):
        E("pe", lambda e, j=j: e.matmul(
            c.pr(0, 128, j * 512, (j + 1) * 512), c.ones_row,
            c.ws[0:1, j * 1024:j * 1024 + 1024].bitcast(F32),
            start=True, stop=True),
          reads=[c.wsB, c.ctB_], writes=gA, tag="ln_bcA")
        E("pe", lambda e, j=j: e.matmul(
            c.pr(0, 128, 2048 + j * 512, 2048 + (j + 1) * 512), c.ones_row,
            c.ws[0:1, 4096 + j * 1024:4096 + j * 1024 + 1024].bitcast(F32),
            start=True, stop=True),
          reads=[c.wsB, c.ctB_], writes=gB, tag="ln_bcB")
    for i in range(NCH):
        E("dve", lambda e, i=i: e.tensor_tensor(
            c.sf[:], src[i][:], c.pr(0, 128, 0, 2048), ALU.mult),
          reads=[srcB[i]] + gA, writes=[c.sfB], tag="ln_n1")
        if gb is None:
            E("dve", lambda e, i=i: e.tensor_tensor(
                out[i][:], c.sf[:], c.pr(0, 128, 2048, 4096), ALU.add),
              reads=[c.sfB] + gB, writes=[outB[i]], tag="ln_n2")
        else:
            gt, gtB, go, bo_ = gb
            E("dve", lambda e, i=i: e.tensor_tensor(
                c.sf[:], c.sf[:], c.pr(0, 128, 2048, 4096), ALU.add),
              reads=[c.sfB] + gB, writes=[c.sfB], tag="ln_n2")
            E("act", lambda e, i=i, gt=gt, go=go: e.activation(
                c.sf[:], c.sf[:], AF.Copy, scale=gt[:, go + i:go + i + 1]),
              reads=[c.sfB, gtB], writes=[c.sfB], tag="ln_g")
            E("dve", lambda e, i=i, gt=gt, bo_=bo_: e.tensor_scalar(
                out[i][:], c.sf[:], gt[:, bo_ + i:bo_ + i + 1], None,
                ALU.add),
              reads=[c.sfB, gtB], writes=[outB[i]], tag="ln_b2")


def stage_cast(c, dram_ap, r0, rows, c0, cols, dest, destB, doff, tag,
               sf_off=0):
    E = c.E
    E("sync", lambda e: e.dma_start(
        c.sf[0:rows, sf_off:sf_off + cols],
        dram_ap[r0:r0 + rows, c0:c0 + cols]),
      writes=[c.sfB], dma=True, tag=f"{tag}_ld")
    E("pool", lambda e: e.tensor_copy(dest[0:rows, doff:doff + cols],
                                      c.sf[0:rows, sf_off:sf_off + cols]),
      reads=[c.sfB], writes=[destB], tag=f"{tag}_cast")


def emit_proj(c, wT_dram, src, srcB, nout, consume, tag, nin=NCH,
              wq_width=512):
    E = c.E
    n_waves = (nout + 3) // 4
    for w in range(n_waves):
        ocs = list(range(w * 4, min(w * 4 + 4, nout)))
        wwidth = len(ocs) * 128
        half = (w % 2) * 4096
        for ic in range(nin):
            stage_cast(c, wT_dram, ic * 128, 128, ocs[0] * 128, wwidth,
                       c.ws, c.wsB, half + ic * 512, f"{tag}w{w}i{ic}")
        for tq in range(4):
            pbase = 0 if tq % 2 == 0 else 2048
            guards = c.pb(pbase, pbase + wwidth)
            for ic in range(nin):
                for k, oc in enumerate(ocs):
                    E("pe", lambda e, ic=ic, k=k, tq=tq, half=half,
                        pbase=pbase: e.matmul(
                          c.pr(0, 128, pbase + k * 512, pbase + k * 512 + 512),
                          c.ws[:, half + ic * 512 + k * 128:
                               half + ic * 512 + k * 128 + 128],
                          src[ic][:, tq * 512:(tq + 1) * 512],
                          start=(ic == 0), stop=(ic == nin - 1)),
                      reads=[c.wsB, srcB[ic]], writes=guards,
                      tag=f"{tag}_mm")
            for k, oc in enumerate(ocs):
                consume(oc, tq * 512, 512, pbase + k * 512,
                        c.pb(pbase + k * 512, pbase + k * 512 + 512))


def emit_l1_ffn(c, W, it=0):
    E = c.E
    tg = f"f{it}"
    emit_ln(c, c.xc, c.xcB, c.xl, c.xlB)
    mu, muB = c.alloc("mu6", [128, 16], F32)
    om, omB = c.alloc("om6", [128, 16], F32)
    E("sync", lambda e: e.dma_start(mu[:], W["mu"]), writes=[muB], dma=True)
    E("dve", lambda e: e.tensor_scalar(om[:], mu[:], -1.0, 1.0, ALU.mult,
                                       ALU.add),
      reads=[muB], writes=[omB], tag="om")
    for i in range(NCH):
        E("dve", lambda e, i=i: e.tensor_scalar(
            c.ua[i][:], c.xl[i][:], mu[:, i:i + 1], None, ALU.mult),
          reads=[c.xlB[i], muB], writes=[c.uaB[i]], tag="xr1")
        E("dve", lambda e, i=i: e.scalar_tensor_tensor(
            c.ua[i][:, 1:2048], c.xl[i][:, 0:2047], om[:, i:i + 1],
            c.ua[i][:, 1:2048], ALU.mult, ALU.add),
          reads=[c.xlB[i], omB, c.uaB[i]], writes=[c.uaB[i]], tag="xr2")
        E("dve", lambda e, i=i: e.tensor_scalar(
            c.sa[:, i * 2048:(i + 1) * 2048], c.xl[i][:],
            mu[:, 8 + i:9 + i], None, ALU.mult),
          reads=[c.xlB[i], muB, c.saB], writes=[c.saB], tag="xk1")
        E("dve", lambda e, i=i: e.scalar_tensor_tensor(
            c.sa[:, i * 2048 + 1:(i + 1) * 2048],
            c.xl[i][:, 0:2047], om[:, 8 + i:9 + i],
            c.sa[:, i * 2048 + 1:(i + 1) * 2048], ALU.mult, ALU.add),
          reads=[c.xlB[i], omB, c.saB], writes=[c.saB], tag="xk2")
    sa_chunks = [c.sa[:, i * 2048:(i + 1) * 2048] for i in range(NCH)]
    saBl = [c.saB] * NCH

    def consume_k(oc, tq0, width, c0, guards):
        E("act", lambda e: e.activation(c.xl[oc][:, tq0:tq0 + width],
                                        c.pr(0, 128, c0, c0 + width), AF.Relu),
          reads=guards, writes=[c.xlB[oc]], tag="k_relu")

    emit_proj(c, W["wkT"], sa_chunks, saBl, NCH, consume_k, f"{tg}k")
    for i in range(NCH):
        E("pool", lambda e, i=i: e.tensor_tensor(c.xl[i][:], c.xl[i][:],
                                                 c.xl[i][:], ALU.mult),
          reads=[c.xlB[i]], writes=[c.xlB[i]], tag="k_sq")
    ua_l = [c.ua[i][:] for i in range(NCH)]

    def consume_r(oc, tq0, width, c0, guards):
        E("act", lambda e: e.activation(
            c.sa[:, oc * 2048 + tq0:oc * 2048 + tq0 + width],
            c.pr(0, 128, c0, c0 + width), AF.Sigmoid),
          reads=guards, writes=[c.saB], tag="r_sig")

    emit_proj(c, W["wrT"], ua_l, c.uaB, NCH, consume_r, f"{tg}r")
    xl_l = [c.xl[i][:] for i in range(NCH)]

    def consume_v(oc, tq0, width, c0, guards):
        E("dve", lambda e: e.tensor_tensor(
            c.sb[:, 0:width], c.pr(0, 128, c0, c0 + width),
            c.sa[:, oc * 2048 + tq0:oc * 2048 + tq0 + width], ALU.mult),
          reads=guards + [c.saB], writes=[c.sbB], tag="v_gate")
        E("dve", lambda e: e.tensor_tensor(
            c.xc[oc][:, tq0:tq0 + width], c.xc[oc][:, tq0:tq0 + width],
            c.sb[:, 0:width], ALU.add),
          reads=[c.sbB, c.xcB[oc]], writes=[c.xcB[oc]], tag="v_res")

    emit_proj(c, W["wvT"], xl_l, c.xlB, NCH, consume_v, f"{tg}v")


def prep_l1_ffn(inputs):
    f32 = np.float32
    mu = np.stack([inputs["L1_ffn_mu_r"], inputs["L1_ffn_mu_k"]])
    mu_pp = np.ascontiguousarray(
        mu.reshape(2, 8, 128).transpose(2, 0, 1).reshape(128, 16), f32)
    return {
        "wrT": np.ascontiguousarray(inputs["L1_ffn_W_r"].T, f32),
        "wkT": np.ascontiguousarray(inputs["L1_ffn_W_k"].T, f32),
        "wvT": np.ascontiguousarray(inputs["L1_ffn_W_v"].T, f32),
        "mu": mu_pp,
    }


def emit_l0_attn_b(c, W):
    E = c.E
    emit_ln(c, c.xc, c.xcB, c.xl, c.xlB)
    E("sync", lambda e: e.dma_start(c.ctb[:, 512:640], W["esel"]),
      writes=[c.ctbB], dma=True, tag="esel")
    E("sync", lambda e: e.dma_start(c.ctb[0:16, 640:1664], W["eselT"]),
      writes=[c.ctbB], dma=True, tag="eselT")
    bpr, bprB = c.alloc("bpr", [128, 8], F32)
    E("sync", lambda e: e.dma_start(bpr[:], W["bproj_pp"]), writes=[bprB],
      dma=True)
    STG = 2048
    TMP = 2560
    TMP2 = 3072
    KP = 3584
    def proj_chunk(oc, tq, pcol):
        for ic in range(NCH):
            off = (ic % 4) * 512
            E("pe", lambda e, ic=ic, oc=oc, tq=tq, pcol=pcol: e.matmul(
                c.pr(0, 128, pcol, pcol + 512),
                c.ws[:, (oc % 8) * 1024 + ic * 128:(oc % 8) * 1024 + ic * 128 + 128],
                c.xl[ic][:, tq * 512:(tq + 1) * 512],
                start=(ic == 0), stop=(ic == NCH - 1)),
              reads=[c.wsB, c.xlB[ic]], writes=c.pb(pcol, pcol + 512),
              tag="s3_mm")

    def stage_w(oc):
        for ic in range(NCH):
            stage_cast(c, W["wqkv"], ic * 128, 128, oc * 128, 128,
                       c.ws, c.wsB, (oc % 8) * 1024 + ic * 128,
                       f"s3w{oc}_{ic}", sf_off=(ic % 4) * 512)

    guards_vs = c.pb(2048, 4096)
    for j in range(NCH):
        oc = 16 + j
        stage_w(oc)
        for tq in range(4):
            pcol = (tq % 2) * 512
            proj_chunk(oc, tq, pcol)
            gv = c.pb(pcol, pcol + 512)
            E("act", lambda e, tq=tq, pcol=pcol: e.copy(
                c.sb[:, STG:STG + 512], c.pr(0, 128, pcol, pcol + 512)),
              reads=gv, writes=[c.sbB], tag="s3_vev")
            E("pe", lambda e, j=j, tq=tq: e.matmul(
                c.pr(0, 16, 2048 + tq * 512, 2048 + (tq + 1) * 512),
                c.ctb[:, 512 + j * 16:512 + (j + 1) * 16],
                c.sb[:, STG:STG + 512],
                start=(j == 0), stop=(j == NCH - 1)),
              reads=[c.ctbB, c.sbB], writes=guards_vs, tag="s3_vsum")
    E("act", lambda e: e.copy(c.sb[0:16, 0:2048], c.pr(0, 16, 2048, 4096)),
      reads=guards_vs, writes=[c.sbB], tag="s3_vsum_ev")

    def elu1(pcol, dst_off):
        g = c.pb(pcol, pcol + 512)
        E("dve", lambda e: e.tensor_scalar(c.sb[:, TMP:TMP + 512],
                                           c.pr(0, 128, pcol, pcol + 512),
                                           0.0, None, ALU.min),
          reads=g, writes=[c.sbB], tag="s3_min")
        E("act", lambda e: e.activation(c.sb[:, TMP2:TMP2 + 512],
                                        c.sb[:, TMP:TMP + 512], AF.Exp),
          reads=[c.sbB], writes=[c.sbB], tag="s3_exp")
        E("act", lambda e: e.activation(c.sb[:, TMP:TMP + 512],
                                        c.pr(0, 128, pcol, pcol + 512),
                                        AF.Relu),
          reads=g + [c.sbB], writes=[c.sbB], tag="s3_relu")
        E("dve", lambda e: e.tensor_tensor(c.sb[:, dst_off:dst_off + 512],
                                           c.sb[:, TMP:TMP + 512],
                                           c.sb[:, TMP2:TMP2 + 512], ALU.add),
          reads=[c.sbB], writes=[c.sbB], tag="s3_add1")

    for j in range(NCH):
        oc = 8 + j
        stage_w(oc)
        for tq in range(4):
            pcol = (tq % 2) * 512
            proj_chunk(oc, tq, pcol)
            elu1(pcol, KP)
            E("pe", lambda e, j=j, tq=tq: e.matmul(
                c.pr(0, 128, 2560, 3072),
                c.ctb[0:16, 640 + j * 128:640 + (j + 1) * 128],
                c.sb[0:16, tq * 512:(tq + 1) * 512],
                start=True, stop=True),
              reads=[c.ctbB, c.sbB], writes=c.pb(2560, 3072), tag="s3_vbc")
            E("dve", lambda e, tq=tq: e.scalar_tensor_tensor(
                c.sb[:, TMP:TMP + 512], c.sb[:, KP:KP + 512], 1.0,
                c.pr(0, 128, 2560, 3072), ALU.mult, ALU.mult,
                accum_out=c.ct[:, 200 + tq:201 + tq]),
              reads=[c.sbB] + c.pb(2560, 3072), writes=[c.sbB, c.ctB_],
              tag="s3_kv")
        E("dve", lambda e, j=j: e.reduce_sum(
            c.ct[:, 240 + j:241 + j], c.ct[:, 200:204], axis=AX.X),
          reads=[c.ctB_], writes=[c.ctB_], tag="s3_kvsum")
    for j in range(NCH):
        oc = j
        stage_w(oc)
        for tq in range(4):
            pcol = (tq % 2) * 512
            proj_chunk(oc, tq, pcol)
            elu1(pcol, KP)
            E("pe", lambda e, j=j, tq=tq: e.matmul(
                c.pr(0, 16, 2048 + tq * 512, 2048 + (tq + 1) * 512),
                c.ctb[:, 512 + j * 16:512 + (j + 1) * 16],
                c.sb[:, KP:KP + 512],
                start=(j == 0), stop=(j == NCH - 1)),
              reads=[c.ctbB, c.sbB], writes=guards_vs, tag="s3_qsum")
    E("act", lambda e: e.copy(c.sb[0:16, 0:2048], c.pr(0, 16, 2048, 4096)),
      reads=guards_vs, writes=[c.sbB], tag="s3_qsum_ev")
    for j in range(NCH):
        for tq in range(4):
            E("pe", lambda e, j=j, tq=tq: e.matmul(
                c.pr(0, 128, 2560, 3072),
                c.ctb[0:16, 640 + j * 128:640 + (j + 1) * 128],
                c.sb[0:16, tq * 512:(tq + 1) * 512],
                start=True, stop=True),
              reads=[c.ctbB, c.sbB], writes=c.pb(2560, 3072), tag="s3_qbc")
            E("act", lambda e, j=j, tq=tq: e.activation(
                c.ua[j][:, tq * 512:(tq + 1) * 512],
                c.pr(0, 128, 2560, 3072), AF.Copy,
                scale=c.ct[:, 240 + j:241 + j]),
              reads=c.pb(2560, 3072) + [c.ctB_], writes=[c.uaB[j]],
              tag="s3_out")
    ua_l = [c.ua[i][:] for i in range(NCH)]

    def consume_o(oc, tq0, width, c0, guards):
        E("dve", lambda e: e.scalar_tensor_tensor(
            c.xc[oc][:, tq0:tq0 + width], c.pr(0, 128, c0, c0 + width),
            bpr[:, oc:oc + 1], c.xc[oc][:, tq0:tq0 + width],
            ALU.add, ALU.add),
          reads=guards + [bprB, c.xcB[oc]], writes=[c.xcB[oc]], tag="s3_res")

    emit_proj(c, W["wproj"], ua_l, c.uaB, NCH, consume_o, "s3o")


def prep_l0_attn_b(inputs):
    f32 = np.float32
    esel = np.zeros((8, 128, 16), f32)
    eselT = np.zeros((8, 16, 128), f32)
    for j in range(8):
        for p in range(128):
            h = 2 * j + (p // 64)
            esel[j, p, h] = 1.0
            eselT[j, h, p] = 1.0
    return {
        "wqkv": np.ascontiguousarray(inputs["L0_attn_b_W_qkv"], f32),
        "wproj": np.ascontiguousarray(inputs["L0_attn_b_W_proj"], f32),
        "bproj_pp": np.ascontiguousarray(
            inputs["L0_attn_b_b_proj"].reshape(8, 128).T, f32),
        "esel": np.ascontiguousarray(
            esel.transpose(1, 0, 2).reshape(128, 128)).astype(BF),
        "eselT": np.ascontiguousarray(
            eselT.transpose(1, 0, 2).reshape(16, 1024)).astype(BF),
    }


def emit_mha(c, W, tag, masked=False):
    E = c.E
    CSH = 12.0
    csh_col = c.const(-CSH)
    c60_col = c.const(-60.0)
    TRI = W.get("tri_off", 2048 - 128)
    KKD, KKC = 536, 552
    for ic in range(NCH):
        stage_cast(c, W["wv2"], ic * 128, 128, 0, 1024, c.ws, c.wsB,
                   ic * 1024, f"{tag}wv{ic}", sf_off=(ic % 2) * 1024)
    if W.get("bv_row") is not None:
        E("sync", lambda e: e.dma_start(c.sf[0:1, 0:1024], W["bv_row"]),
          writes=[c.sfB], dma=True, tag="bv_ld")
        E("pool", lambda e: e.tensor_copy(c.sb[0:1, 0:1024],
                                          c.sf[0:1, 0:1024]),
          reads=[c.sfB], writes=[c.sbB], tag="bv_cast")
    for kt in range(16):
        pc = (kt % 2) * 1024
        g = c.pb(pc, pc + 1024)
        for ic in range(NCH):
            for half in range(2):
                E("pe", lambda e, ic=ic, half=half, kt=kt, pc=pc: e.matmul(
                    c.pr(0, 128, pc + half * 512, pc + half * 512 + 512),
                    c.xl[ic][:, kt * 128:(kt + 1) * 128],
                    c.ws[:, ic * 1024 + half * 512:
                         ic * 1024 + half * 512 + 512],
                    start=(ic == 0), stop=(ic == NCH - 1 and
                                           W.get("bv_row") is None)),
                  reads=[c.xlB[ic], c.wsB], writes=g, tag=f"{tag}_vmm")
        if W.get("bv_row") is not None:
            for half in range(2):
                E("pe", lambda e, half=half, pc=pc: e.matmul(
                    c.pr(0, 128, pc + half * 512, pc + half * 512 + 512),
                    c.ones_row16,
                    c.sb[0:1, half * 512:half * 512 + 512],
                    start=False, stop=True),
                  reads=[c.sbB], writes=g, tag=f"{tag}_vbias")
        base = kt * 1056
        E("pool", lambda e, base=base: e.memset(
            c.sa[:, base:base + 1056].rearrange(
                "p (h x) -> p h x", x=66)[:, :, 64:66], 1.0),
          writes=[c.saB], tag=f"{tag}_vones")
        E("act", lambda e, base=base, pc=pc: e.activation(
            c.sa[:, base:base + 1056].rearrange(
                "p (h x) -> p h x", x=66)[:, :, 0:64],
            c.pr(0, 128, pc, pc + 1024).rearrange(
                "p (h x) -> p h x", x=64),
            AF.Copy),
          reads=g + [c.saB], writes=[c.saB], tag=f"{tag}_vev")
    QO, KO, PT = 0, 2048, 4096
    for gi in range(8):
        for ic in range(NCH):
            stage_cast(c, W["wq2"], ic * 128, 128, gi * 128, 128,
                       c.ws, c.wsB, ic * 128, f"{tag}q{gi}_{ic}",
                       sf_off=(ic % 4) * 512)
            if not W.get("k_in_ua"):
                stage_cast(c, W["wk2"], ic * 128, 128, gi * 128, 128,
                           c.ws, c.wsB, 1024 + ic * 128, f"{tag}k{gi}_{ic}",
                           sf_off=(ic % 4) * 512)
        for which, wsoff, dst, bias in (
                ("q", 0, QO, W.get("bq_pp")),
                ("k", 1024, KO, W.get("bk_pp"))):
            if which == "k" and W.get("k_in_ua"):
                continue
            for tq in range(4):
                pc = 2048 + (tq % 2) * 512
                g = c.pb(pc, pc + 512)
                for ic in range(NCH):
                    E("pe", lambda e, ic=ic, tq=tq, pc=pc, wsoff=wsoff:
                        e.matmul(
                          c.pr(0, 128, pc, pc + 512),
                          c.ws[:, wsoff + ic * 128:wsoff + ic * 128 + 128],
                          c.xl[ic][:, tq * 512:(tq + 1) * 512],
                          start=(ic == 0), stop=(ic == NCH - 1)),
                      reads=[c.wsB, c.xlB[ic]], writes=g,
                      tag=f"{tag}_{which}mm")
                if bias is not None:
                    E("dve", lambda e, tq=tq, pc=pc, dst=dst, gi=gi, b=bias:
                        e.tensor_scalar(
                          c.sb[:, dst + tq * 512:dst + (tq + 1) * 512],
                          c.pr(0, 128, pc, pc + 512),
                          b[0][:, gi:gi + 1], None, ALU.add),
                      reads=g + [bias[1]], writes=[c.sbB],
                      tag=f"{tag}_{which}b")
                else:
                    E("act", lambda e, tq=tq, pc=pc, dst=dst: e.copy(
                        c.sb[:, dst + tq * 512:dst + (tq + 1) * 512],
                        c.pr(0, 128, pc, pc + 512)),
                      reads=g, writes=[c.sbB], tag=f"{tag}_{which}ev")
        for hh in range(2):
            h = gi * 2 + hh
            kb = hh * 64
            for kt in range(16):
                q0 = kt * 128
                blocks = []
                nb = min((q0 // 512 + 1) * 512, 2048)
                blocks.append((q0, nb))
                while nb < 2048:
                    blocks.append((nb, min(nb + 512, 2048)))
                    nb += 512
                for bi, (qa, qb) in enumerate(blocks):
                    pc = (kt % 2) * 512
                    g = c.pb(pc, pc + 512)
                    if W.get("k_in_ua"):
                        lk = c.ua[gi][kb:kb + 64, kt * 128:(kt + 1) * 128]
                        lkB = [c.uaB[gi]]
                    else:
                        lk = c.sb[kb:kb + 64,
                                  KO + kt * 128:KO + (kt + 1) * 128]
                        lkB = [c.sbB]
                    E("pe", lambda e, lk=lk, qa=qa, qb=qb, pc=pc, kb=kb:
                        e.matmul(c.pr(0, 128, pc, pc + qb - qa), lk,
                                 c.sb[kb:kb + 64, QO + qa:QO + qb],
                                 start=True, stop=True),
                      reads=lkB + [c.sbB], writes=g, tag=f"{tag}_smm")
                    po = PT + (bi % 2) * 512
                    if not masked:
                        E("act", lambda e, pc=pc, po=po, qa=qa, qb=qb:
                            e.activation(c.sb[:, po:po + qb - qa],
                                         c.pr(0, 128, pc, pc + qb - qa),
                                         AF.Exp, bias=csh_col,
                                         scale=0.125),
                          reads=g + [c.ctB_], writes=[c.sbB],
                          tag=f"{tag}_exp")
                    else:
                        E("act", lambda e, pc=pc, qa=qa, qb=qb, kt=kt:
                            e.activation(c.sf[:, 0:qb - qa],
                                         c.pr(0, 128, pc, pc + qb - qa),
                                         AF.Copy,
                                         scale=c.ct[:, KKD + kt:KKD + kt + 1]),
                          reads=g + [c.ctB_], writes=[c.sfB],
                          tag=f"{tag}_t1")
                        E("dve", lambda e, qa=qa, qb=qb, kt=kt:
                            e.tensor_scalar(c.sf[:, 512:512 + qb - qa],
                                            c.sf[:, 0:qb - qa],
                                            c.ct[:, KKC + kt:KKC + kt + 1],
                                            None, ALU.add),
                          reads=[c.sfB, c.ctB_], writes=[c.sfB],
                          tag=f"{tag}_t1b")
                        E("pool", lambda e, qa=qa, qb=qb:
                            e.tensor_tensor(c.sf[:, 0:qb - qa],
                                            c.sf[:, 512:512 + qb - qa],
                                            c.sb[:, 2048 + qa:2048 + qb],
                                            ALU.mult),
                          reads=[c.sfB, c.sbB], writes=[c.sfB],
                          tag=f"{tag}_t2")
                        E("act", lambda e, po=po, qa=qa, qb=qb:
                            e.activation(c.sb[:, po:po + qb - qa],
                                         c.sf[:, 0:qb - qa],
                                         AF.Exp, bias=c60_col),
                          reads=[c.sfB, c.ctB_], writes=[c.sbB],
                          tag=f"{tag}_exp")
                    if bi == 0:
                        E("pool", lambda e, po=po: e.tensor_tensor(
                            c.sb[:, po:po + 128], c.sb[:, po:po + 128],
                            c.ctb[:, TRI:TRI + 128], ALU.mult),
                          reads=[c.sbB, c.ctbB], writes=[c.sbB],
                          tag=f"{tag}_tri")
                    E("pe", lambda e, kt=kt, h=h, po=po, qa=qa, qb=qb:
                        e.matmul(
                          c.pr(0, 65, 2048 + qa, 2048 + qb),
                          c.sa[:, kt * 1056 + h * 66:
                               kt * 1056 + h * 66 + 65],
                          c.sb[:, po:po + qb - qa],
                          start=(kt == 0),
                          stop=(kt == min(15, (qb - 1) // 128))),
                      reads=[c.saB, c.sbB],
                      writes=c.pb(2048 + qa, 2048 + qb), tag=f"{tag}_av")
            gctx = c.pb(2048, 4096)
            E("dve", lambda e: e.reciprocal(c.sf[0:1, :],
                                            c.pr(64, 65, 2048, 4096)),
              reads=gctx, writes=[c.sfB], tag=f"{tag}_rcp")
            for j in range(4):
                E("pe", lambda e, j=j: e.matmul(
                    c.pr(0, 64, j * 512, (j + 1) * 512),
                    c.ones_row[0:1, 0:64],
                    c.sf[0:1, j * 512:(j + 1) * 512], start=True, stop=True),
                  reads=[c.sfB, c.ctB_],
                  writes=c.pb(j * 512, (j + 1) * 512), tag=f"{tag}_rbc")
            E("act", lambda e: e.copy(c.sf[0:64, :], c.pr(0, 64, 0, 2048)),
              reads=c.pb(0, 2048), writes=[c.sfB], tag=f"{tag}_rbe")
            E("dve", lambda e, gi=gi, hh=hh: e.tensor_tensor(
                c.ua[gi][hh * 64:(hh + 1) * 64, :],
                c.pr(0, 64, 2048, 4096), c.sf[0:64, :], ALU.mult),
              reads=gctx + [c.sfB], writes=[c.uaB[gi]], tag=f"{tag}_ctx")
    ua_l = [c.ua[i][:] for i in range(NCH)]
    bo = W.get("bo_pp")

    def consume_o(oc, tq0, width, c0, guards):
        if bo is not None:
            E("dve", lambda e: e.scalar_tensor_tensor(
                c.xc[oc][:, tq0:tq0 + width], c.pr(0, 128, c0, c0 + width),
                bo[0][:, oc:oc + 1], c.xc[oc][:, tq0:tq0 + width],
                ALU.add, ALU.add),
              reads=guards + [bo[1], c.xcB[oc]], writes=[c.xcB[oc]],
              tag=f"{tag}_res")
        else:
            E("dve", lambda e: e.tensor_tensor(
                c.xc[oc][:, tq0:tq0 + width],
                c.pr(0, 128, c0, c0 + width),
                c.xc[oc][:, tq0:tq0 + width], ALU.add),
              reads=guards + [c.xcB[oc]], writes=[c.xcB[oc]],
              tag=f"{tag}_res")

    emit_proj(c, W["wo"], ua_l, c.uaB, NCH, consume_o, f"{tag}o")


def emit_l0_attn_a(c, W):
    emit_ln(c, c.xc, c.xcB, c.xl, c.xlB)
    bq, bqB = c.alloc("bq", [128, 8], F32)
    bk, bkB = c.alloc("bk", [128, 8], F32)
    bo, boB = c.alloc("bo", [128, 8], F32)
    E = c.E
    E("sync", lambda e: e.dma_start(bq[:], W["bq_pp"]), writes=[bqB], dma=True)
    E("sync", lambda e: e.dma_start(bk[:], W["bk_pp"]), writes=[bkB], dma=True)
    E("sync", lambda e: e.dma_start(bo[:], W["bo_pp"]), writes=[boB], dma=True)
    E("sync", lambda e: e.dma_start(c.ctb[:, 2048 - 128:2048], W["tri"]),
      writes=[c.ctbB], dma=True, tag="tri")
    W2 = dict(W)
    W2["bq_pp"] = (bq, bqB)
    W2["bk_pp"] = (bk, bkB)
    W2["bo_pp"] = (bo, boB)
    emit_mha(c, W2, "s1", masked=False)


def prep_l0_attn_a(inputs):
    f32 = np.float32
    tri = np.triu(np.ones((128, 128), f32)).astype(BF)
    return {
        "wq2": np.ascontiguousarray(
            inputs["L0_attn_a_Wq"].transpose(1, 0, 2).reshape(1024, 1024),
            f32),
        "wk2": np.ascontiguousarray(
            inputs["L0_attn_a_Wk"].transpose(1, 0, 2).reshape(1024, 1024),
            f32),
        "wv2": np.ascontiguousarray(
            inputs["L0_attn_a_Wv"].transpose(1, 0, 2).reshape(1024, 1024),
            f32),
        "wo": np.ascontiguousarray(inputs["L0_attn_a_Wo"], f32),
        "bq_pp": np.ascontiguousarray(
            inputs["L0_attn_a_bq"].reshape(8, 128).T, f32),
        "bk_pp": np.ascontiguousarray(
            inputs["L0_attn_a_bk"].reshape(8, 128).T, f32),
        "bv_row": np.ascontiguousarray(
            inputs["L0_attn_a_bv"].reshape(1, 1024), f32),
        "bo_pp": np.ascontiguousarray(
            inputs["L0_attn_a_bo"].reshape(8, 128).T, f32),
        "tri": tri,
    }


def emit_l1_attn_a(c, W):
    E = c.E
    emit_ln(c, c.xc, c.xcB, c.xl, c.xlB)
    E("sync", lambda e: e.dma_start(c.ctb[:, 512:640], W["tri"]),
      writes=[c.ctbB], dma=True, tag="s5tri")
    bo, boB = c.alloc("s5bo", [128, 8], F32)
    E("sync", lambda e: e.dma_start(bo[:], W["bout_pp"]), writes=[boB],
      dma=True)
    br1, br1B = c.alloc("s5br1", [128, 8], F32)
    E("sync", lambda e: e.dma_start(br1[:], W["br1_pp"]), writes=[br1B],
      dma=True)
    br2, br2B = c.alloc("s5br2", [1, 8], F32)
    E("sync", lambda e: e.dma_start(br2[:], W["br2_pp"]), writes=[br2B],
      dma=True)
    wqkvT = W["wqkvT"]
    for gi in range(8):
        for ic in range(NCH):
            stage_cast(c, wqkvT, ic * 128, 128, 1024 + gi * 128, 128,
                       c.ws, c.wsB, ic * 128, f"s5K{gi}_{ic}",
                       sf_off=(ic % 4) * 512)
        for tq in range(4):
            pc = (tq % 2) * 512
            g = c.pb(pc, pc + 512)
            for ic in range(NCH):
                E("pe", lambda e, ic=ic, tq=tq, pc=pc: e.matmul(
                    c.pr(0, 128, pc, pc + 512),
                    c.ws[:, ic * 128:ic * 128 + 128],
                    c.xl[ic][:, tq * 512:(tq + 1) * 512],
                    start=(ic == 0), stop=(ic == NCH - 1)),
                  reads=[c.wsB, c.xlB[ic]], writes=g, tag="s5_kmm")
            E("act", lambda e, gi=gi, tq=tq, pc=pc: e.copy(
                c.ua[gi][:, tq * 512:(tq + 1) * 512],
                c.pr(0, 128, pc, pc + 512)),
              reads=g, writes=[c.uaB[gi]], tag="s5_kev")
        E("dve", lambda e, gi=gi: e.reduce_sum(
            c.ct[:, 264 + gi * 32:264 + (gi + 1) * 32],
            c.ua[gi][:].rearrange("p (b t) -> p b t", t=64), axis=AX.X),
          reads=[c.uaB[gi]], writes=[c.ctB_], tag="s5_blk")
    E("dve", lambda e: e.tensor_scalar(
        c.ct[:, 264:520], c.ct[:, 264:520], 1.0 / 64, None, ALU.mult),
      reads=[c.ctB_], writes=[c.ctB_], tag="s5_blkm")
    E("pool", lambda e: e.tensor_copy(c.ctb[:, 640:896], c.ct[:, 264:520]),
      reads=[c.ctB_], writes=[c.ctbB], tag="s5_blkc")
    for ic in range(NCH):
        stage_cast(c, W["wr1T"], ic * 128, 128, 0, 128,
                   c.ws, c.wsB, ic * 128, f"s5r1_{ic}",
                   sf_off=(ic % 4) * 512)
    stage_cast(c, W["wr2T8"], 0, 128, 0, 8, c.ws, c.wsB, 1024, "s5r2")
    gr = c.pb(1024, 1536)
    for ic in range(NCH):
        E("pe", lambda e, ic=ic: e.matmul(
            c.pr(0, 128, 1024, 1056),
            c.ws[:, ic * 128:ic * 128 + 128],
            c.ctb[:, 640 + ic * 32:640 + (ic + 1) * 32],
            start=(ic == 0), stop=(ic == NCH - 1)),
          reads=[c.wsB, c.ctbB], writes=gr, tag="s5_r1mm")
    E("dve", lambda e: e.tensor_scalar(
        c.sf[:, 0:32], c.pr(0, 128, 1024, 1056), br1[:, 0:1], None, ALU.add),
      reads=gr + [br1B], writes=[c.sfB], tag="s5_r1b")
    E("act", lambda e: e.activation(c.ctb[:, 896:928], c.sf[:, 0:32],
                                    AF.Relu),
      reads=[c.sfB], writes=[c.ctbB], tag="s5_r1relu")
    E("pe", lambda e: e.matmul(
        c.pr(0, 1, 1536, 1568), c.ws[:, 1024:1025], c.ctb[:, 896:928],
        start=True, stop=True),
      reads=[c.wsB, c.ctbB], writes=c.pb(1536, 2048), tag="s5_r2mm")
    E("dve", lambda e: e.tensor_scalar(
        c.sf[0:1, 0:32], c.pr(0, 1, 1536, 1568), br2[0:1, 0:1], None,
        ALU.add),
      reads=c.pb(1536, 2048) + [br2B], writes=[c.sfB], tag="s5_r2b")
    E("act", lambda e: e.sign(c.sf[0:1, 32:64], c.sf[0:1, 0:32]),
      reads=[c.sfB], writes=[c.sfB], tag="s5_sign")
    E("dve", lambda e: e.tensor_scalar(c.sf[0:1, 32:64], c.sf[0:1, 32:64],
                                       0.0, None, ALU.max),
      reads=[c.sfB], writes=[c.sfB], tag="s5_keep")
    cc_in = c.dint("s5cc_in", [1, 32], F32)
    cc_out = c.dint("s5cc_out", [8, 32], F32)
    ccinB = c.kb.buf("ccin")
    ccoutB = c.kb.buf("ccout")
    E("sync", lambda e: e.dma_start(cc_in, c.sf[0:1, 32:64]),
      reads=[c.sfB], writes=[ccinB], dma=True, tag="s5_ccw")
    E("pool", lambda e: e.collective_compute(
        "AllGather", ALU.bypass, [list(range(B))], [cc_in.opt()],
        [cc_out.opt()]),
      reads=[ccinB], writes=[ccoutB], coll=True, tag="s5_ag")
    E("sync", lambda e: e.dma_start(c.sf[0:1, 0:32], cc_out[0:1, :]),
      reads=[ccoutB], writes=[c.sfB], dma=True, tag="s5_ccr")
    k01c, k01cB = c.alloc("k01c", [32, 8], F32)
    k01b, k01bB = c.alloc("k01b", [32, 8], BF16)
    E("pool", lambda e: e.dma_start(k01c[0:32, 0:1], c.sf[0:1, 0:32],
                                    allow_slow_non_contiguous=True),
      reads=[c.sfB], writes=[k01cB], dma=True, tag="s5_k01t")
    E("pool", lambda e: e.tensor_copy(k01b[:, 0:1], k01c[:, 0:1]),
      reads=[k01cB], writes=[k01bB], tag="s5_k01c")
    E("sync", lambda e: e.dma_start(c.ws[0:32, 1024:3072], W["esel32"]),
      writes=[c.wsB], dma=True, tag="s5_e32")
    for j in range(4):
        E("pe", lambda e, j=j: e.matmul(
            c.pr(0, 1, j * 512, (j + 1) * 512), k01b[:, 0:1],
            c.ws[0:32, 1024 + j * 512:1024 + (j + 1) * 512],
            start=True, stop=True),
          reads=[k01bB, c.wsB], writes=c.pb(j * 512, (j + 1) * 512),
          tag="s5_kkrow")
    E("act", lambda e: e.copy(c.sf[0:1, :], c.pr(0, 1, 0, 2048)),
      reads=c.pb(0, 2048), writes=[c.sfB], tag="s5_kkev")
    for j in range(4):
        E("pe", lambda e, j=j: e.matmul(
            c.pr(0, 128, 2048 + j * 512, 2048 + (j + 1) * 512),
            c.ones_row, c.sf[0:1, j * 512:(j + 1) * 512],
            start=True, stop=True),
          reads=[c.sfB, c.ctB_], writes=c.pb(2048 + j * 512,
                                             2048 + (j + 1) * 512),
          tag="s5_kqbc")
    E("act", lambda e: e.copy(c.sb[:, 2048:4096], c.pr(0, 128, 2048, 4096)),
      reads=c.pb(2048, 4096), writes=[c.sbB], tag="s5_kqev")
    for kt in range(16):
        E("pool", lambda e, kt=kt: e.dma_start(
            c.ct[:, 520 + kt:521 + kt], c.sf[0:1, kt * 128:(kt + 1) * 128],
            allow_slow_non_contiguous=True),
          reads=[c.sfB], writes=[c.ctB_], dma=True, tag="s5_kkcol")
    E("dve", lambda e: e.tensor_scalar(c.ct[:, 536:552], c.ct[:, 520:536],
                                       1.0 / 32, None, ALU.mult),
      reads=[c.ctB_], writes=[c.ctB_], tag="s5_kkd")
    E("dve", lambda e: e.tensor_scalar(c.ct[:, 552:568], c.ct[:, 520:536],
                                       60.0, None, ALU.mult),
      reads=[c.ctB_], writes=[c.ctB_], tag="s5_kkc")
    W2 = dict(W)
    W2["wq2"] = wqkvT[0:1024, 0:1024]
    W2["wv2"] = wqkvT[0:1024, 2048:3072]
    W2["wo"] = W["woT"]
    W2["bo_pp"] = (bo, boB)
    W2["k_in_ua"] = True
    W2["tri_off"] = 512
    emit_mha(c, W2, "s5", masked=True)


def prep_l1_attn_a(inputs):
    f32 = np.float32
    tri = np.triu(np.ones((128, 128), f32)).astype(BF)
    esel32 = np.zeros((32, 2048), f32)
    for b_ in range(32):
        esel32[b_, b_ * 64:(b_ + 1) * 64] = 1.0
    wr2 = np.zeros((128, 8), f32)
    wr2[:, 0:1] = inputs["L1_attn_a_w_r2"].T
    br2 = np.zeros((1, 8), f32)
    br2[0, 0] = inputs["L1_attn_a_b_r2"][0]
    return {
        "wqkvT": np.ascontiguousarray(inputs["L1_attn_a_w_qkv"].T, f32),
        "wr1T": np.ascontiguousarray(inputs["L1_attn_a_w_r1"].T, f32),
        "wr2T8": wr2,
        "br1_pp": np.ascontiguousarray(
            np.repeat(inputs["L1_attn_a_b_r1"].reshape(128, 1), 8, axis=1),
            f32),
        "br2_pp": br2,
        "woT": np.ascontiguousarray(inputs["L1_attn_a_w_out"].T, f32),
        "bout_pp": np.ascontiguousarray(
            inputs["L1_attn_a_b_out"].reshape(8, 128).T, f32),
        "tri": tri,
        "esel32": esel32.astype(BF),
    }


def emit_ffn_prepass(c, W):
    E = c.E
    w1bf = c.dint("w1bf", [8192, 4096], BF16)
    w2bf = c.dint("w2bf", [32768, 1024], BF16)
    w1bfB = c.kb.buf("w1bf")
    w2bfB = c.kb.buf("w2bf")
    E("pool", lambda e: e.engine_nop(), reads=[c.wsB],
      writes=c.wsHB[0:2], tag="pp_open")
    for i in range(128):
        r0 = (i // 2) * 128
        co = (i % 2) * 2048
        h = i % 2
        E("sync", lambda e, r0=r0, co=co, h=h: e.dma_start(
            c.sf[:, h * 1024:(h + 1) * 1024].bitcast(F32)[:, 0:1024]
            if False else c.sf[:, 0:2048][:, h * 1024:h * 1024 + 1024],
            W["w1"][r0:r0 + 128, co:co + 1024]),
          writes=[c.sfB], dma=True, tag="pp1l") if False else None
    for i in range(256):
        r0 = (i // 4) * 128
        co = (i % 4) * 1024
        h = i % 2
        E("sync", lambda e, r0=r0, co=co, h=h: e.dma_start(
            c.sf[:, h * 1024:h * 1024 + 1024],
            W["w1"][r0:r0 + 128, co:co + 1024]),
          writes=[c.sfB], dma=True, tag="pp1l")
        E("dve", lambda e, h=h: e.tensor_copy(
            c.ws[:, h * 4096:h * 4096 + 1024],
            c.sf[:, h * 1024:h * 1024 + 1024]),
          reads=[c.sfB], writes=[c.wsHB[h]], tag="pp1c")
        E("pool", lambda e, r0=r0, co=co, h=h: e.dma_start(
            w1bf[r0:r0 + 128, co:co + 1024],
            c.ws[:, h * 4096:h * 4096 + 1024]),
          reads=[c.wsHB[h]], writes=[w1bfB], dma=True, tag="pp1s")
    for i in range(256):
        r0 = i * 128
        h = i % 2
        E("sync", lambda e, r0=r0, h=h: e.dma_start(
            c.sf[:, h * 1024:h * 1024 + 1024],
            W["w2"][r0:r0 + 128, 0:1024]),
          writes=[c.sfB], dma=True, tag="pp2l")
        E("dve", lambda e, h=h: e.tensor_copy(
            c.ws[:, h * 4096:h * 4096 + 1024],
            c.sf[:, h * 1024:h * 1024 + 1024]),
          reads=[c.sfB], writes=[c.wsHB[h]], tag="pp2c")
        E("pool", lambda e, r0=r0, h=h: e.dma_start(
            w2bf[r0:r0 + 128, 0:1024], c.ws[:, h * 4096:h * 4096 + 1024]),
          reads=[c.wsHB[h]], writes=[w2bfB], dma=True, tag="pp2s")
    E("pool", lambda e: e.engine_nop(), reads=c.wsHB[0:2],
      writes=[c.wsB], tag="pp_close")
    W["w1bf"] = w1bf
    W["w2bf"] = w2bf
    W["w1bfB"] = w1bfB
    W["w2bfB"] = w2bfB


def emit_l0_ffn(c, W, it=0):
    E = c.E
    tg = f"moe{it}"
    emit_ln(c, c.xc, c.xcB, c.xl, c.xlB)
    E("pool", lambda e: e.engine_nop(), reads=[c.wsB], writes=c.wsHB[0:2],
      tag=f"{tg}_wsopen")
    if W.get("spill"):
        for i in range(NCH):
            E("sync", lambda e, i=i: e.dma_start(
                W["x_dram"][i * 128:(i + 1) * 128, :], c.xc[i][:]),
              reads=[c.xcB[i]], dma=True, tag=f"{tg}_spill{i}")
    idf, idfB = c.alloc("idf", [128, 128], F32)
    E("sync", lambda e: e.dma_start(idf[:], W["ident"]), writes=[idfB],
      dma=True)
    b1t, b1tB = c.alloc("b1t", [128, 256], F32)
    b2t, b2tB = c.alloc("b2t", [128, 64], F32)
    gbt, gbtB = c.alloc("gbt2", [128, 16], F32)
    bgt, bgtB = c.alloc("bgt", [8, 8], F32)
    E("sync", lambda e: e.dma_start(b1t[:], W["b1_pt"]), writes=[b1tB],
      dma=True)
    E("sync", lambda e: e.dma_start(b2t[:], W["b2_pt"]), writes=[b2tB],
      dma=True)
    E("sync", lambda e: e.dma_start(gbt[:], W["gb_pp"]), writes=[gbtB],
      dma=True)
    E("sync", lambda e: e.dma_start(bgt[:], W["bg8"]), writes=[bgtB],
      dma=True)
    stage_cast(c, W["wg_pp"], 0, 128, 0, 64, c.ctb, c.ctbB, 512, f"{tg}wg")
    gL = c.pb(0, 2048)
    for ic in range(NCH):
        for j in range(4):
            E("pe", lambda e, ic=ic, j=j: e.matmul(
                c.pr(0, 8, j * 512, (j + 1) * 512),
                c.ctb[:, 512 + ic * 8:512 + (ic + 1) * 8],
                c.xl[ic][:, j * 512:(j + 1) * 512],
                start=(ic == 0), stop=(ic == NCH - 1)),
              reads=[c.ctbB, c.xlB[ic]], writes=gL, tag=f"{tg}_gmm")
    E("dve", lambda e: e.tensor_scalar(c.sf[0:8, :], c.pr(0, 8, 0, 2048),
                                       bgt[0:8, 0:1], None, ALU.add),
      reads=gL + [bgtB], writes=[c.sfB], tag=f"{tg}_bg")
    gT = c.pb(2048, 2560)
    for i in range(16):
        E("pe", lambda e, i=i: e.transpose(
            c.pr(0, 128, 2048 + (i % 16) * 8, 2048 + (i % 16) * 8 + 8),
            c.sf[0:8, i * 128:(i + 1) * 128], idf[0:8, 0:8]),
          reads=[c.sfB, idfB], writes=gT, tag=f"{tg}_ltr")
    E("act", lambda e: e.copy(c.ct[:, 264:392], c.pr(0, 128, 2048, 2176)),
      reads=gT, writes=[c.ctB_], tag=f"{tg}_ltr_ev")
    for i in range(16):
        L = c.ct[:, 264 + i * 8:272 + i * 8]
        CM = c.ct[:, 392 + i * 8:400 + i * 8]
        nm1 = c.ct[:, 521:522]
        m1 = c.ct[:, 520:521]
        m2 = c.ct[:, 522:523]
        den = c.ct[:, 523:524]
        sc1 = c.ct[:, 524:532]
        sc2 = c.ct[:, 532:540]
        E("dve", lambda e, L=L, nm1=nm1: e.reduce_max(nm1, L, axis=AX.X,
                                                      negate=True),
          reads=[c.ctB_], writes=[c.ctB_], tag=f"{tg}_m1")
        E("dve", lambda e, m1=m1, nm1=nm1: e.tensor_scalar(
            m1, nm1, -1.0, None, ALU.mult),
          reads=[c.ctB_], writes=[c.ctB_], tag=f"{tg}_m1b")
        E("dve", lambda e, L=L, m1=m1, sc1=sc1: e.tensor_scalar(
            sc1, L, m1, None, ALU.is_ge),
          reads=[c.ctB_], writes=[c.ctB_], tag=f"{tg}_ge1")
        E("dve", lambda e, L=L, sc1=sc1, sc2=sc2: e.scalar_tensor_tensor(
            sc2, sc1, -1e30, L, ALU.mult, ALU.add),
          reads=[c.ctB_], writes=[c.ctB_], tag=f"{tg}_lp")
        E("dve", lambda e, m2=m2, sc2=sc2: e.reduce_max(m2, sc2, axis=AX.X),
          reads=[c.ctB_], writes=[c.ctB_], tag=f"{tg}_m2")
        E("dve", lambda e, m2=m2, nm1=nm1, den=den: e.tensor_tensor(
            den, m2, nm1, ALU.add),
          reads=[c.ctB_], writes=[c.ctB_], tag=f"{tg}_dm")
        E("act", lambda e, den=den: e.activation(den, den, AF.Exp),
          reads=[c.ctB_], writes=[c.ctB_], tag=f"{tg}_de")
        E("dve", lambda e, den=den: e.tensor_scalar(den, den, 1.0, None,
                                                    ALU.add),
          reads=[c.ctB_], writes=[c.ctB_], tag=f"{tg}_d1")
        E("dve", lambda e, den=den: e.reciprocal(den, den),
          reads=[c.ctB_], writes=[c.ctB_], tag=f"{tg}_dr")
        E("dve", lambda e, L=L, m2=m2, sc1=sc1: e.tensor_scalar(
            sc1, L, m2, None, ALU.is_ge),
          reads=[c.ctB_], writes=[c.ctB_], tag=f"{tg}_ge2")
        E("act", lambda e, L=L, nm1=nm1, sc2=sc2: e.activation(
            sc2, L, AF.Exp, bias=nm1),
          reads=[c.ctB_], writes=[c.ctB_], tag=f"{tg}_ex")
        E("dve", lambda e, sc2=sc2, den=den, CM=CM: e.tensor_scalar(
            CM, sc2, den, None, ALU.mult),
          reads=[c.ctB_], writes=[c.ctB_], tag=f"{tg}_cm1")
        E("dve", lambda e, CM=CM, sc1=sc1: e.tensor_tensor(
            CM, CM, sc1, ALU.mult),
          reads=[c.ctB_], writes=[c.ctB_], tag=f"{tg}_cm2")
    gT2 = c.pb(2048, 4096)
    for i in range(16):
        E("pe", lambda e, i=i: e.transpose(
            c.pr(0, 8, 2048 + i * 128, 2048 + (i + 1) * 128),
            c.ct[:, 392 + i * 8:400 + i * 8], idf[:]),
          reads=[c.ctB_, idfB], writes=gT2, tag=f"{tg}_ctr")
    E("act", lambda e: e.copy(c.sf[0:8, :], c.pr(0, 8, 2048, 4096)),
      reads=gT2, writes=[c.sfB], tag=f"{tg}_cev")
    cmb_dram = c.dint(f"cmb{it}", [8, 2048], F32)
    cmbB = c.kb.buf(f"cmb{it}")
    E("sync", lambda e: e.dma_start(cmb_dram, c.sf[0:8, :]),
      reads=[c.sfB], writes=[cmbB], dma=True, tag=f"{tg}_cst")
    for i in range(NCH):
        E("pool", lambda e, i=i: e.tensor_copy(c.ua[i][:], c.xl[i][:]),
          reads=[c.xlB[i]], writes=[c.uaB[i]], tag=f"{tg}_ui")
    for ex in range(E_EXP):
        w1ap = W["w1"]
        w2ap = W["w2"]
        w1r = ex * 1024
        w2r = ex * 4096
        for j in range(4):
            E("pool", lambda e, ex=ex, j=j: e.dma_start(
                c.ct[0:1, 264:776],
                cmb_dram[ex:ex + 1, j * 512:(j + 1) * 512]),
              reads=[cmbB], writes=[c.ctB_], dma=True, tag=f"{tg}_crow")
            E("pe", lambda e, j=j: e.matmul(
                c.pr(0, 128, 1024, 1536), c.ones_row, c.ct[0:1, 264:776],
                start=True, stop=True),
              reads=[c.ctB_], writes=c.pb(1024, 1536), tag=f"{tg}_cbc")
            E("act", lambda e, j=j: e.copy(
                c.sb[:, j * 512:(j + 1) * 512],
                c.pr(0, 128, 1024, 1536)),
              reads=c.pb(1024, 1536), writes=[c.sbB], tag=f"{tg}_cbev")
        for hc in range(32):
            E("sync", lambda e, hc=hc, w2r=w2r: e.dma_start(
                c.xc[hc // 4][:].bitcast(BF16)[:, (hc % 4) * 1024:
                                               (hc % 4) * 1024 + 1024],
                W["w2bf"][w2r + hc * 128:w2r + (hc + 1) * 128, 0:1024]),
              reads=[W["w2bfB"]], writes=[c.xcB[hc // 4]], dma=True,
              tag=f"{tg}_w2l")
        for tq in range(4):
            t0 = tq * 512
            for gph in range(8):
                half = (gph % 2) * 4096
                hb = c.wsHB[gph % 2]
                for ic in range(NCH):
                    E("sync", lambda e, ic=ic, gph=gph, half=half, w1r=w1r:
                        e.dma_start(
                          c.ws[:, half + ic * 512:half + (ic + 1) * 512],
                          W["w1bf"][w1r + ic * 128:w1r + (ic + 1) * 128,
                                    gph * 512:(gph + 1) * 512]),
                      reads=[W["w1bfB"]], writes=[hb], dma=True,
                      tag=f"{tg}_w1l")
                for hcl in range(4):
                    hc = gph * 4 + hcl
                    pc = (hc % 2) * 512
                    gH = c.pb(pc, pc + 512)
                    for ic in range(NCH):
                        E("pe", lambda e, ic=ic, hcl=hcl, half=half, pc=pc,
                            t0=t0: e.matmul(
                              c.pr(0, 128, pc, pc + 512),
                              c.ws[:, half + ic * 512 + hcl * 128:
                                   half + ic * 512 + hcl * 128 + 128],
                              c.xl[ic][:, t0:t0 + 512],
                              start=(ic == 0), stop=(ic == NCH - 1)),
                          reads=[hb, c.xlB[ic]], writes=gH,
                          tag=f"{tg}_amm")
                    E("act", lambda e, hc=hc, pc=pc, ex=ex: e.activation(
                        c.sa[:, hc * 512:(hc + 1) * 512],
                        c.pr(0, 128, pc, pc + 512), AF.Gelu,
                        bias=b1t[:, ex * 32 + hc:ex * 32 + hc + 1]),
                      reads=c.pb(pc, pc + 512) + [b1tB], writes=[c.saB],
                      tag=f"{tg}_gelu")
            for w in range(2):
                gY = c.pb(2048, 4096)
                for hc in range(32):
                    for k in range(4):
                        dc = w * 4 + k
                        E("pe", lambda e, hc=hc, k=k, dc=dc: e.matmul(
                            c.pr(0, 128, 2048 + k * 512, 2048 + k * 512 + 512),
                            c.xc[hc // 4][:].bitcast(BF16)[
                                :, (hc % 4) * 1024 + dc * 128:
                                (hc % 4) * 1024 + dc * 128 + 128],
                            c.sa[:, hc * 512:(hc + 1) * 512],
                            start=(hc == 0), stop=(hc == 31)),
                          reads=[c.xcB[hc // 4], c.saB],
                          writes=c.pb(2048 + k * 512, 2048 + k * 512 + 512),
                          tag=f"{tg}_bmm")
                for k in range(4):
                    dc = w * 4 + k
                    gk = c.pb(2048 + k * 512, 2048 + k * 512 + 512)
                    E("dve", lambda e, k=k, dc=dc, ex=ex, t0=t0:
                        e.scalar_tensor_tensor(
                          c.sb[:, 2048 + k * 512:2048 + (k + 1) * 512],
                          c.pr(0, 128, 2048 + k * 512, 2048 + k * 512 + 512),
                          b2t[:, ex * 8 + dc:ex * 8 + dc + 1],
                          c.sb[:, t0:t0 + 512],
                          ALU.add, ALU.mult),
                      reads=gk + [b2tB, c.sbB], writes=[c.sbB],
                      tag=f"{tg}_yev")
                    E("pool", lambda e, k=k, dc=dc, t0=t0: e.tensor_tensor(
                        c.ua[dc][:, t0:t0 + 512], c.ua[dc][:, t0:t0 + 512],
                        c.sb[:, 2048 + k * 512:2048 + (k + 1) * 512],
                        ALU.add),
                      reads=[c.sbB, c.uaB[dc]], writes=[c.uaB[dc]],
                      tag=f"{tg}_uacc")
    E("pool", lambda e: e.engine_nop(), reads=c.wsHB[0:2], writes=[c.wsB],
      tag=f"{tg}_wsclose")
    gA = c.pb(0, 2048)
    gB = c.pb(2048, 4096)
    for i in range(NCH):
        E("pool", lambda e, i=i: e.tensor_tensor(
            c.sb[:, 0:2048], c.ua[i][:], c.ua[i][:], ALU.mult),
          reads=[c.uaB[i]], writes=[c.sbB], tag=f"{tg}_usq")
        for j in range(4):
            E("pe", lambda e, i=i, j=j: e.matmul(
                c.pr(0, 1, j * 512, (j + 1) * 512), c.ones_col16,
                c.ua[i][:, j * 512:(j + 1) * 512],
                start=(i == 0), stop=(i == NCH - 1)),
              reads=[c.uaB[i], c.ctbB], writes=gA, tag=f"{tg}_us")
            E("pe", lambda e, i=i, j=j: e.matmul(
                c.pr(32, 33, j * 512, (j + 1) * 512), c.ones_col16,
                c.sb[:, j * 512:(j + 1) * 512],
                start=(i == 0), stop=(i == NCH - 1)),
              reads=[c.sbB, c.ctbB], writes=gA, tag=f"{tg}_us2")
    r_mean = c.sf[0:1, :]
    r_a = c.ws[0:1, 0:4096].bitcast(F32)
    r_b = c.ws[0:1, 4096:8192].bitcast(F32)
    eps_col = c.const(LN_EPS)
    E("dve", lambda e: e.tensor_scalar(r_mean, c.pr(0, 1, 0, 2048),
                                       1.0 / D, None, ALU.mult),
      reads=gA, writes=[c.sfB], tag=f"{tg}_fmean")
    E("dve", lambda e: e.tensor_scalar(r_a, c.pr(32, 33, 0, 2048),
                                       1.0 / D, None, ALU.mult),
      reads=gA, writes=[c.wsB], tag=f"{tg}_fms")
    E("dve", lambda e: e.scalar_tensor_tensor(
        r_b, r_mean, -1.0, r_mean, ALU.mult, ALU.mult),
      reads=[c.sfB], writes=[c.wsB], tag=f"{tg}_fnm2")
    E("dve", lambda e: e.tensor_tensor(r_a, r_a, r_b, ALU.add),
      reads=[c.wsB], writes=[c.wsB], tag=f"{tg}_fvar")
    E("act", lambda e: e.activation(r_b, r_a, AF.Ln, bias=eps_col[0:1, :]),
      reads=[c.wsB, c.ctB_], writes=[c.wsB], tag=f"{tg}_fln")
    E("act", lambda e: e.activation(r_a, r_b, AF.Exp, scale=-0.5),
      reads=[c.wsB], writes=[c.wsB], tag=f"{tg}_frstd")
    E("dve", lambda e: e.scalar_tensor_tensor(
        r_b, r_mean, -1.0, r_a, ALU.mult, ALU.mult),
      reads=[c.sfB, c.wsB], writes=[c.wsB], tag=f"{tg}_fb")
    for j in range(4):
        E("pe", lambda e, j=j: e.matmul(
            c.pr(0, 128, j * 512, (j + 1) * 512), c.ones_row,
            c.ws[0:1, j * 1024:j * 1024 + 1024].bitcast(F32),
            start=True, stop=True),
          reads=[c.wsB, c.ctB_], writes=gA, tag=f"{tg}_fbcA")
        E("pe", lambda e, j=j: e.matmul(
            c.pr(0, 128, 2048 + j * 512, 2048 + (j + 1) * 512), c.ones_row,
            c.ws[0:1, 4096 + j * 1024:4096 + j * 1024 + 1024].bitcast(F32),
            start=True, stop=True),
          reads=[c.wsB, c.ctB_], writes=gB, tag=f"{tg}_fbcB")
    for i in range(NCH):
        E("sync", lambda e, i=i: e.dma_start(
            c.xc[i][:], W["x_dram"][i * 128:(i + 1) * 128, :]),
          writes=[c.xcB[i]], dma=True, tag=f"{tg}_xrl{i}")
        E("dve", lambda e, i=i: e.tensor_tensor(
            c.sf[:], c.ua[i][:], c.pr(0, 128, 0, 2048), ALU.mult),
          reads=[c.uaB[i]] + gA, writes=[c.sfB], tag=f"{tg}_fn1")
        E("dve", lambda e, i=i: e.tensor_tensor(
            c.sf[:], c.sf[:], c.pr(0, 128, 2048, 4096), ALU.add),
          reads=[c.sfB] + gB, writes=[c.sfB], tag=f"{tg}_fn2")
        E("act", lambda e, i=i: e.activation(
            c.sf[:], c.sf[:], AF.Copy, scale=gbt[:, i:i + 1]),
          reads=[c.sfB, gbtB], writes=[c.sfB], tag=f"{tg}_fg")
        E("dve", lambda e, i=i: e.scalar_tensor_tensor(
            c.xc[i][:], c.sf[:], gbt[:, 8 + i:9 + i], c.xc[i][:],
            ALU.add, ALU.add),
          reads=[c.sfB, gbtB, c.xcB[i]], writes=[c.xcB[i]],
          tag=f"{tg}_fres")


def prep_l0_ffn(inputs):
    f32 = np.float32
    bg8 = np.repeat(inputs["L0_ffn_bg"].reshape(8, 1), 8, axis=1)
    gb = np.stack([inputs["L0_ffn_gamma"], inputs["L0_ffn_beta"]])
    return {
        "wg_pp": np.ascontiguousarray(
            inputs["L0_ffn_Wg"].reshape(8, 128, 8).transpose(1, 0, 2)
            .reshape(128, 64), f32),
        "bg8": np.ascontiguousarray(bg8, f32),
        "w1": np.ascontiguousarray(inputs["L0_ffn_W1"].reshape(8192, 4096), f32),
        "w2": np.ascontiguousarray(inputs["L0_ffn_W2"].reshape(32768, 1024), f32),
        "b1_pt": np.ascontiguousarray(
            inputs["L0_ffn_b1"].reshape(8, 32, 128).transpose(2, 0, 1)
            .reshape(128, 256), f32),
        "b2_pt": np.ascontiguousarray(
            inputs["L0_ffn_b2"].reshape(8, 8, 128).transpose(2, 0, 1)
            .reshape(128, 64), f32),
        "gb_pp": np.ascontiguousarray(
            gb.reshape(2, 8, 128).transpose(2, 0, 1).reshape(128, 16), f32),
        "ident": np.eye(128, dtype=f32),
    }


def emit_s7_qkv(c, W, qkv_out, qkv_outB):
    E = c.E
    emit_ln(c, c.xc, c.xcB, c.xl, c.xlB)
    gbt, gbtB = c.alloc("s7gb", [128, 16], F32)
    E("sync", lambda e: e.dma_start(gbt[:], W["gb_pp"]), writes=[gbtB],
      dma=True)
    emit_ln(c, c.xl, c.xlB, c.ua, c.uaB, eps=1e-6, src16=True,
            gb=(gbt, gbtB, 0, 8))
    for t3 in range(3):
        for oc in range(NCH):
            for ic in range(NCH):
                stage_cast(c, W["qkvk"], t3 * 1024 + ic * 128, 128,
                           oc * 128, 128, c.ws, c.wsB,
                           ((oc % 4) * 8 + ic) * 128, f"s7w{t3}_{oc}_{ic}",
                           sf_off=(ic % 4) * 512)
            for tq in range(4):
                pc = (tq % 2) * 512
                g = c.pb(pc, pc + 512)
                for ic in range(NCH):
                    E("pe", lambda e, ic=ic, oc=oc, tq=tq, pc=pc: e.matmul(
                        c.pr(0, 128, pc, pc + 512),
                        c.ws[:, ((oc % 4) * 8 + ic) * 128:
                             ((oc % 4) * 8 + ic) * 128 + 128],
                        c.ua[ic][:, tq * 512:(tq + 1) * 512],
                        start=(ic == 0), stop=(ic == NCH - 1)),
                      reads=[c.wsB, c.uaB[ic]], writes=g, tag="s7_pmm")
                E("act", lambda e, tq=tq, pc=pc: e.copy(
                    c.xl[3][:, tq * 512:(tq + 1) * 512],
                    c.pr(0, 128, pc, pc + 512)),
                  reads=g, writes=[c.xlB[3]], tag="s7_kvev")
            E("sync", lambda e, t3=t3, oc=oc: e.dma_start(
                qkv_out[t3 * 1024 + oc * 128:t3 * 1024 + (oc + 1) * 128, :],
                c.xl[3][:]),
              reads=[c.xlB[3]], writes=[qkv_outB], dma=True, tag="s7_kvst")


def emit_s7_attn(c, W, q_in, kv_all, kv_allB):
    E = c.E
    msk, mskB = c.alloc("s7msk", [128, 8], F32)
    E("sync", lambda e: e.dma_start(msk[:], W["mask_pp"]), writes=[mskB],
      dma=True)
    E("sync", lambda e: e.dma_start(c.ctb[:, 512:528], W["esel8"]),
      writes=[c.ctbB], dma=True, tag="s7e8")
    E("sync", lambda e: e.dma_start(c.ctb[0:16, 528:1552], W["eselT"]),
      writes=[c.ctbB], dma=True, tag="s7eT")
    for oc in range(NCH):
        E("sync", lambda e, oc=oc: e.dma_start(
            c.sa[:, oc * 2048:(oc + 1) * 2048],
            q_in[oc * 128:(oc + 1) * 128, :]),
          writes=[c.saB], dma=True, tag="s7_qld")
    E("sync", lambda e: e.dma_start(c.ws[:], W["sel2"]),
      writes=[c.wsB], dma=True, tag="s7_sel2")
    gL = c.pb(0, 2048)
    for kb in range(B):
        for ic in range(NCH):
            koff = (ic % 2) * 2048
            E("sync", lambda e, kb=kb, ic=ic, koff=koff: e.dma_start(
                c.sb[:, koff:koff + 2048],
                kv_all[kb * 2048 + ic * 128:kb * 2048 + (ic + 1) * 128, :]),
              writes=[c.sbB], dma=True, tag="s7_kld")
            E("dve", lambda e, ic=ic, koff=koff: e.tensor_tensor(
                c.xl[0][:], c.sa[:, ic * 2048:(ic + 1) * 2048],
                c.sb[:, koff:koff + 2048], ALU.mult),
              reads=[c.sbB, c.saB], writes=[c.xlB[0]], tag="s7_prod")
            for j in range(4):
                E("pe", lambda e, kb=kb, ic=ic, j=j: e.matmul(
                    c.pr(0, 128, j * 512, (j + 1) * 512),
                    c.ws[:, (ic * 8 + kb) * 128:(ic * 8 + kb) * 128 + 128],
                    c.xl[0][:, j * 512:(j + 1) * 512],
                    start=(ic == 0 and kb == 0),
                    stop=(ic == NCH - 1 and kb == B - 1)),
                  reads=[c.wsB, c.xlB[0]], writes=gL, tag="s7_lmm")
    E("act", lambda e: e.activation(c.xl[1][:], c.pr(0, 128, 0, 2048),
                                    AF.Exp, bias=msk[:, 0:1], scale=0.125),
      reads=gL + [mskB], writes=[c.xlB[1]], tag="s7_exp")
    for j in range(4):
        E("pe", lambda e, j=j: e.matmul(
            c.pr(0, 16, j * 512, (j + 1) * 512), c.ctb[:, 512:528],
            c.xl[1][:, j * 512:(j + 1) * 512], start=True, stop=True),
          reads=[c.ctbB, c.xlB[1]], writes=gL, tag="s7_rs")

    def _rcp(e):
        with c.nc.allow_low_precision(reason="softmax normalizer in bf16"):
            return e.reciprocal(c.xl[2][0:16, :], c.pr(0, 16, 0, 2048))
    E("dve", _rcp, reads=gL, writes=[c.xlB[2]], tag="s7_rcp")
    E("sync", lambda e: e.dma_start(c.ws[:], W["sel2T"]),
      writes=[c.wsB], dma=True, tag="s7_sel2T")
    for ic in range(NCH):
        for kb in range(B):
            koff = (kb % 2) * 2048
            E("sync", lambda e, kb=kb, ic=ic, koff=koff: e.dma_start(
                c.sb[:, koff:koff + 2048],
                kv_all[kb * 2048 + 1024 + ic * 128:
                       kb * 2048 + 1024 + (ic + 1) * 128, :]),
              writes=[c.sbB], dma=True, tag="s7_vld")
            for j in range(4):
                E("pe", lambda e, kb=kb, ic=ic, j=j: e.matmul(
                    c.pr(0, 128, 2048 + j * 512, 2048 + (j + 1) * 512),
                    c.ws[:, (ic * 8 + kb) * 128:(ic * 8 + kb) * 128 + 128],
                    c.xl[1][:, j * 512:(j + 1) * 512],
                    start=True, stop=True),
                  reads=[c.wsB, c.xlB[1]],
                  writes=c.pb(2048 + j * 512, 2048 + (j + 1) * 512),
                  tag="s7_pbc")
            if kb == 0:
                for j in range(4):
                    E("dve", lambda e, j=j, koff=koff: e.tensor_tensor(
                        c.sf[:, j * 512:(j + 1) * 512],
                        c.sb[:, koff + j * 512:koff + (j + 1) * 512],
                        c.pr(0, 128, 2048 + j * 512, 2048 + (j + 1) * 512),
                        ALU.mult),
                      reads=[c.sbB] + c.pb(2048 + j * 512,
                                           2048 + (j + 1) * 512),
                      writes=[c.sfB], tag="s7_cx0")
            else:
                for j in range(4):
                    E("dve", lambda e, j=j, koff=koff: e.tensor_tensor(
                        c.sb[:, 4096 + (j % 3) * 512:
                             4096 + (j % 3) * 512 + 512],
                        c.sb[:, koff + j * 512:koff + (j + 1) * 512],
                        c.pr(0, 128, 2048 + j * 512, 2048 + (j + 1) * 512),
                        ALU.mult),
                      reads=[c.sbB] + c.pb(2048 + j * 512,
                                           2048 + (j + 1) * 512),
                      writes=[c.sbB], tag="s7_cxm")
                    E("pool", lambda e, j=j: e.tensor_tensor(
                        c.sf[:, j * 512:(j + 1) * 512],
                        c.sf[:, j * 512:(j + 1) * 512],
                        c.sb[:, 4096 + (j % 3) * 512:
                             4096 + (j % 3) * 512 + 512], ALU.add),
                      reads=[c.sbB, c.sfB], writes=[c.sfB], tag="s7_cxa")
        for j in range(4):
            E("pe", lambda e, ic=ic, j=j: e.matmul(
                c.pr(0, 128, 2048 + j * 512, 2048 + (j + 1) * 512),
                c.ctb[0:16, 528 + ic * 128:528 + (ic + 1) * 128],
                c.xl[2][0:16, j * 512:(j + 1) * 512],
                start=True, stop=True),
              reads=[c.ctbB, c.xlB[2]],
              writes=c.pb(2048 + j * 512, 2048 + (j + 1) * 512),
              tag="s7_rbc")
            E("dve", lambda e, ic=ic, j=j: e.tensor_tensor(
                c.ua[ic][:, j * 512:(j + 1) * 512],
                c.sf[:, j * 512:(j + 1) * 512],
                c.pr(0, 128, 2048 + j * 512, 2048 + (j + 1) * 512),
                ALU.mult),
              reads=[c.sfB] + c.pb(2048 + j * 512, 2048 + (j + 1) * 512),
              writes=[c.uaB[ic]], tag="s7_nrm")
    ua_l = [c.ua[i][:] for i in range(NCH)]

    def consume_o(oc, tq0, width, c0, guards):
        E("dve", lambda e: e.tensor_tensor(
            c.xc[oc][:, tq0:tq0 + width], c.pr(0, 128, c0, c0 + width),
            c.xc[oc][:, tq0:tq0 + width], ALU.add),
          reads=guards + [c.xcB[oc]], writes=[c.xcB[oc]], tag="s7_res")

    emit_proj(c, W["wout"], ua_l, c.uaB, NCH, consume_o, "s7o")


def prep_l1_attn_b(inputs):
    f32 = np.float32
    esel8 = np.zeros((128, 16), f32)
    eselT = np.zeros((8, 16, 128), f32)
    sel2 = np.zeros((128, 64 * 128), f32)
    sel2T = np.zeros((128, 64 * 128), f32)
    for ic in range(8):
        for p in range(128):
            h = 2 * ic + p // 64
            for kb in range(8):
                sel2[p, (ic * 8 + kb) * 128 + h * 8 + kb] = 1.0
                sel2T[h * 8 + kb, (ic * 8 + kb) * 128 + p] = 1.0
            eselT[ic, h % 16, p] = 0.0
    for pp in range(128):
        esel8[pp, pp // 8] = 1.0
    for ic in range(8):
        for p in range(128):
            h = 2 * ic + p // 64
            eselT[ic, h, p] = 1.0
    gb = np.stack([inputs["L1_attn_b_ln_scale"], inputs["L1_attn_b_ln_bias"]])
    masks = []
    for core in range(8):
        m = np.zeros((128, 8), f32)
        for p in range(128):
            if p % 8 > core:
                m[p, 0] = -1e10
        masks.append(np.ascontiguousarray(m))
    return {
        "qkvk": np.ascontiguousarray(
            inputs["L1_attn_b_qkv_kernel"].transpose(1, 0, 2)
            .reshape(3072, 1024), f32),
        "wout": np.ascontiguousarray(inputs["L1_attn_b_out_kernel"], f32),
        "gb_pp": np.ascontiguousarray(
            gb.reshape(2, 8, 128).transpose(2, 0, 1).reshape(128, 16), f32),
        "esel8": esel8.astype(BF),
        "eselT": np.ascontiguousarray(
            eselT.transpose(1, 0, 2).reshape(16, 1024)).astype(BF),
        "sel2": sel2.astype(BF),
        "sel2T": sel2T.astype(BF),
        "mask_pp": masks,
    }


def run_launch(subs, x_fm, inputs, extra=None):
    c = C("L" + "_".join(s for s in subs))
    x_in = c.din("x_in", [np.ascontiguousarray(x_fm[i]) for i in range(B)],
                 percore=True)
    x_out = c.dout("x_out", (1024, 2048), F32)
    load_x(c, x_in)

    def din_w(pfx, W):
        out = {}
        for k, v in W.items():
            dt = BF16 if v.dtype == BF else F32
            out[k] = c.din(f"{pfx}_{k}", v, dt=dt)
        return out

    for s in subs:
        if s in ("s6", "s8"):
            emit_l1_ffn(c, din_w(s, prep_l1_ffn(inputs)),
                        it=0 if s == "s6" else 1)
        elif s == "s3":
            emit_l0_attn_b(c, din_w(s, prep_l0_attn_b(inputs)))
        elif s == "s1":
            emit_l0_attn_a(c, din_w(s, prep_l0_attn_a(inputs)))
        elif s == "s5":
            emit_l1_attn_a(c, din_w(s, prep_l1_attn_a(inputs)))
        elif s == "s7a":
            Wh = prep_l1_attn_b(inputs)
            Wh.pop("mask_pp")
            W = din_w(s, {"qkvk": Wh["qkvk"], "gb_pp": Wh["gb_pp"]})
            qkv_out = c.dout("qkv_out", (3 * 1024, 2048), BF16)
            qkvB = c.kb.buf("qkv_out")
            emit_s7_qkv(c, W, qkv_out, qkvB)
        elif s == "s7b":
            Wh = prep_l1_attn_b(inputs)
            masks = Wh.pop("mask_pp")
            W = din_w(s, {k: Wh[k] for k in
                          ("wout", "esel8", "eselT", "sel2", "sel2T")})
            W["mask_pp"] = c.din("s7_mask_pp", masks, percore=True)
            q_in = c.din("s7_q", extra["q"], dt=BF16, percore=True)
            kv_all = c.din("s7_kv", extra["kv"], dt=BF16)
            emit_s7_attn(c, W, q_in, kv_all, c.kb.buf("kv_all"))
        elif s in ("s2", "s4"):
            W = din_w(s, prep_l0_ffn(inputs))
            W["x_dram"] = x_in
            W["spill"] = False
            emit_ffn_prepass(c, W)
            emit_l0_ffn(c, W, it=0 if s == "s2" else 1)
        else:
            raise NotImplementedError(s)
    store_x(c, x_out)
    stats = c.finish()
    res = c.run()
    out = np.stack([res.results[i]["x_out"] for i in range(B)])
    ex = {}
    if "s7a" in subs:
        ex["qkv"] = [np.asarray(res.results[i]["qkv_out"]) for i in range(B)]
    return out, stats, ex


LAUNCH_PLAN = [["s1"], ["s2"], ["s3"], ["s4"], ["s5"], ["s6"],
               ["s7a"], ["s7b"], ["s8"]]
FUSED2 = True


def run_fused_A(x_fm, inputs):
    c = C("fA")
    x_in = c.din("x_in", [np.ascontiguousarray(x_fm[i]) for i in range(B)],
                 percore=True)
    x_out = c.dout("x_out", (1024, 2048), F32)
    qkv_out = c.dout("qkv_out", (3 * 1024, 2048), BF16)
    qkvB = c.kb.buf("qkv_out")
    xspill = c.dint("xspill", (1024, 2048), F32)

    def din_w(pfx, W):
        out = {}
        for k, v in W.items():
            dt = BF16 if v.dtype == BF else F32
            out[k] = c.din(f"{pfx}_{k}", v, dt=dt)
        return out

    load_x(c, x_in)
    emit_l0_attn_a(c, din_w("s1", prep_l0_attn_a(inputs)))
    Wf = din_w("s2", prep_l0_ffn(inputs))
    Wf["x_dram"] = xspill
    Wf["spill"] = True
    emit_ffn_prepass(c, Wf)
    emit_l0_ffn(c, Wf, it=0)
    emit_l0_attn_b(c, din_w("s3", prep_l0_attn_b(inputs)))
    emit_l0_ffn(c, Wf, it=1)
    emit_l1_attn_a(c, din_w("s5", prep_l1_attn_a(inputs)))
    W6 = din_w("s6", prep_l1_ffn(inputs))
    emit_l1_ffn(c, W6, it=0)
    W7 = prep_l1_attn_b(inputs)
    W7.pop("mask_pp")
    W7a = din_w("s7a", {"qkvk": W7["qkvk"], "gb_pp": W7["gb_pp"]})
    emit_s7_qkv(c, W7a, qkv_out, qkvB)
    store_x(c, x_out)
    stats = c.finish()
    res = c.run()
    out = np.stack([res.results[i]["x_out"] for i in range(B)])
    qkv = [np.asarray(res.results[i]["qkv_out"]) for i in range(B)]
    return out, stats, qkv


def run_fused_B(x_fm, inputs, extra):
    c = C("fB")
    x_in = c.din("x_in", [np.ascontiguousarray(x_fm[i]) for i in range(B)],
                 percore=True)
    x_out = c.dout("x_out", (1024, 2048), F32)

    def din_w(pfx, W):
        out = {}
        for k, v in W.items():
            dt = BF16 if v.dtype == BF else F32
            out[k] = c.din(f"{pfx}_{k}", v, dt=dt)
        return out

    load_x(c, x_in)
    Wh = prep_l1_attn_b(inputs)
    masks = Wh.pop("mask_pp")
    W = din_w("s7b", {k: Wh[k] for k in
                      ("wout", "esel8", "eselT", "sel2", "sel2T")})
    W["mask_pp"] = c.din("s7_mask_pp", masks, percore=True)
    q_in = c.din("s7_q", extra["q"], dt=BF16, percore=True)
    kv_all = c.din("s7_kv", extra["kv"], dt=BF16)
    emit_s7_attn(c, W, q_in, kv_all, c.kb.buf("kv_all"))
    emit_l1_ffn(c, din_w("s8", prep_l1_ffn(inputs)), it=1)
    store_x(c, x_out)
    stats = c.finish()
    res = c.run()
    return np.stack([res.results[i]["x_out"] for i in range(B)]), stats


def kernel(**inputs):
    x = inputs["L0_attn_a_x"]
    x_fm = np.ascontiguousarray(x.transpose(0, 2, 1)).astype(np.float32)
    if FUSED2:
        x_fm, _, qkv = run_fused_A(x_fm, inputs)
        extra = {
            "q": [np.ascontiguousarray(qkv[b][0:1024]) for b in range(B)],
            "kv": np.ascontiguousarray(
                np.concatenate([qkv[b][1024:3072] for b in range(B)],
                               axis=0)),
        }
        x_fm, _ = run_fused_B(x_fm, inputs, extra)
        return np.ascontiguousarray(x_fm.transpose(0, 2, 1))
    extra = None
    for subs in LAUNCH_PLAN:
        x_fm, _, ex = run_launch(subs, x_fm, inputs, extra=extra)
        if "qkv" in ex:
            qkv = ex["qkv"]
            extra = {
                "q": [np.ascontiguousarray(qkv[b][0:1024]) for b in range(B)],
                "kv": np.ascontiguousarray(
                    np.concatenate([qkv[b][1024:3072] for b in range(B)],
                                   axis=0)),
            }
    return np.ascontiguousarray(x_fm.transpose(0, 2, 1))


# version 08 (final)
# speedup vs baseline: 2.0178x; 1.0957x over previous
import numpy as np
import ml_dtypes
import concourse.bass as bass
import concourse.mybir as mybir
from concourse.bass_utils import run_bass_kernel_spmd

F32 = mybir.dt.float32
BF16 = mybir.dt.bfloat16
AF = mybir.ActivationFunctionType
ALU = mybir.AluOpType
AX = mybir.AxisListType
BF = ml_dtypes.bfloat16

ENGINES = ("pe", "act", "dve", "pool", "sync")
SAME_ENGINE_COMPUTE_EDGE = True


class Op:
    __slots__ = ("eng", "kind", "fn", "deps", "seq", "needs_inc", "sem_val",
                 "tag")

    def __init__(self, eng, kind, fn, tag):
        self.eng = eng
        self.kind = kind
        self.fn = fn
        self.deps = []
        self.seq = -1
        self.needs_inc = False
        self.sem_val = None
        self.tag = tag


class Buf:
    __slots__ = ("name", "last_write", "readers")

    def __init__(self, name):
        self.name = name
        self.last_write = None
        self.readers = []


class KB:
    def __init__(self, nc):
        self.nc = nc
        self.ops = {e: [] for e in ENGINES}
        self.final_waits = []

    def buf(self, name):
        return Buf(name)

    def _keep_dep(self, dep, op):
        if dep.kind == "l" or op.kind == "l":
            return True
        if dep.eng != op.eng:
            return True
        if dep.kind == "d" and op.kind == "d":
            return False
        if dep.kind == "c" and op.kind == "c":
            if dep.eng == "pe":
                return False
            return SAME_ENGINE_COMPUTE_EDGE
        return True

    def emit(self, eng, fn, reads=(), writes=(), dma=False, tag="",
             extra_deps=(), coll=False, hard_deps=()):
        op = Op(eng, "l" if coll else ("d" if dma else "c"), fn, tag)
        deps = set()
        for b in reads:
            if b.last_write is not None:
                deps.add(b.last_write)
        for b in writes:
            if b.last_write is not None:
                deps.add(b.last_write)
            deps.update(b.readers)
        deps.update(extra_deps)
        deps.discard(op)
        op.deps = [d for d in deps if self._keep_dep(d, op)]
        for d in hard_deps:
            if d is not op and d not in op.deps:
                op.deps.append(d)
        for b in reads:
            if op not in b.readers:
                b.readers.append(op)
        for b in writes:
            b.last_write = op
            b.readers = []
        self.ops[eng].append(op)
        op.seq = len(self.ops[eng]) - 1
        return op

    def require_done(self, op):
        self.final_waits.append(op)

    def finalize_and_emit(self):
        nc = self.nc
        for e in ENGINES:
            for op in self.ops[e]:
                for d in op.deps:
                    d.needs_inc = True
        for op in self.final_waits:
            op.needs_inc = True
        LIMIT = 56000
        n_epochs = {}
        n_coll = 0
        for e in ENGINES:
            for k, step in (("c", 1), ("d", 16)):
                ep = 0
                v = 0
                for op in self.ops[e]:
                    if op.kind == "l" and op.needs_inc and op.sem_val is None:
                        op.sem_val = ("L", n_coll, 1)
                        n_coll += 1
                        continue
                    if op.kind != k or not op.needs_inc:
                        continue
                    if v + step > LIMIT:
                        ep += 1
                        v = 0
                    v += step
                    op.sem_val = (ep, v)
                n_epochs[(e, k)] = ep + 1 if v > 0 else 0
        sems = {}
        for (e, k), n in n_epochs.items():
            for ep in range(n):
                sems[(e, k, ep)] = nc.alloc_semaphore(f"s_{e}_{k}{ep}")
        for i in range(n_coll):
            sems[("coll", i)] = nc.alloc_semaphore(f"s_coll{i}")
        highwater = {}

        def wait_dep(eng_name, engobj, dep):
            if dep.sem_val[0] == "L":
                key = ("coll", dep.sem_val[1])
                if not highwater.get((eng_name, key)):
                    engobj.wait_ge(sems[key], 1)
                    highwater[(eng_name, key)] = 1
                return
            ep, val = dep.sem_val
            key = (dep.eng, dep.kind, ep)
            hw = highwater.get((eng_name, key), 0)
            if val > hw:
                engobj.wait_ge(sems[key], val)
                highwater[(eng_name, key)] = val

        def emit_stream(eng_name, engobj):
            for op in self.ops[eng_name]:
                for dep in sorted(op.deps, key=lambda d: (d.eng, d.kind,
                                                          d.sem_val)):
                    wait_dep(eng_name, engobj, dep)
                inst = op.fn(engobj)
                if op.needs_inc:
                    if op.sem_val[0] == "L":
                        inst.then_inc(sems[("coll", op.sem_val[1])])
                    else:
                        ep, _ = op.sem_val
                        inst.then_inc(sems[(op.eng, op.kind, ep)],
                                      16 if op.kind == "d" else 1)
            if eng_name == "pool":
                for op in self.final_waits:
                    wait_dep(eng_name, engobj, op)

        with nc.Block() as blk:

            @blk.tensor
            def _(tensor):
                emit_stream("pe", tensor)

            @blk.scalar
            def _(scalar):
                emit_stream("act", scalar)

            @blk.vector
            def _(vector):
                emit_stream("dve", vector)

            @blk.gpsimd
            def _(gpsimd):
                emit_stream("pool", gpsimd)

            @blk.sync
            def _(sync):
                emit_stream("sync", sync)

        return {e: len(self.ops[e]) for e in ENGINES}


B, T, D = 8, 2048, 1024
NCH = 8
HEADS, DH = 16, 64
E_EXP, HFF = 8, 4096
LN_EPS = 1e-5


class C:
    def __init__(self, tag):
        self.nc = bass.Bass("TRN2", target_bir_lowering=False)
        self.kb = KB(self.nc)
        self.E = self.kb.emit
        self.inp = {}
        self.tag = tag
        nc = self.nc
        self.xc = [nc.alloc_sbuf_tensor(f"xc{i}", [128, 2048], F32)
                   for i in range(NCH)]
        self.xl = [nc.alloc_sbuf_tensor(f"xl{i}", [128, 2048], BF16)
                   for i in range(NCH)]
        self.sa = nc.alloc_sbuf_tensor("sa", [128, 16896], BF16)
        self.sb = nc.alloc_sbuf_tensor("sb", [128, 5632], BF16)
        self.ua = [nc.alloc_sbuf_tensor(f"ua{i}", [128, 2048], BF16)
                   for i in range(NCH)]
        self.sf = nc.alloc_sbuf_tensor("sf", [128, 2048], F32)
        self.ws = nc.alloc_sbuf_tensor("ws", [128, 8192], BF16)
        self.ct = nc.alloc_sbuf_tensor("ct", [128, 896], F32)
        self.ctb = nc.alloc_sbuf_tensor("ctb", [128, 2048], BF16)
        self.ps = nc.alloc_psum_tensor("ps", [128, 4096], F32)
        kb = self.kb
        self.xcB = [kb.buf(f"xc{i}") for i in range(NCH)]
        self.xlB = [kb.buf(f"xl{i}") for i in range(NCH)]
        self.saB = kb.buf("sa")
        self.sbB = kb.buf("sb")
        self.uaB = [kb.buf(f"ua{i}") for i in range(NCH)]
        self.sfB = kb.buf("sf")
        self.wsB = kb.buf("ws")
        self.wsHB = [kb.buf("wsh0"), kb.buf("wsh1")]
        self.sbQB = kb.buf("sb_q")
        self.sbKB = kb.buf("sb_k")
        self.sbPB = [kb.buf("sb_pt0"), kb.buf("sb_pt1")]
        self.wsScrB = [kb.buf("ws_scr0"), kb.buf("ws_scr1")]
        self.ctB_ = kb.buf("ct")
        self.ctbB = kb.buf("ctb")
        self.psB = [kb.buf(f"psb{i}") for i in range(8)]
        self._const_cols = {}
        self._n_const = 0
        self._extra_sb = {}
        self._init_consts()

    def _init_consts(self):
        self.E("pool", lambda e: e.memset(self.ct[:, 64:65], 1.0),
               writes=[self.ctB_], tag="ones_col")
        self.E("pool", lambda e: e.memset(self.ct[0:1, 66:194], 1.0),
               writes=[self.ctB_], tag="ones_row")
        self.E("pool", lambda e: e.memset(self.ct[32:33, 66:194], 1.0),
               writes=[self.ctB_], tag="ones_row32")
        self.E("pool", lambda e: e.memset(self.ct[64:65, 66:194], 1.0),
               writes=[self.ctB_], tag="ones_row64")
        self.E("pool", lambda e: e.memset(self.ctb[0:1, 0:128], 1.0),
               writes=[self.ctbB], tag="ones_row16")
        self.E("pool", lambda e: e.memset(self.ctb[:, 128:129], 1.0),
               writes=[self.ctbB], tag="ones_col16")
        self.ones_col = self.ct[:, 64:65]
        self.ones_row = self.ct[0:1, 66:194]
        self.ones_row32 = self.ct[32:33, 66:194]
        self.ones_row64 = self.ct[64:65, 66:194]
        self.ones_row16 = self.ctb[0:1, 0:128]
        self.ones_col16 = self.ctb[:, 128:129]

    def alloc(self, name, shape, dt):
        if name in self._extra_sb:
            return self._extra_sb[name]
        t = self.nc.alloc_sbuf_tensor(f"{name}_{self.tag}", list(shape), dt)
        b = self.kb.buf(name)
        self._extra_sb[name] = (t, b)
        return t, b

    def pr(self, p0, p1, c0, c1):
        return self.ps[p0:p1, c0:c1]

    def pb(self, c0, c1):
        return [self.psB[i] for i in range(c0 // 512, (c1 + 511) // 512)]

    def din(self, name, arr, dt=F32, percore=False):
        self.inp[name] = arr
        shape = list(arr[0].shape if percore else arr.shape)
        return self.nc.dram_tensor(name, shape, dt, kind="ExternalInput").ap()

    def dout(self, name, shape, dt=F32):
        return self.nc.dram_tensor(name, list(shape), dt,
                                   kind="ExternalOutput").ap()

    def dint(self, name, shape, dt):
        return self.nc.dram_tensor(name, list(shape), dt, kind="Internal").ap()

    def const(self, val):
        if val not in self._const_cols:
            j = self._n_const
            self._n_const += 1
            assert j < 64
            self.kb.emit("pool",
                         lambda e, j=j, v=val: e.memset(self.ct[:, j:j + 1], v),
                         writes=[self.ctB_], tag=f"const{val}")
            self._const_cols[val] = j
        j = self._const_cols[val]
        return self.ct[:, j:j + 1]

    def finish(self):
        for e in ("sync", "pool", "act", "dve"):
            for op in self.kb.ops[e]:
                if op.kind == "d":
                    self.kb.require_done(op)
        return self.kb.finalize_and_emit()

    def run(self, trace=False):
        in_maps = []
        for core in range(B):
            m = {}
            for k, v in self.inp.items():
                m[k] = v[core] if isinstance(v, list) else v
            in_maps.append(m)
        return run_bass_kernel_spmd(self.nc, in_maps, core_ids=list(range(B)),
                                    trace=trace)


def load_x(c, x_dram):
    for i in range(NCH):
        c.E("sync", lambda e, i=i: e.dma_start(
            c.xc[i][:], x_dram[i * 128:(i + 1) * 128, :]),
            writes=[c.xcB[i]], dma=True, tag=f"ldx{i}")


def store_x(c, y_dram):
    for i in range(NCH):
        c.E("sync", lambda e, i=i: e.dma_start(
            y_dram[i * 128:(i + 1) * 128, :], c.xc[i][:]),
            reads=[c.xcB[i]], dma=True, tag=f"stx{i}")


def emit_ln(c, src, srcB, out, outB, eps=LN_EPS, src16=False, gb=None):
    E = c.E
    gA = c.pb(0, 2048)
    gB = c.pb(2048, 4096)
    ocol = c.ones_col16 if src16 else c.ones_col
    ocolB = c.ctbB if src16 else c.ctB_
    for i in range(NCH):
        E("act", lambda e, i=i: e.square(c.sf[:], src[i][:]),
          reads=[srcB[i]], writes=[c.sfB], tag="ln_sq")
        for j in range(4):
            E("pe", lambda e, i=i, j=j: e.matmul(
                c.pr(0, 1, j * 512, (j + 1) * 512), ocol,
                src[i][:, j * 512:(j + 1) * 512],
                start=(i == 0), stop=(i == NCH - 1)),
              reads=[srcB[i], ocolB], writes=gA, tag="ln_s")
            E("pe", lambda e, i=i, j=j: e.matmul(
                c.pr(32, 33, j * 512, (j + 1) * 512), c.ones_col,
                c.sf[:, j * 512:(j + 1) * 512],
                start=(i == 0), stop=(i == NCH - 1)),
              reads=[c.sfB, c.ctB_], writes=gA, tag="ln_s2")
    r_mean = c.sf[0:1, :]
    r_a = c.ws[0:1, 0:4096].bitcast(F32)
    r_b = c.ws[0:1, 4096:8192].bitcast(F32)
    E("dve", lambda e: e.tensor_scalar(r_mean, c.pr(0, 1, 0, 2048),
                                       1.0 / D, None, ALU.mult),
      reads=gA, writes=[c.sfB], tag="ln_mean")
    E("dve", lambda e: e.tensor_scalar(r_a, c.pr(32, 33, 0, 2048),
                                       1.0 / D, None, ALU.mult),
      reads=gA, writes=[c.wsB], tag="ln_ms")
    E("dve", lambda e: e.scalar_tensor_tensor(
        r_b, r_mean, -1.0, r_mean, ALU.mult, ALU.mult),
      reads=[c.sfB], writes=[c.wsB], tag="ln_nm2")
    E("dve", lambda e: e.tensor_tensor(r_a, r_a, r_b, ALU.add),
      reads=[c.wsB], writes=[c.wsB], tag="ln_var")
    eps_col = c.const(eps)
    E("act", lambda e: e.activation(r_b, r_a, AF.Ln,
                                    bias=eps_col[0:1, :]),
      reads=[c.wsB, c.ctB_], writes=[c.wsB], tag="ln_ln")
    E("act", lambda e: e.activation(r_a, r_b, AF.Exp, scale=-0.5),
      reads=[c.wsB], writes=[c.wsB], tag="ln_rstd")
    E("dve", lambda e: e.scalar_tensor_tensor(
        r_b, r_mean, -1.0, r_a, ALU.mult, ALU.mult),
      reads=[c.sfB, c.wsB], writes=[c.wsB], tag="ln_b")
    for j in range(4):
        E("pe", lambda e, j=j: e.matmul(
            c.pr(0, 128, j * 512, (j + 1) * 512), c.ones_row,
            c.ws[0:1, j * 1024:j * 1024 + 1024].bitcast(F32),
            start=True, stop=True),
          reads=[c.wsB, c.ctB_], writes=gA, tag="ln_bcA")
        E("pe", lambda e, j=j: e.matmul(
            c.pr(0, 128, 2048 + j * 512, 2048 + (j + 1) * 512), c.ones_row,
            c.ws[0:1, 4096 + j * 1024:4096 + j * 1024 + 1024].bitcast(F32),
            start=True, stop=True),
          reads=[c.wsB, c.ctB_], writes=gB, tag="ln_bcB")
    for i in range(NCH):
        E("dve", lambda e, i=i: e.tensor_tensor(
            c.sf[:], src[i][:], c.pr(0, 128, 0, 2048), ALU.mult),
          reads=[srcB[i]] + gA, writes=[c.sfB], tag="ln_n1")
        if gb is None:
            E("dve", lambda e, i=i: e.tensor_tensor(
                out[i][:], c.sf[:], c.pr(0, 128, 2048, 4096), ALU.add),
              reads=[c.sfB] + gB, writes=[outB[i]], tag="ln_n2")
        else:
            gt, gtB, go, bo_ = gb
            E("dve", lambda e, i=i: e.tensor_tensor(
                c.sf[:], c.sf[:], c.pr(0, 128, 2048, 4096), ALU.add),
              reads=[c.sfB] + gB, writes=[c.sfB], tag="ln_n2")
            E("act", lambda e, i=i, gt=gt, go=go: e.activation(
                c.sf[:], c.sf[:], AF.Copy, scale=gt[:, go + i:go + i + 1]),
              reads=[c.sfB, gtB], writes=[c.sfB], tag="ln_g")
            E("dve", lambda e, i=i, gt=gt, bo_=bo_: e.tensor_scalar(
                out[i][:], c.sf[:], gt[:, bo_ + i:bo_ + i + 1], None,
                ALU.add),
              reads=[c.sfB, gtB], writes=[outB[i]], tag="ln_b2")


def stage_cast(c, dram_ap, r0, rows, c0, cols, dest, destB, doff, tag,
               sf_off=0):
    E = c.E
    E("sync", lambda e: e.dma_start(
        c.sf[0:rows, sf_off:sf_off + cols],
        dram_ap[r0:r0 + rows, c0:c0 + cols]),
      writes=[c.sfB], dma=True, tag=f"{tag}_ld")
    E("pool", lambda e: e.tensor_copy(dest[0:rows, doff:doff + cols],
                                      c.sf[0:rows, sf_off:sf_off + cols]),
      reads=[c.sfB], writes=[destB], tag=f"{tag}_cast")


def emit_proj(c, wT_dram, src, srcB, nout, consume, tag, nin=NCH,
              wq_width=512):
    E = c.E
    n_waves = (nout + 3) // 4
    for w in range(n_waves):
        ocs = list(range(w * 4, min(w * 4 + 4, nout)))
        wwidth = len(ocs) * 128
        half = (w % 2) * 4096
        for ic in range(nin):
            stage_cast(c, wT_dram, ic * 128, 128, ocs[0] * 128, wwidth,
                       c.ws, c.wsB, half + ic * 512, f"{tag}w{w}i{ic}")
        for tq in range(4):
            pbase = 0 if tq % 2 == 0 else 2048
            guards = c.pb(pbase, pbase + wwidth)
            for ic in range(nin):
                for k, oc in enumerate(ocs):
                    E("pe", lambda e, ic=ic, k=k, tq=tq, half=half,
                        pbase=pbase: e.matmul(
                          c.pr(0, 128, pbase + k * 512, pbase + k * 512 + 512),
                          c.ws[:, half + ic * 512 + k * 128:
                               half + ic * 512 + k * 128 + 128],
                          src[ic][:, tq * 512:(tq + 1) * 512],
                          start=(ic == 0), stop=(ic == nin - 1)),
                      reads=[c.wsB, srcB[ic]], writes=guards,
                      tag=f"{tag}_mm")
            for k, oc in enumerate(ocs):
                consume(oc, tq * 512, 512, pbase + k * 512,
                        c.pb(pbase + k * 512, pbase + k * 512 + 512))


def emit_l1_ffn(c, W, it=0):
    E = c.E
    tg = f"f{it}"
    emit_ln(c, c.xc, c.xcB, c.xl, c.xlB)
    mu, muB = c.alloc("mu6", [128, 16], F32)
    om, omB = c.alloc("om6", [128, 16], F32)
    E("sync", lambda e: e.dma_start(mu[:], W["mu"]), writes=[muB], dma=True)
    E("dve", lambda e: e.tensor_scalar(om[:], mu[:], -1.0, 1.0, ALU.mult,
                                       ALU.add),
      reads=[muB], writes=[omB], tag="om")
    for i in range(NCH):
        E("dve", lambda e, i=i: e.tensor_scalar(
            c.ua[i][:], c.xl[i][:], mu[:, i:i + 1], None, ALU.mult),
          reads=[c.xlB[i], muB], writes=[c.uaB[i]], tag="xr1")
        E("dve", lambda e, i=i: e.scalar_tensor_tensor(
            c.ua[i][:, 1:2048], c.xl[i][:, 0:2047], om[:, i:i + 1],
            c.ua[i][:, 1:2048], ALU.mult, ALU.add),
          reads=[c.xlB[i], omB, c.uaB[i]], writes=[c.uaB[i]], tag="xr2")
        E("dve", lambda e, i=i: e.tensor_scalar(
            c.sa[:, i * 2048:(i + 1) * 2048], c.xl[i][:],
            mu[:, 8 + i:9 + i], None, ALU.mult),
          reads=[c.xlB[i], muB, c.saB], writes=[c.saB], tag="xk1")
        E("dve", lambda e, i=i: e.scalar_tensor_tensor(
            c.sa[:, i * 2048 + 1:(i + 1) * 2048],
            c.xl[i][:, 0:2047], om[:, 8 + i:9 + i],
            c.sa[:, i * 2048 + 1:(i + 1) * 2048], ALU.mult, ALU.add),
          reads=[c.xlB[i], omB, c.saB], writes=[c.saB], tag="xk2")
    sa_chunks = [c.sa[:, i * 2048:(i + 1) * 2048] for i in range(NCH)]
    saBl = [c.saB] * NCH

    def consume_k(oc, tq0, width, c0, guards):
        E("act", lambda e: e.activation(c.xl[oc][:, tq0:tq0 + width],
                                        c.pr(0, 128, c0, c0 + width), AF.Relu),
          reads=guards, writes=[c.xlB[oc]], tag="k_relu")

    emit_proj(c, W["wkT"], sa_chunks, saBl, NCH, consume_k, f"{tg}k")
    for i in range(NCH):
        E("pool", lambda e, i=i: e.tensor_tensor(c.xl[i][:], c.xl[i][:],
                                                 c.xl[i][:], ALU.mult),
          reads=[c.xlB[i]], writes=[c.xlB[i]], tag="k_sq")
    ua_l = [c.ua[i][:] for i in range(NCH)]

    def consume_r(oc, tq0, width, c0, guards):
        E("act", lambda e: e.activation(
            c.sa[:, oc * 2048 + tq0:oc * 2048 + tq0 + width],
            c.pr(0, 128, c0, c0 + width), AF.Sigmoid),
          reads=guards, writes=[c.saB], tag="r_sig")

    emit_proj(c, W["wrT"], ua_l, c.uaB, NCH, consume_r, f"{tg}r")
    xl_l = [c.xl[i][:] for i in range(NCH)]

    def consume_v(oc, tq0, width, c0, guards):
        E("dve", lambda e: e.tensor_tensor(
            c.sb[:, 0:width], c.pr(0, 128, c0, c0 + width),
            c.sa[:, oc * 2048 + tq0:oc * 2048 + tq0 + width], ALU.mult),
          reads=guards + [c.saB], writes=[c.sbB], tag="v_gate")
        E("dve", lambda e: e.tensor_tensor(
            c.xc[oc][:, tq0:tq0 + width], c.xc[oc][:, tq0:tq0 + width],
            c.sb[:, 0:width], ALU.add),
          reads=[c.sbB, c.xcB[oc]], writes=[c.xcB[oc]], tag="v_res")

    emit_proj(c, W["wvT"], xl_l, c.xlB, NCH, consume_v, f"{tg}v")


def prep_l1_ffn(inputs):
    f32 = np.float32
    mu = np.stack([inputs["L1_ffn_mu_r"], inputs["L1_ffn_mu_k"]])
    mu_pp = np.ascontiguousarray(
        mu.reshape(2, 8, 128).transpose(2, 0, 1).reshape(128, 16), f32)
    return {
        "wrT": np.ascontiguousarray(inputs["L1_ffn_W_r"].T, f32),
        "wkT": np.ascontiguousarray(inputs["L1_ffn_W_k"].T, f32),
        "wvT": np.ascontiguousarray(inputs["L1_ffn_W_v"].T, f32),
        "mu": mu_pp,
    }


def emit_l0_attn_b(c, W):
    E = c.E
    emit_ln(c, c.xc, c.xcB, c.xl, c.xlB)
    E("sync", lambda e: e.dma_start(c.ctb[:, 512:640], W["esel"]),
      writes=[c.ctbB], dma=True, tag="esel")
    E("sync", lambda e: e.dma_start(c.ctb[0:16, 640:1664], W["eselT"]),
      writes=[c.ctbB], dma=True, tag="eselT")
    bpr, bprB = c.alloc("bpr", [128, 8], F32)
    E("sync", lambda e: e.dma_start(bpr[:], W["bproj_pp"]), writes=[bprB],
      dma=True)
    STG = 2048
    TMP = 2560
    TMP2 = 3072
    KP = 3584
    def proj_chunk(oc, tq, pcol):
        for ic in range(NCH):
            off = (ic % 4) * 512
            E("pe", lambda e, ic=ic, oc=oc, tq=tq, pcol=pcol: e.matmul(
                c.pr(0, 128, pcol, pcol + 512),
                c.ws[:, (oc % 8) * 1024 + ic * 128:(oc % 8) * 1024 + ic * 128 + 128],
                c.xl[ic][:, tq * 512:(tq + 1) * 512],
                start=(ic == 0), stop=(ic == NCH - 1)),
              reads=[c.wsB, c.xlB[ic]], writes=c.pb(pcol, pcol + 512),
              tag="s3_mm")

    def stage_w(oc):
        for ic in range(NCH):
            stage_cast(c, W["wqkv"], ic * 128, 128, oc * 128, 128,
                       c.ws, c.wsB, (oc % 8) * 1024 + ic * 128,
                       f"s3w{oc}_{ic}", sf_off=(ic % 4) * 512)

    guards_vs = c.pb(2048, 4096)
    for j in range(NCH):
        oc = 16 + j
        stage_w(oc)
        for tq in range(4):
            pcol = (tq % 2) * 512
            proj_chunk(oc, tq, pcol)
            gv = c.pb(pcol, pcol + 512)
            E("act", lambda e, tq=tq, pcol=pcol: e.copy(
                c.sb[:, STG:STG + 512], c.pr(0, 128, pcol, pcol + 512)),
              reads=gv, writes=[c.sbB], tag="s3_vev")
            E("pe", lambda e, j=j, tq=tq: e.matmul(
                c.pr(0, 16, 2048 + tq * 512, 2048 + (tq + 1) * 512),
                c.ctb[:, 512 + j * 16:512 + (j + 1) * 16],
                c.sb[:, STG:STG + 512],
                start=(j == 0), stop=(j == NCH - 1)),
              reads=[c.ctbB, c.sbB], writes=guards_vs, tag="s3_vsum")
    E("act", lambda e: e.copy(c.sb[0:16, 0:2048], c.pr(0, 16, 2048, 4096)),
      reads=guards_vs, writes=[c.sbB], tag="s3_vsum_ev")

    def elu1(pcol, dst_off):
        g = c.pb(pcol, pcol + 512)
        E("dve", lambda e: e.tensor_scalar(c.sb[:, TMP:TMP + 512],
                                           c.pr(0, 128, pcol, pcol + 512),
                                           0.0, None, ALU.min),
          reads=g, writes=[c.sbB], tag="s3_min")
        E("act", lambda e: e.activation(c.sb[:, TMP2:TMP2 + 512],
                                        c.sb[:, TMP:TMP + 512], AF.Exp),
          reads=[c.sbB], writes=[c.sbB], tag="s3_exp")
        E("act", lambda e: e.activation(c.sb[:, TMP:TMP + 512],
                                        c.pr(0, 128, pcol, pcol + 512),
                                        AF.Relu),
          reads=g + [c.sbB], writes=[c.sbB], tag="s3_relu")
        E("dve", lambda e: e.tensor_tensor(c.sb[:, dst_off:dst_off + 512],
                                           c.sb[:, TMP:TMP + 512],
                                           c.sb[:, TMP2:TMP2 + 512], ALU.add),
          reads=[c.sbB], writes=[c.sbB], tag="s3_add1")

    for j in range(NCH):
        oc = 8 + j
        stage_w(oc)
        for tq in range(4):
            pcol = (tq % 2) * 512
            proj_chunk(oc, tq, pcol)
            elu1(pcol, KP)
            E("pe", lambda e, j=j, tq=tq: e.matmul(
                c.pr(0, 128, 2560, 3072),
                c.ctb[0:16, 640 + j * 128:640 + (j + 1) * 128],
                c.sb[0:16, tq * 512:(tq + 1) * 512],
                start=True, stop=True),
              reads=[c.ctbB, c.sbB], writes=c.pb(2560, 3072), tag="s3_vbc")
            E("dve", lambda e, tq=tq: e.scalar_tensor_tensor(
                c.sb[:, TMP:TMP + 512], c.sb[:, KP:KP + 512], 1.0,
                c.pr(0, 128, 2560, 3072), ALU.mult, ALU.mult,
                accum_out=c.ct[:, 200 + tq:201 + tq]),
              reads=[c.sbB] + c.pb(2560, 3072), writes=[c.sbB, c.ctB_],
              tag="s3_kv")
        E("dve", lambda e, j=j: e.reduce_sum(
            c.ct[:, 240 + j:241 + j], c.ct[:, 200:204], axis=AX.X),
          reads=[c.ctB_], writes=[c.ctB_], tag="s3_kvsum")
    for j in range(NCH):
        oc = j
        stage_w(oc)
        for tq in range(4):
            pcol = (tq % 2) * 512
            proj_chunk(oc, tq, pcol)
            elu1(pcol, KP)
            E("pe", lambda e, j=j, tq=tq: e.matmul(
                c.pr(0, 16, 2048 + tq * 512, 2048 + (tq + 1) * 512),
                c.ctb[:, 512 + j * 16:512 + (j + 1) * 16],
                c.sb[:, KP:KP + 512],
                start=(j == 0), stop=(j == NCH - 1)),
              reads=[c.ctbB, c.sbB], writes=guards_vs, tag="s3_qsum")
    E("act", lambda e: e.copy(c.sb[0:16, 0:2048], c.pr(0, 16, 2048, 4096)),
      reads=guards_vs, writes=[c.sbB], tag="s3_qsum_ev")
    for j in range(NCH):
        for tq in range(4):
            E("pe", lambda e, j=j, tq=tq: e.matmul(
                c.pr(0, 128, 2560, 3072),
                c.ctb[0:16, 640 + j * 128:640 + (j + 1) * 128],
                c.sb[0:16, tq * 512:(tq + 1) * 512],
                start=True, stop=True),
              reads=[c.ctbB, c.sbB], writes=c.pb(2560, 3072), tag="s3_qbc")
            E("act", lambda e, j=j, tq=tq: e.activation(
                c.ua[j][:, tq * 512:(tq + 1) * 512],
                c.pr(0, 128, 2560, 3072), AF.Copy,
                scale=c.ct[:, 240 + j:241 + j]),
              reads=c.pb(2560, 3072) + [c.ctB_], writes=[c.uaB[j]],
              tag="s3_out")
    ua_l = [c.ua[i][:] for i in range(NCH)]

    def consume_o(oc, tq0, width, c0, guards):
        E("dve", lambda e: e.scalar_tensor_tensor(
            c.xc[oc][:, tq0:tq0 + width], c.pr(0, 128, c0, c0 + width),
            bpr[:, oc:oc + 1], c.xc[oc][:, tq0:tq0 + width],
            ALU.add, ALU.add),
          reads=guards + [bprB, c.xcB[oc]], writes=[c.xcB[oc]], tag="s3_res")

    emit_proj(c, W["wproj"], ua_l, c.uaB, NCH, consume_o, "s3o")


def prep_l0_attn_b(inputs):
    f32 = np.float32
    esel = np.zeros((8, 128, 16), f32)
    eselT = np.zeros((8, 16, 128), f32)
    for j in range(8):
        for p in range(128):
            h = 2 * j + (p // 64)
            esel[j, p, h] = 1.0
            eselT[j, h, p] = 1.0
    return {
        "wqkv": np.ascontiguousarray(inputs["L0_attn_b_W_qkv"], f32),
        "wproj": np.ascontiguousarray(inputs["L0_attn_b_W_proj"], f32),
        "bproj_pp": np.ascontiguousarray(
            inputs["L0_attn_b_b_proj"].reshape(8, 128).T, f32),
        "esel": np.ascontiguousarray(
            esel.transpose(1, 0, 2).reshape(128, 128)).astype(BF),
        "eselT": np.ascontiguousarray(
            eselT.transpose(1, 0, 2).reshape(16, 1024)).astype(BF),
    }


def emit_mha(c, W, tag, masked=False):
    E = c.E
    CSH = 12.0
    csh_col = c.const(-CSH)
    c60_col = c.const(-60.0)
    TRI = W.get("tri_off", 2048 - 128)
    KKD, KKC = 536, 552
    for ic in range(NCH):
        stage_cast(c, W["wv2"], ic * 128, 128, 0, 1024, c.ws, c.wsB,
                   ic * 1024, f"{tag}wv{ic}", sf_off=(ic % 2) * 1024)
    if W.get("bv_row") is not None:
        E("sync", lambda e: e.dma_start(c.sf[0:1, 0:1024], W["bv_row"]),
          writes=[c.sfB], dma=True, tag="bv_ld")
        E("pool", lambda e: e.tensor_copy(c.sb[0:1, 0:1024],
                                          c.sf[0:1, 0:1024]),
          reads=[c.sfB], writes=[c.sbB], tag="bv_cast")
    for kt in range(16):
        pc = (kt % 2) * 1024
        g = c.pb(pc, pc + 1024)
        for ic in range(NCH):
            for half in range(2):
                E("pe", lambda e, ic=ic, half=half, kt=kt, pc=pc: e.matmul(
                    c.pr(0, 128, pc + half * 512, pc + half * 512 + 512),
                    c.xl[ic][:, kt * 128:(kt + 1) * 128],
                    c.ws[:, ic * 1024 + half * 512:
                         ic * 1024 + half * 512 + 512],
                    start=(ic == 0), stop=(ic == NCH - 1 and
                                           W.get("bv_row") is None)),
                  reads=[c.xlB[ic], c.wsB], writes=g, tag=f"{tag}_vmm")
        if W.get("bv_row") is not None:
            for half in range(2):
                E("pe", lambda e, half=half, pc=pc: e.matmul(
                    c.pr(0, 128, pc + half * 512, pc + half * 512 + 512),
                    c.ones_row16,
                    c.sb[0:1, half * 512:half * 512 + 512],
                    start=False, stop=True),
                  reads=[c.sbB], writes=g, tag=f"{tag}_vbias")
        base = kt * 1056
        E("pool", lambda e, base=base: e.memset(
            c.sa[:, base:base + 1056].rearrange(
                "p (h x) -> p h x", x=66)[:, :, 64:66], 1.0),
          writes=[c.saB], tag=f"{tag}_vones")
        E("act", lambda e, base=base, pc=pc: e.activation(
            c.sa[:, base:base + 1056].rearrange(
                "p (h x) -> p h x", x=66)[:, :, 0:64],
            c.pr(0, 128, pc, pc + 1024).rearrange(
                "p (h x) -> p h x", x=64),
            AF.Copy),
          reads=g + [c.saB], writes=[c.saB], tag=f"{tag}_vev")
    QO, KO, PT = 0, 2048, 4096
    E("pool", lambda e: e.engine_nop(), reads=[],
      writes=[c.sbB, c.wsB, c.sbQB, c.sbKB] + c.sbPB + c.wsScrB,
      tag=f"{tag}_open")
    scr = [c.ws[:, 4096 + p * 2048:4096 + (p + 1) * 2048].bitcast(F32)
           for p in range(2)]
    for gi in range(8):
        for ic in range(NCH):
            stage_cast(c, W["wq2"], ic * 128, 128, gi * 128, 128,
                       c.ws, c.wsB, ic * 128, f"{tag}q{gi}_{ic}",
                       sf_off=(ic % 4) * 512)
            if not W.get("k_in_ua"):
                stage_cast(c, W["wk2"], ic * 128, 128, gi * 128, 128,
                           c.ws, c.wsB, 1024 + ic * 128, f"{tag}k{gi}_{ic}",
                           sf_off=(ic % 4) * 512)
        for which, wsoff, dst, bias in (
                ("q", 0, QO, W.get("bq_pp")),
                ("k", 1024, KO, W.get("bk_pp"))):
            if which == "k" and W.get("k_in_ua"):
                continue
            for tq in range(4):
                pc = 2048 + (tq % 2) * 512
                g = c.pb(pc, pc + 512)
                for ic in range(NCH):
                    E("pe", lambda e, ic=ic, tq=tq, pc=pc, wsoff=wsoff:
                        e.matmul(
                          c.pr(0, 128, pc, pc + 512),
                          c.ws[:, wsoff + ic * 128:wsoff + ic * 128 + 128],
                          c.xl[ic][:, tq * 512:(tq + 1) * 512],
                          start=(ic == 0), stop=(ic == NCH - 1)),
                      reads=[c.wsB, c.xlB[ic]], writes=g,
                      tag=f"{tag}_{which}mm")
                dstB = c.sbQB if which == "q" else c.sbKB
                if bias is not None:
                    E("dve", lambda e, tq=tq, pc=pc, dst=dst, gi=gi, b=bias:
                        e.tensor_scalar(
                          c.sb[:, dst + tq * 512:dst + (tq + 1) * 512],
                          c.pr(0, 128, pc, pc + 512),
                          b[0][:, gi:gi + 1], None, ALU.add),
                      reads=g + [bias[1]], writes=[dstB],
                      tag=f"{tag}_{which}b")
                else:
                    E("act", lambda e, tq=tq, pc=pc, dst=dst: e.copy(
                        c.sb[:, dst + tq * 512:dst + (tq + 1) * 512],
                        c.pr(0, 128, pc, pc + 512)),
                      reads=g, writes=[dstB], tag=f"{tag}_{which}ev")
        for hh in range(2):
            h = gi * 2 + hh
            kb = hh * 64
            pending_av = [None]
            bb = [0]

            def flush_av(c=c):
                if pending_av[0] is not None:
                    fn, reads_, writes_ = pending_av[0]
                    E("pe", fn, reads=reads_, writes=writes_,
                      tag=f"{tag}_av")
                    pending_av[0] = None

            for kt in range(16):
                q0 = kt * 128
                blocks = []
                nb = min((q0 // 512 + 1) * 512, 2048)
                blocks.append((q0, nb))
                while nb < 2048:
                    blocks.append((nb, min(nb + 512, 2048)))
                    nb += 512
                for bi, (qa, qb) in enumerate(blocks):
                    pc = (bb[0] % 2) * 512
                    g = c.pb(pc, pc + 512)
                    po = PT + (bb[0] % 2) * 512
                    bb[0] += 1
                    if W.get("k_in_ua"):
                        lk = c.ua[gi][kb:kb + 64, kt * 128:(kt + 1) * 128]
                        lkB = [c.uaB[gi]]
                    else:
                        lk = c.sb[kb:kb + 64,
                                  KO + kt * 128:KO + (kt + 1) * 128]
                        lkB = [c.sbKB]
                    E("pe", lambda e, lk=lk, qa=qa, qb=qb, pc=pc, kb=kb:
                        e.matmul(c.pr(0, 128, pc, pc + qb - qa), lk,
                                 c.sb[kb:kb + 64, QO + qa:QO + qb],
                                 start=True, stop=True),
                      reads=lkB + [c.sbQB], writes=g, tag=f"{tag}_smm")
                    par = (bb[0] - 1) % 2
                    ptB = c.sbPB[par]
                    scrT = scr[par]
                    scrB = c.wsScrB[par]
                    if not masked:
                        E("act", lambda e, pc=pc, po=po, qa=qa, qb=qb:
                            e.activation(c.sb[:, po:po + qb - qa],
                                         c.pr(0, 128, pc, pc + qb - qa),
                                         AF.Exp, bias=csh_col,
                                         scale=0.125),
                          reads=g + [c.ctB_], writes=[ptB],
                          tag=f"{tag}_exp")
                    else:
                        E("dve", lambda e, pc=pc, qa=qa, qb=qb, kt=kt,
                            scrT=scrT:
                            e.tensor_scalar(scrT[:, 0:qb - qa],
                                            c.pr(0, 128, pc, pc + qb - qa),
                                            c.ct[:, KKD + kt:KKD + kt + 1],
                                            c.ct[:, KKC + kt:KKC + kt + 1],
                                            ALU.mult, ALU.add),
                          reads=g + [c.ctB_], writes=[scrB],
                          tag=f"{tag}_t1")
                        E("pool", lambda e, qa=qa, qb=qb, scrT=scrT:
                            e.tensor_tensor(scrT[:, 512:512 + qb - qa],
                                            scrT[:, 0:qb - qa],
                                            c.sb[:, 2048 + qa:2048 + qb],
                                            ALU.mult),
                          reads=[scrB, c.sbKB], writes=[scrB],
                          tag=f"{tag}_t2")
                        E("act", lambda e, po=po, qa=qa, qb=qb, scrT=scrT:
                            e.activation(c.sb[:, po:po + qb - qa],
                                         scrT[:, 512:512 + qb - qa],
                                         AF.Exp, bias=c60_col),
                          reads=[scrB, c.ctB_], writes=[ptB],
                          tag=f"{tag}_exp")
                    if bi == 0:
                        E("pool", lambda e, po=po: e.tensor_tensor(
                            c.sb[:, po:po + 128], c.sb[:, po:po + 128],
                            c.ctb[:, TRI:TRI + 128], ALU.mult),
                          reads=[ptB, c.ctbB], writes=[ptB],
                          tag=f"{tag}_tri")
                    flush_av()
                    pending_av[0] = (
                        (lambda e, kt=kt, h=h, po=po, qa=qa, qb=qb:
                         e.matmul(
                             c.pr(0, 65, 2048 + qa, 2048 + qb),
                             c.sa[:, kt * 1056 + h * 66:
                                  kt * 1056 + h * 66 + 65],
                             c.sb[:, po:po + qb - qa],
                             start=(kt == 0),
                             stop=(kt == min(15, (qb - 1) // 128)))),
                        [c.saB, ptB],
                        c.pb(2048 + qa, 2048 + qb))
            flush_av()
            gctx = c.pb(2048, 4096)
            E("dve", lambda e: e.reciprocal(c.sf[0:1, :],
                                            c.pr(64, 65, 2048, 4096)),
              reads=gctx, writes=[c.sfB], tag=f"{tag}_rcp")
            for j in range(4):
                E("pe", lambda e, j=j: e.matmul(
                    c.pr(0, 64, j * 512, (j + 1) * 512),
                    c.ones_row[0:1, 0:64],
                    c.sf[0:1, j * 512:(j + 1) * 512], start=True, stop=True),
                  reads=[c.sfB, c.ctB_],
                  writes=c.pb(j * 512, (j + 1) * 512), tag=f"{tag}_rbc")
            E("act", lambda e: e.copy(c.sf[0:64, :], c.pr(0, 64, 0, 2048)),
              reads=c.pb(0, 2048), writes=[c.sfB], tag=f"{tag}_rbe")
            E("dve", lambda e, gi=gi, hh=hh: e.tensor_tensor(
                c.ua[gi][hh * 64:(hh + 1) * 64, :],
                c.pr(0, 64, 2048, 4096), c.sf[0:64, :], ALU.mult),
              reads=gctx + [c.sfB], writes=[c.uaB[gi]], tag=f"{tag}_ctx")
    E("pool", lambda e: e.engine_nop(), reads=[],
      writes=[c.sbB, c.wsB, c.sbQB, c.sbKB] + c.sbPB + c.wsScrB,
      tag=f"{tag}_close")
    ua_l = [c.ua[i][:] for i in range(NCH)]
    bo = W.get("bo_pp")

    def consume_o(oc, tq0, width, c0, guards):
        if bo is not None:
            E("dve", lambda e: e.scalar_tensor_tensor(
                c.xc[oc][:, tq0:tq0 + width], c.pr(0, 128, c0, c0 + width),
                bo[0][:, oc:oc + 1], c.xc[oc][:, tq0:tq0 + width],
                ALU.add, ALU.add),
              reads=guards + [bo[1], c.xcB[oc]], writes=[c.xcB[oc]],
              tag=f"{tag}_res")
        else:
            E("dve", lambda e: e.tensor_tensor(
                c.xc[oc][:, tq0:tq0 + width],
                c.pr(0, 128, c0, c0 + width),
                c.xc[oc][:, tq0:tq0 + width], ALU.add),
              reads=guards + [c.xcB[oc]], writes=[c.xcB[oc]],
              tag=f"{tag}_res")

    emit_proj(c, W["wo"], ua_l, c.uaB, NCH, consume_o, f"{tag}o")


def emit_l0_attn_a(c, W):
    emit_ln(c, c.xc, c.xcB, c.xl, c.xlB)
    bq, bqB = c.alloc("bq", [128, 8], F32)
    bk, bkB = c.alloc("bk", [128, 8], F32)
    bo, boB = c.alloc("bo", [128, 8], F32)
    E = c.E
    E("sync", lambda e: e.dma_start(bq[:], W["bq_pp"]), writes=[bqB], dma=True)
    E("sync", lambda e: e.dma_start(bk[:], W["bk_pp"]), writes=[bkB], dma=True)
    E("sync", lambda e: e.dma_start(bo[:], W["bo_pp"]), writes=[boB], dma=True)
    E("sync", lambda e: e.dma_start(c.ctb[:, 2048 - 128:2048], W["tri"]),
      writes=[c.ctbB], dma=True, tag="tri")
    W2 = dict(W)
    W2["bq_pp"] = (bq, bqB)
    W2["bk_pp"] = (bk, bkB)
    W2["bo_pp"] = (bo, boB)
    emit_mha(c, W2, "s1", masked=False)


def prep_l0_attn_a(inputs):
    f32 = np.float32
    tri = np.triu(np.ones((128, 128), f32)).astype(BF)
    return {
        "wq2": np.ascontiguousarray(
            inputs["L0_attn_a_Wq"].transpose(1, 0, 2).reshape(1024, 1024),
            f32),
        "wk2": np.ascontiguousarray(
            inputs["L0_attn_a_Wk"].transpose(1, 0, 2).reshape(1024, 1024),
            f32),
        "wv2": np.ascontiguousarray(
            inputs["L0_attn_a_Wv"].transpose(1, 0, 2).reshape(1024, 1024),
            f32),
        "wo": np.ascontiguousarray(inputs["L0_attn_a_Wo"], f32),
        "bq_pp": np.ascontiguousarray(
            inputs["L0_attn_a_bq"].reshape(8, 128).T, f32),
        "bk_pp": np.ascontiguousarray(
            inputs["L0_attn_a_bk"].reshape(8, 128).T, f32),
        "bv_row": np.ascontiguousarray(
            inputs["L0_attn_a_bv"].reshape(1, 1024), f32),
        "bo_pp": np.ascontiguousarray(
            inputs["L0_attn_a_bo"].reshape(8, 128).T, f32),
        "tri": tri,
    }


def emit_l1_attn_a(c, W):
    E = c.E
    emit_ln(c, c.xc, c.xcB, c.xl, c.xlB)
    E("sync", lambda e: e.dma_start(c.ctb[:, 512:640], W["tri"]),
      writes=[c.ctbB], dma=True, tag="s5tri")
    bo, boB = c.alloc("s5bo", [128, 8], F32)
    E("sync", lambda e: e.dma_start(bo[:], W["bout_pp"]), writes=[boB],
      dma=True)
    br1, br1B = c.alloc("s5br1", [128, 8], F32)
    E("sync", lambda e: e.dma_start(br1[:], W["br1_pp"]), writes=[br1B],
      dma=True)
    br2, br2B = c.alloc("s5br2", [1, 8], F32)
    E("sync", lambda e: e.dma_start(br2[:], W["br2_pp"]), writes=[br2B],
      dma=True)
    wqkvT = W["wqkvT"]
    for gi in range(8):
        for ic in range(NCH):
            stage_cast(c, wqkvT, ic * 128, 128, 1024 + gi * 128, 128,
                       c.ws, c.wsB, ic * 128, f"s5K{gi}_{ic}",
                       sf_off=(ic % 4) * 512)
        for tq in range(4):
            pc = (tq % 2) * 512
            g = c.pb(pc, pc + 512)
            for ic in range(NCH):
                E("pe", lambda e, ic=ic, tq=tq, pc=pc: e.matmul(
                    c.pr(0, 128, pc, pc + 512),
                    c.ws[:, ic * 128:ic * 128 + 128],
                    c.xl[ic][:, tq * 512:(tq + 1) * 512],
                    start=(ic == 0), stop=(ic == NCH - 1)),
                  reads=[c.wsB, c.xlB[ic]], writes=g, tag="s5_kmm")
            E("act", lambda e, gi=gi, tq=tq, pc=pc: e.copy(
                c.ua[gi][:, tq * 512:(tq + 1) * 512],
                c.pr(0, 128, pc, pc + 512)),
              reads=g, writes=[c.uaB[gi]], tag="s5_kev")
        E("dve", lambda e, gi=gi: e.reduce_sum(
            c.ct[:, 264 + gi * 32:264 + (gi + 1) * 32],
            c.ua[gi][:].rearrange("p (b t) -> p b t", t=64), axis=AX.X),
          reads=[c.uaB[gi]], writes=[c.ctB_], tag="s5_blk")
    E("dve", lambda e: e.tensor_scalar(
        c.ct[:, 264:520], c.ct[:, 264:520], 1.0 / 64, None, ALU.mult),
      reads=[c.ctB_], writes=[c.ctB_], tag="s5_blkm")
    E("pool", lambda e: e.tensor_copy(c.ctb[:, 640:896], c.ct[:, 264:520]),
      reads=[c.ctB_], writes=[c.ctbB], tag="s5_blkc")
    for ic in range(NCH):
        stage_cast(c, W["wr1T"], ic * 128, 128, 0, 128,
                   c.ws, c.wsB, ic * 128, f"s5r1_{ic}",
                   sf_off=(ic % 4) * 512)
    stage_cast(c, W["wr2T8"], 0, 128, 0, 8, c.ws, c.wsB, 1024, "s5r2")
    gr = c.pb(1024, 1536)
    for ic in range(NCH):
        E("pe", lambda e, ic=ic: e.matmul(
            c.pr(0, 128, 1024, 1056),
            c.ws[:, ic * 128:ic * 128 + 128],
            c.ctb[:, 640 + ic * 32:640 + (ic + 1) * 32],
            start=(ic == 0), stop=(ic == NCH - 1)),
          reads=[c.wsB, c.ctbB], writes=gr, tag="s5_r1mm")
    E("dve", lambda e: e.tensor_scalar(
        c.sf[:, 0:32], c.pr(0, 128, 1024, 1056), br1[:, 0:1], None, ALU.add),
      reads=gr + [br1B], writes=[c.sfB], tag="s5_r1b")
    E("act", lambda e: e.activation(c.ctb[:, 896:928], c.sf[:, 0:32],
                                    AF.Relu),
      reads=[c.sfB], writes=[c.ctbB], tag="s5_r1relu")
    E("pe", lambda e: e.matmul(
        c.pr(0, 1, 1536, 1568), c.ws[:, 1024:1025], c.ctb[:, 896:928],
        start=True, stop=True),
      reads=[c.wsB, c.ctbB], writes=c.pb(1536, 2048), tag="s5_r2mm")
    E("dve", lambda e: e.tensor_scalar(
        c.sf[0:1, 0:32], c.pr(0, 1, 1536, 1568), br2[0:1, 0:1], None,
        ALU.add),
      reads=c.pb(1536, 2048) + [br2B], writes=[c.sfB], tag="s5_r2b")
    E("act", lambda e: e.sign(c.sf[0:1, 32:64], c.sf[0:1, 0:32]),
      reads=[c.sfB], writes=[c.sfB], tag="s5_sign")
    E("dve", lambda e: e.tensor_scalar(c.sf[0:1, 32:64], c.sf[0:1, 32:64],
                                       0.0, None, ALU.max),
      reads=[c.sfB], writes=[c.sfB], tag="s5_keep")
    cc_in = c.dint("s5cc_in", [1, 32], F32)
    cc_out = c.dint("s5cc_out", [8, 32], F32)
    ccinB = c.kb.buf("ccin")
    ccoutB = c.kb.buf("ccout")
    E("sync", lambda e: e.dma_start(cc_in, c.sf[0:1, 32:64]),
      reads=[c.sfB], writes=[ccinB], dma=True, tag="s5_ccw")
    E("pool", lambda e: e.collective_compute(
        "AllGather", ALU.bypass, [list(range(B))], [cc_in.opt()],
        [cc_out.opt()]),
      reads=[ccinB], writes=[ccoutB], coll=True, tag="s5_ag")
    E("sync", lambda e: e.dma_start(c.sf[0:1, 0:32], cc_out[0:1, :]),
      reads=[ccoutB], writes=[c.sfB], dma=True, tag="s5_ccr")
    k01c, k01cB = c.alloc("k01c", [32, 8], F32)
    k01b, k01bB = c.alloc("k01b", [32, 8], BF16)
    E("pool", lambda e: e.dma_start(k01c[0:32, 0:1], c.sf[0:1, 0:32],
                                    allow_slow_non_contiguous=True),
      reads=[c.sfB], writes=[k01cB], dma=True, tag="s5_k01t")
    E("pool", lambda e: e.tensor_copy(k01b[:, 0:1], k01c[:, 0:1]),
      reads=[k01cB], writes=[k01bB], tag="s5_k01c")
    E("sync", lambda e: e.dma_start(c.ws[0:32, 1024:3072], W["esel32"]),
      writes=[c.wsB], dma=True, tag="s5_e32")
    for j in range(4):
        E("pe", lambda e, j=j: e.matmul(
            c.pr(0, 1, j * 512, (j + 1) * 512), k01b[:, 0:1],
            c.ws[0:32, 1024 + j * 512:1024 + (j + 1) * 512],
            start=True, stop=True),
          reads=[k01bB, c.wsB], writes=c.pb(j * 512, (j + 1) * 512),
          tag="s5_kkrow")
    E("act", lambda e: e.copy(c.sf[0:1, :], c.pr(0, 1, 0, 2048)),
      reads=c.pb(0, 2048), writes=[c.sfB], tag="s5_kkev")
    for j in range(4):
        E("pe", lambda e, j=j: e.matmul(
            c.pr(0, 128, 2048 + j * 512, 2048 + (j + 1) * 512),
            c.ones_row, c.sf[0:1, j * 512:(j + 1) * 512],
            start=True, stop=True),
          reads=[c.sfB, c.ctB_], writes=c.pb(2048 + j * 512,
                                             2048 + (j + 1) * 512),
          tag="s5_kqbc")
    E("act", lambda e: e.copy(c.sb[:, 2048:4096], c.pr(0, 128, 2048, 4096)),
      reads=c.pb(2048, 4096), writes=[c.sbB], tag="s5_kqev")
    for kt in range(16):
        E("pool", lambda e, kt=kt: e.dma_start(
            c.ct[:, 520 + kt:521 + kt], c.sf[0:1, kt * 128:(kt + 1) * 128],
            allow_slow_non_contiguous=True),
          reads=[c.sfB], writes=[c.ctB_], dma=True, tag="s5_kkcol")
    E("dve", lambda e: e.tensor_scalar(c.ct[:, 536:552], c.ct[:, 520:536],
                                       1.0 / 32, None, ALU.mult),
      reads=[c.ctB_], writes=[c.ctB_], tag="s5_kkd")
    E("dve", lambda e: e.tensor_scalar(c.ct[:, 552:568], c.ct[:, 520:536],
                                       60.0, None, ALU.mult),
      reads=[c.ctB_], writes=[c.ctB_], tag="s5_kkc")
    W2 = dict(W)
    W2["wq2"] = wqkvT[0:1024, 0:1024]
    W2["wv2"] = wqkvT[0:1024, 2048:3072]
    W2["wo"] = W["woT"]
    W2["bo_pp"] = (bo, boB)
    W2["k_in_ua"] = True
    W2["tri_off"] = 512
    emit_mha(c, W2, "s5", masked=True)


def prep_l1_attn_a(inputs):
    f32 = np.float32
    tri = np.triu(np.ones((128, 128), f32)).astype(BF)
    esel32 = np.zeros((32, 2048), f32)
    for b_ in range(32):
        esel32[b_, b_ * 64:(b_ + 1) * 64] = 1.0
    wr2 = np.zeros((128, 8), f32)
    wr2[:, 0:1] = inputs["L1_attn_a_w_r2"].T
    br2 = np.zeros((1, 8), f32)
    br2[0, 0] = inputs["L1_attn_a_b_r2"][0]
    return {
        "wqkvT": np.ascontiguousarray(inputs["L1_attn_a_w_qkv"].T, f32),
        "wr1T": np.ascontiguousarray(inputs["L1_attn_a_w_r1"].T, f32),
        "wr2T8": wr2,
        "br1_pp": np.ascontiguousarray(
            np.repeat(inputs["L1_attn_a_b_r1"].reshape(128, 1), 8, axis=1),
            f32),
        "br2_pp": br2,
        "woT": np.ascontiguousarray(inputs["L1_attn_a_w_out"].T, f32),
        "bout_pp": np.ascontiguousarray(
            inputs["L1_attn_a_b_out"].reshape(8, 128).T, f32),
        "tri": tri,
        "esel32": esel32.astype(BF),
    }


def emit_ffn_prepass(c, W):
    E = c.E
    w1bf = c.dint("w1bf", [8192, 4096], BF16)
    w2bf = c.dint("w2bf", [32768, 1024], BF16)
    w1bfB = c.kb.buf("w1bf")
    w2bfB = c.kb.buf("w2bf")
    E("pool", lambda e: e.engine_nop(), reads=[c.wsB],
      writes=c.wsHB[0:2], tag="pp_open")
    for i in range(128):
        r0 = (i // 2) * 128
        co = (i % 2) * 2048
        h = i % 2
        E("sync", lambda e, r0=r0, co=co, h=h: e.dma_start(
            c.sf[:, h * 1024:(h + 1) * 1024].bitcast(F32)[:, 0:1024]
            if False else c.sf[:, 0:2048][:, h * 1024:h * 1024 + 1024],
            W["w1"][r0:r0 + 128, co:co + 1024]),
          writes=[c.sfB], dma=True, tag="pp1l") if False else None
    for i in range(256):
        r0 = (i // 4) * 128
        co = (i % 4) * 1024
        h = i % 2
        E("sync", lambda e, r0=r0, co=co, h=h: e.dma_start(
            c.sf[:, h * 1024:h * 1024 + 1024],
            W["w1"][r0:r0 + 128, co:co + 1024]),
          writes=[c.sfB], dma=True, tag="pp1l")
        E("dve", lambda e, h=h: e.tensor_copy(
            c.ws[:, h * 4096:h * 4096 + 1024],
            c.sf[:, h * 1024:h * 1024 + 1024]),
          reads=[c.sfB], writes=[c.wsHB[h]], tag="pp1c")
        E("pool", lambda e, r0=r0, co=co, h=h: e.dma_start(
            w1bf[r0:r0 + 128, co:co + 1024],
            c.ws[:, h * 4096:h * 4096 + 1024]),
          reads=[c.wsHB[h]], writes=[w1bfB], dma=True, tag="pp1s")
    for i in range(256):
        r0 = i * 128
        h = i % 2
        E("sync", lambda e, r0=r0, h=h: e.dma_start(
            c.sf[:, h * 1024:h * 1024 + 1024],
            W["w2"][r0:r0 + 128, 0:1024]),
          writes=[c.sfB], dma=True, tag="pp2l")
        E("dve", lambda e, h=h: e.tensor_copy(
            c.ws[:, h * 4096:h * 4096 + 1024],
            c.sf[:, h * 1024:h * 1024 + 1024]),
          reads=[c.sfB], writes=[c.wsHB[h]], tag="pp2c")
        E("pool", lambda e, r0=r0, h=h: e.dma_start(
            w2bf[r0:r0 + 128, 0:1024], c.ws[:, h * 4096:h * 4096 + 1024]),
          reads=[c.wsHB[h]], writes=[w2bfB], dma=True, tag="pp2s")
    E("pool", lambda e: e.engine_nop(), reads=c.wsHB[0:2],
      writes=[c.wsB], tag="pp_close")
    W["w1bf"] = w1bf
    W["w2bf"] = w2bf
    W["w1bfB"] = w1bfB
    W["w2bfB"] = w2bfB


def emit_l0_ffn(c, W, it=0):
    E = c.E
    tg = f"moe{it}"
    emit_ln(c, c.xc, c.xcB, c.xl, c.xlB)
    E("pool", lambda e: e.engine_nop(), reads=[c.wsB], writes=c.wsHB[0:2],
      tag=f"{tg}_wsopen")
    if W.get("spill"):
        for i in range(NCH):
            E("sync", lambda e, i=i: e.dma_start(
                W["x_dram"][i * 128:(i + 1) * 128, :], c.xc[i][:]),
              reads=[c.xcB[i]], dma=True, tag=f"{tg}_spill{i}")
    idf, idfB = c.alloc("idf", [128, 128], F32)
    E("sync", lambda e: e.dma_start(idf[:], W["ident"]), writes=[idfB],
      dma=True)
    b1t, b1tB = c.alloc("b1t", [128, 256], F32)
    b2t, b2tB = c.alloc("b2t", [128, 64], F32)
    gbt, gbtB = c.alloc("gbt2", [128, 16], F32)
    bgt, bgtB = c.alloc("bgt", [8, 8], F32)
    E("sync", lambda e: e.dma_start(b1t[:], W["b1_pt"]), writes=[b1tB],
      dma=True)
    E("sync", lambda e: e.dma_start(b2t[:], W["b2_pt"]), writes=[b2tB],
      dma=True)
    E("sync", lambda e: e.dma_start(gbt[:], W["gb_pp"]), writes=[gbtB],
      dma=True)
    E("sync", lambda e: e.dma_start(bgt[:], W["bg8"]), writes=[bgtB],
      dma=True)
    stage_cast(c, W["wg_pp"], 0, 128, 0, 64, c.ctb, c.ctbB, 512, f"{tg}wg")
    gL = c.pb(0, 2048)
    for ic in range(NCH):
        for j in range(4):
            E("pe", lambda e, ic=ic, j=j: e.matmul(
                c.pr(0, 8, j * 512, (j + 1) * 512),
                c.ctb[:, 512 + ic * 8:512 + (ic + 1) * 8],
                c.xl[ic][:, j * 512:(j + 1) * 512],
                start=(ic == 0), stop=(ic == NCH - 1)),
              reads=[c.ctbB, c.xlB[ic]], writes=gL, tag=f"{tg}_gmm")
    E("dve", lambda e: e.tensor_scalar(c.sf[0:8, :], c.pr(0, 8, 0, 2048),
                                       bgt[0:8, 0:1], None, ALU.add),
      reads=gL + [bgtB], writes=[c.sfB], tag=f"{tg}_bg")
    gT = c.pb(2048, 2560)
    for i in range(16):
        E("pe", lambda e, i=i: e.transpose(
            c.pr(0, 128, 2048 + (i % 16) * 8, 2048 + (i % 16) * 8 + 8),
            c.sf[0:8, i * 128:(i + 1) * 128], idf[0:8, 0:8]),
          reads=[c.sfB, idfB], writes=gT, tag=f"{tg}_ltr")
    E("act", lambda e: e.copy(c.ct[:, 264:392], c.pr(0, 128, 2048, 2176)),
      reads=gT, writes=[c.ctB_], tag=f"{tg}_ltr_ev")
    for i in range(16):
        L = c.ct[:, 264 + i * 8:272 + i * 8]
        CM = c.ct[:, 392 + i * 8:400 + i * 8]
        nm1 = c.ct[:, 521:522]
        m1 = c.ct[:, 520:521]
        m2 = c.ct[:, 522:523]
        den = c.ct[:, 523:524]
        sc1 = c.ct[:, 524:532]
        sc2 = c.ct[:, 532:540]
        E("dve", lambda e, L=L, nm1=nm1: e.reduce_max(nm1, L, axis=AX.X,
                                                      negate=True),
          reads=[c.ctB_], writes=[c.ctB_], tag=f"{tg}_m1")
        E("dve", lambda e, m1=m1, nm1=nm1: e.tensor_scalar(
            m1, nm1, -1.0, None, ALU.mult),
          reads=[c.ctB_], writes=[c.ctB_], tag=f"{tg}_m1b")
        E("dve", lambda e, L=L, m1=m1, sc1=sc1: e.tensor_scalar(
            sc1, L, m1, None, ALU.is_ge),
          reads=[c.ctB_], writes=[c.ctB_], tag=f"{tg}_ge1")
        E("dve", lambda e, L=L, sc1=sc1, sc2=sc2: e.scalar_tensor_tensor(
            sc2, sc1, -1e30, L, ALU.mult, ALU.add),
          reads=[c.ctB_], writes=[c.ctB_], tag=f"{tg}_lp")
        E("dve", lambda e, m2=m2, sc2=sc2: e.reduce_max(m2, sc2, axis=AX.X),
          reads=[c.ctB_], writes=[c.ctB_], tag=f"{tg}_m2")
        E("dve", lambda e, m2=m2, nm1=nm1, den=den: e.tensor_tensor(
            den, m2, nm1, ALU.add),
          reads=[c.ctB_], writes=[c.ctB_], tag=f"{tg}_dm")
        E("act", lambda e, den=den: e.activation(den, den, AF.Exp),
          reads=[c.ctB_], writes=[c.ctB_], tag=f"{tg}_de")
        E("dve", lambda e, den=den: e.tensor_scalar(den, den, 1.0, None,
                                                    ALU.add),
          reads=[c.ctB_], writes=[c.ctB_], tag=f"{tg}_d1")
        E("dve", lambda e, den=den: e.reciprocal(den, den),
          reads=[c.ctB_], writes=[c.ctB_], tag=f"{tg}_dr")
        E("dve", lambda e, L=L, m2=m2, sc1=sc1: e.tensor_scalar(
            sc1, L, m2, None, ALU.is_ge),
          reads=[c.ctB_], writes=[c.ctB_], tag=f"{tg}_ge2")
        E("act", lambda e, L=L, nm1=nm1, sc2=sc2: e.activation(
            sc2, L, AF.Exp, bias=nm1),
          reads=[c.ctB_], writes=[c.ctB_], tag=f"{tg}_ex")
        E("dve", lambda e, sc2=sc2, den=den, CM=CM: e.tensor_scalar(
            CM, sc2, den, None, ALU.mult),
          reads=[c.ctB_], writes=[c.ctB_], tag=f"{tg}_cm1")
        E("dve", lambda e, CM=CM, sc1=sc1: e.tensor_tensor(
            CM, CM, sc1, ALU.mult),
          reads=[c.ctB_], writes=[c.ctB_], tag=f"{tg}_cm2")
    gT2 = c.pb(2048, 4096)
    for i in range(16):
        E("pe", lambda e, i=i: e.transpose(
            c.pr(0, 8, 2048 + i * 128, 2048 + (i + 1) * 128),
            c.ct[:, 392 + i * 8:400 + i * 8], idf[:]),
          reads=[c.ctB_, idfB], writes=gT2, tag=f"{tg}_ctr")
    E("act", lambda e: e.copy(c.sf[0:8, :], c.pr(0, 8, 2048, 4096)),
      reads=gT2, writes=[c.sfB], tag=f"{tg}_cev")
    cmb_dram = c.dint(f"cmb{it}", [8, 2048], F32)
    cmbB = c.kb.buf(f"cmb{it}")
    E("sync", lambda e: e.dma_start(cmb_dram, c.sf[0:8, :]),
      reads=[c.sfB], writes=[cmbB], dma=True, tag=f"{tg}_cst")
    for i in range(NCH):
        E("pool", lambda e, i=i: e.tensor_copy(c.ua[i][:], c.xl[i][:]),
          reads=[c.xlB[i]], writes=[c.uaB[i]], tag=f"{tg}_ui")
    for ex in range(E_EXP):
        w1ap = W["w1"]
        w2ap = W["w2"]
        w1r = ex * 1024
        w2r = ex * 4096
        for j in range(4):
            E("pool", lambda e, ex=ex, j=j: e.dma_start(
                c.ct[0:1, 264:776],
                cmb_dram[ex:ex + 1, j * 512:(j + 1) * 512]),
              reads=[cmbB], writes=[c.ctB_], dma=True, tag=f"{tg}_crow")
            E("pe", lambda e, j=j: e.matmul(
                c.pr(0, 128, 1024, 1536), c.ones_row, c.ct[0:1, 264:776],
                start=True, stop=True),
              reads=[c.ctB_], writes=c.pb(1024, 1536), tag=f"{tg}_cbc")
            E("act", lambda e, j=j: e.copy(
                c.sb[:, j * 512:(j + 1) * 512],
                c.pr(0, 128, 1024, 1536)),
              reads=c.pb(1024, 1536), writes=[c.sbB], tag=f"{tg}_cbev")
        for hc in range(32):
            E("sync", lambda e, hc=hc, w2r=w2r: e.dma_start(
                c.xc[hc // 4][:].bitcast(BF16)[:, (hc % 4) * 1024:
                                               (hc % 4) * 1024 + 1024],
                W["w2bf"][w2r + hc * 128:w2r + (hc + 1) * 128, 0:1024]),
              reads=[W["w2bfB"]], writes=[c.xcB[hc // 4]], dma=True,
              tag=f"{tg}_w2l")
        for tq in range(4):
            t0 = tq * 512
            for gph in range(8):
                half = (gph % 2) * 4096
                hb = c.wsHB[gph % 2]
                for ic in range(NCH):
                    E("sync", lambda e, ic=ic, gph=gph, half=half, w1r=w1r:
                        e.dma_start(
                          c.ws[:, half + ic * 512:half + (ic + 1) * 512],
                          W["w1bf"][w1r + ic * 128:w1r + (ic + 1) * 128,
                                    gph * 512:(gph + 1) * 512]),
                      reads=[W["w1bfB"]], writes=[hb], dma=True,
                      tag=f"{tg}_w1l")
                for hcl in range(4):
                    hc = gph * 4 + hcl
                    pc = (hc % 2) * 512
                    gH = c.pb(pc, pc + 512)
                    for ic in range(NCH):
                        E("pe", lambda e, ic=ic, hcl=hcl, half=half, pc=pc,
                            t0=t0: e.matmul(
                              c.pr(0, 128, pc, pc + 512),
                              c.ws[:, half + ic * 512 + hcl * 128:
                                   half + ic * 512 + hcl * 128 + 128],
                              c.xl[ic][:, t0:t0 + 512],
                              start=(ic == 0), stop=(ic == NCH - 1)),
                          reads=[hb, c.xlB[ic]], writes=gH,
                          tag=f"{tg}_amm")
                    E("act", lambda e, hc=hc, pc=pc, ex=ex: e.activation(
                        c.sa[:, hc * 512:(hc + 1) * 512],
                        c.pr(0, 128, pc, pc + 512), AF.Gelu,
                        bias=b1t[:, ex * 32 + hc:ex * 32 + hc + 1]),
                      reads=c.pb(pc, pc + 512) + [b1tB], writes=[c.saB],
                      tag=f"{tg}_gelu")
            for w in range(2):
                gY = c.pb(2048, 4096)
                for hc in range(32):
                    for k in range(4):
                        dc = w * 4 + k
                        E("pe", lambda e, hc=hc, k=k, dc=dc: e.matmul(
                            c.pr(0, 128, 2048 + k * 512, 2048 + k * 512 + 512),
                            c.xc[hc // 4][:].bitcast(BF16)[
                                :, (hc % 4) * 1024 + dc * 128:
                                (hc % 4) * 1024 + dc * 128 + 128],
                            c.sa[:, hc * 512:(hc + 1) * 512],
                            start=(hc == 0), stop=(hc == 31)),
                          reads=[c.xcB[hc // 4], c.saB],
                          writes=c.pb(2048 + k * 512, 2048 + k * 512 + 512),
                          tag=f"{tg}_bmm")
                for k in range(4):
                    dc = w * 4 + k
                    gk = c.pb(2048 + k * 512, 2048 + k * 512 + 512)
                    E("dve", lambda e, k=k, dc=dc, ex=ex, t0=t0:
                        e.scalar_tensor_tensor(
                          c.sb[:, 2048 + k * 512:2048 + (k + 1) * 512],
                          c.pr(0, 128, 2048 + k * 512, 2048 + k * 512 + 512),
                          b2t[:, ex * 8 + dc:ex * 8 + dc + 1],
                          c.sb[:, t0:t0 + 512],
                          ALU.add, ALU.mult),
                      reads=gk + [b2tB, c.sbB], writes=[c.sbB],
                      tag=f"{tg}_yev")
                    E("pool", lambda e, k=k, dc=dc, t0=t0: e.tensor_tensor(
                        c.ua[dc][:, t0:t0 + 512], c.ua[dc][:, t0:t0 + 512],
                        c.sb[:, 2048 + k * 512:2048 + (k + 1) * 512],
                        ALU.add),
                      reads=[c.sbB, c.uaB[dc]], writes=[c.uaB[dc]],
                      tag=f"{tg}_uacc")
    E("pool", lambda e: e.engine_nop(), reads=c.wsHB[0:2], writes=[c.wsB],
      tag=f"{tg}_wsclose")
    gA = c.pb(0, 2048)
    gB = c.pb(2048, 4096)
    for i in range(NCH):
        E("pool", lambda e, i=i: e.tensor_tensor(
            c.sb[:, 0:2048], c.ua[i][:], c.ua[i][:], ALU.mult),
          reads=[c.uaB[i]], writes=[c.sbB], tag=f"{tg}_usq")
        for j in range(4):
            E("pe", lambda e, i=i, j=j: e.matmul(
                c.pr(0, 1, j * 512, (j + 1) * 512), c.ones_col16,
                c.ua[i][:, j * 512:(j + 1) * 512],
                start=(i == 0), stop=(i == NCH - 1)),
              reads=[c.uaB[i], c.ctbB], writes=gA, tag=f"{tg}_us")
            E("pe", lambda e, i=i, j=j: e.matmul(
                c.pr(32, 33, j * 512, (j + 1) * 512), c.ones_col16,
                c.sb[:, j * 512:(j + 1) * 512],
                start=(i == 0), stop=(i == NCH - 1)),
              reads=[c.sbB, c.ctbB], writes=gA, tag=f"{tg}_us2")
    r_mean = c.sf[0:1, :]
    r_a = c.ws[0:1, 0:4096].bitcast(F32)
    r_b = c.ws[0:1, 4096:8192].bitcast(F32)
    eps_col = c.const(LN_EPS)
    E("dve", lambda e: e.tensor_scalar(r_mean, c.pr(0, 1, 0, 2048),
                                       1.0 / D, None, ALU.mult),
      reads=gA, writes=[c.sfB], tag=f"{tg}_fmean")
    E("dve", lambda e: e.tensor_scalar(r_a, c.pr(32, 33, 0, 2048),
                                       1.0 / D, None, ALU.mult),
      reads=gA, writes=[c.wsB], tag=f"{tg}_fms")
    E("dve", lambda e: e.scalar_tensor_tensor(
        r_b, r_mean, -1.0, r_mean, ALU.mult, ALU.mult),
      reads=[c.sfB], writes=[c.wsB], tag=f"{tg}_fnm2")
    E("dve", lambda e: e.tensor_tensor(r_a, r_a, r_b, ALU.add),
      reads=[c.wsB], writes=[c.wsB], tag=f"{tg}_fvar")
    E("act", lambda e: e.activation(r_b, r_a, AF.Ln, bias=eps_col[0:1, :]),
      reads=[c.wsB, c.ctB_], writes=[c.wsB], tag=f"{tg}_fln")
    E("act", lambda e: e.activation(r_a, r_b, AF.Exp, scale=-0.5),
      reads=[c.wsB], writes=[c.wsB], tag=f"{tg}_frstd")
    E("dve", lambda e: e.scalar_tensor_tensor(
        r_b, r_mean, -1.0, r_a, ALU.mult, ALU.mult),
      reads=[c.sfB, c.wsB], writes=[c.wsB], tag=f"{tg}_fb")
    for j in range(4):
        E("pe", lambda e, j=j: e.matmul(
            c.pr(0, 128, j * 512, (j + 1) * 512), c.ones_row,
            c.ws[0:1, j * 1024:j * 1024 + 1024].bitcast(F32),
            start=True, stop=True),
          reads=[c.wsB, c.ctB_], writes=gA, tag=f"{tg}_fbcA")
        E("pe", lambda e, j=j: e.matmul(
            c.pr(0, 128, 2048 + j * 512, 2048 + (j + 1) * 512), c.ones_row,
            c.ws[0:1, 4096 + j * 1024:4096 + j * 1024 + 1024].bitcast(F32),
            start=True, stop=True),
          reads=[c.wsB, c.ctB_], writes=gB, tag=f"{tg}_fbcB")
    for i in range(NCH):
        E("sync", lambda e, i=i: e.dma_start(
            c.xc[i][:], W["x_dram"][i * 128:(i + 1) * 128, :]),
          writes=[c.xcB[i]], dma=True, tag=f"{tg}_xrl{i}")
        E("dve", lambda e, i=i: e.tensor_tensor(
            c.sf[:], c.ua[i][:], c.pr(0, 128, 0, 2048), ALU.mult),
          reads=[c.uaB[i]] + gA, writes=[c.sfB], tag=f"{tg}_fn1")
        E("dve", lambda e, i=i: e.tensor_tensor(
            c.sf[:], c.sf[:], c.pr(0, 128, 2048, 4096), ALU.add),
          reads=[c.sfB] + gB, writes=[c.sfB], tag=f"{tg}_fn2")
        E("act", lambda e, i=i: e.activation(
            c.sf[:], c.sf[:], AF.Copy, scale=gbt[:, i:i + 1]),
          reads=[c.sfB, gbtB], writes=[c.sfB], tag=f"{tg}_fg")
        E("dve", lambda e, i=i: e.scalar_tensor_tensor(
            c.xc[i][:], c.sf[:], gbt[:, 8 + i:9 + i], c.xc[i][:],
            ALU.add, ALU.add),
          reads=[c.sfB, gbtB, c.xcB[i]], writes=[c.xcB[i]],
          tag=f"{tg}_fres")


def prep_l0_ffn(inputs):
    f32 = np.float32
    bg8 = np.repeat(inputs["L0_ffn_bg"].reshape(8, 1), 8, axis=1)
    gb = np.stack([inputs["L0_ffn_gamma"], inputs["L0_ffn_beta"]])
    return {
        "wg_pp": np.ascontiguousarray(
            inputs["L0_ffn_Wg"].reshape(8, 128, 8).transpose(1, 0, 2)
            .reshape(128, 64), f32),
        "bg8": np.ascontiguousarray(bg8, f32),
        "w1": np.ascontiguousarray(inputs["L0_ffn_W1"].reshape(8192, 4096), f32),
        "w2": np.ascontiguousarray(inputs["L0_ffn_W2"].reshape(32768, 1024), f32),
        "b1_pt": np.ascontiguousarray(
            inputs["L0_ffn_b1"].reshape(8, 32, 128).transpose(2, 0, 1)
            .reshape(128, 256), f32),
        "b2_pt": np.ascontiguousarray(
            inputs["L0_ffn_b2"].reshape(8, 8, 128).transpose(2, 0, 1)
            .reshape(128, 64), f32),
        "gb_pp": np.ascontiguousarray(
            gb.reshape(2, 8, 128).transpose(2, 0, 1).reshape(128, 16), f32),
        "ident": np.eye(128, dtype=f32),
    }


def emit_s7_qkv(c, W, qkv_out, qkv_outB):
    E = c.E
    emit_ln(c, c.xc, c.xcB, c.xl, c.xlB)
    gbt, gbtB = c.alloc("s7gb", [128, 16], F32)
    E("sync", lambda e: e.dma_start(gbt[:], W["gb_pp"]), writes=[gbtB],
      dma=True)
    emit_ln(c, c.xl, c.xlB, c.ua, c.uaB, eps=1e-6, src16=True,
            gb=(gbt, gbtB, 0, 8))
    for t3 in range(3):
        for oc in range(NCH):
            for ic in range(NCH):
                stage_cast(c, W["qkvk"], t3 * 1024 + ic * 128, 128,
                           oc * 128, 128, c.ws, c.wsB,
                           ((oc % 4) * 8 + ic) * 128, f"s7w{t3}_{oc}_{ic}",
                           sf_off=(ic % 4) * 512)
            for tq in range(4):
                pc = (tq % 2) * 512
                g = c.pb(pc, pc + 512)
                for ic in range(NCH):
                    E("pe", lambda e, ic=ic, oc=oc, tq=tq, pc=pc: e.matmul(
                        c.pr(0, 128, pc, pc + 512),
                        c.ws[:, ((oc % 4) * 8 + ic) * 128:
                             ((oc % 4) * 8 + ic) * 128 + 128],
                        c.ua[ic][:, tq * 512:(tq + 1) * 512],
                        start=(ic == 0), stop=(ic == NCH - 1)),
                      reads=[c.wsB, c.uaB[ic]], writes=g, tag="s7_pmm")
                E("act", lambda e, tq=tq, pc=pc: e.copy(
                    c.xl[3][:, tq * 512:(tq + 1) * 512],
                    c.pr(0, 128, pc, pc + 512)),
                  reads=g, writes=[c.xlB[3]], tag="s7_kvev")
            E("sync", lambda e, t3=t3, oc=oc: e.dma_start(
                qkv_out[t3 * 1024 + oc * 128:t3 * 1024 + (oc + 1) * 128, :],
                c.xl[3][:]),
              reads=[c.xlB[3]], writes=[qkv_outB], dma=True, tag="s7_kvst")


def emit_s7_attn(c, W, q_in, kv_all, kv_allB):
    E = c.E
    msk, mskB = c.alloc("s7msk", [128, 8], F32)
    E("sync", lambda e: e.dma_start(msk[:], W["mask_pp"]), writes=[mskB],
      dma=True)
    E("sync", lambda e: e.dma_start(c.ctb[:, 512:528], W["esel8"]),
      writes=[c.ctbB], dma=True, tag="s7e8")
    E("sync", lambda e: e.dma_start(c.ctb[0:16, 528:1552], W["eselT"]),
      writes=[c.ctbB], dma=True, tag="s7eT")
    for oc in range(NCH):
        E("sync", lambda e, oc=oc: e.dma_start(
            c.sa[:, oc * 2048:(oc + 1) * 2048],
            q_in[oc * 128:(oc + 1) * 128, :]),
          writes=[c.saB], dma=True, tag="s7_qld")
    E("sync", lambda e: e.dma_start(c.ws[:], W["sel2"]),
      writes=[c.wsB], dma=True, tag="s7_sel2")
    gL = c.pb(0, 2048)
    for kb in range(B):
        for ic in range(NCH):
            koff = (ic % 2) * 2048
            E("sync", lambda e, kb=kb, ic=ic, koff=koff: e.dma_start(
                c.sb[:, koff:koff + 2048],
                kv_all[kb * 2048 + ic * 128:kb * 2048 + (ic + 1) * 128, :]),
              writes=[c.sbB], dma=True, tag="s7_kld")
            E("dve", lambda e, ic=ic, koff=koff: e.tensor_tensor(
                c.xl[0][:], c.sa[:, ic * 2048:(ic + 1) * 2048],
                c.sb[:, koff:koff + 2048], ALU.mult),
              reads=[c.sbB, c.saB], writes=[c.xlB[0]], tag="s7_prod")
            for j in range(4):
                E("pe", lambda e, kb=kb, ic=ic, j=j: e.matmul(
                    c.pr(0, 128, j * 512, (j + 1) * 512),
                    c.ws[:, (ic * 8 + kb) * 128:(ic * 8 + kb) * 128 + 128],
                    c.xl[0][:, j * 512:(j + 1) * 512],
                    start=(ic == 0 and kb == 0),
                    stop=(ic == NCH - 1 and kb == B - 1)),
                  reads=[c.wsB, c.xlB[0]], writes=gL, tag="s7_lmm")
    E("act", lambda e: e.activation(c.xl[1][:], c.pr(0, 128, 0, 2048),
                                    AF.Exp, bias=msk[:, 0:1], scale=0.125),
      reads=gL + [mskB], writes=[c.xlB[1]], tag="s7_exp")
    for j in range(4):
        E("pe", lambda e, j=j: e.matmul(
            c.pr(0, 16, j * 512, (j + 1) * 512), c.ctb[:, 512:528],
            c.xl[1][:, j * 512:(j + 1) * 512], start=True, stop=True),
          reads=[c.ctbB, c.xlB[1]], writes=gL, tag="s7_rs")

    def _rcp(e):
        with c.nc.allow_low_precision(reason="softmax normalizer in bf16"):
            return e.reciprocal(c.xl[2][0:16, :], c.pr(0, 16, 0, 2048))
    E("dve", _rcp, reads=gL, writes=[c.xlB[2]], tag="s7_rcp")
    E("sync", lambda e: e.dma_start(c.ws[:], W["sel2T"]),
      writes=[c.wsB], dma=True, tag="s7_sel2T")
    for ic in range(NCH):
        for kb in range(B):
            koff = (kb % 2) * 2048
            E("sync", lambda e, kb=kb, ic=ic, koff=koff: e.dma_start(
                c.sb[:, koff:koff + 2048],
                kv_all[kb * 2048 + 1024 + ic * 128:
                       kb * 2048 + 1024 + (ic + 1) * 128, :]),
              writes=[c.sbB], dma=True, tag="s7_vld")
            for j in range(4):
                E("pe", lambda e, kb=kb, ic=ic, j=j: e.matmul(
                    c.pr(0, 128, 2048 + j * 512, 2048 + (j + 1) * 512),
                    c.ws[:, (ic * 8 + kb) * 128:(ic * 8 + kb) * 128 + 128],
                    c.xl[1][:, j * 512:(j + 1) * 512],
                    start=True, stop=True),
                  reads=[c.wsB, c.xlB[1]],
                  writes=c.pb(2048 + j * 512, 2048 + (j + 1) * 512),
                  tag="s7_pbc")
            if kb == 0:
                for j in range(4):
                    E("dve", lambda e, j=j, koff=koff: e.tensor_tensor(
                        c.sf[:, j * 512:(j + 1) * 512],
                        c.sb[:, koff + j * 512:koff + (j + 1) * 512],
                        c.pr(0, 128, 2048 + j * 512, 2048 + (j + 1) * 512),
                        ALU.mult),
                      reads=[c.sbB] + c.pb(2048 + j * 512,
                                           2048 + (j + 1) * 512),
                      writes=[c.sfB], tag="s7_cx0")
            else:
                for j in range(4):
                    E("dve", lambda e, j=j, koff=koff: e.tensor_tensor(
                        c.sb[:, 4096 + (j % 3) * 512:
                             4096 + (j % 3) * 512 + 512],
                        c.sb[:, koff + j * 512:koff + (j + 1) * 512],
                        c.pr(0, 128, 2048 + j * 512, 2048 + (j + 1) * 512),
                        ALU.mult),
                      reads=[c.sbB] + c.pb(2048 + j * 512,
                                           2048 + (j + 1) * 512),
                      writes=[c.sbB], tag="s7_cxm")
                    E("pool", lambda e, j=j: e.tensor_tensor(
                        c.sf[:, j * 512:(j + 1) * 512],
                        c.sf[:, j * 512:(j + 1) * 512],
                        c.sb[:, 4096 + (j % 3) * 512:
                             4096 + (j % 3) * 512 + 512], ALU.add),
                      reads=[c.sbB, c.sfB], writes=[c.sfB], tag="s7_cxa")
        for j in range(4):
            E("pe", lambda e, ic=ic, j=j: e.matmul(
                c.pr(0, 128, 2048 + j * 512, 2048 + (j + 1) * 512),
                c.ctb[0:16, 528 + ic * 128:528 + (ic + 1) * 128],
                c.xl[2][0:16, j * 512:(j + 1) * 512],
                start=True, stop=True),
              reads=[c.ctbB, c.xlB[2]],
              writes=c.pb(2048 + j * 512, 2048 + (j + 1) * 512),
              tag="s7_rbc")
            E("dve", lambda e, ic=ic, j=j: e.tensor_tensor(
                c.ua[ic][:, j * 512:(j + 1) * 512],
                c.sf[:, j * 512:(j + 1) * 512],
                c.pr(0, 128, 2048 + j * 512, 2048 + (j + 1) * 512),
                ALU.mult),
              reads=[c.sfB] + c.pb(2048 + j * 512, 2048 + (j + 1) * 512),
              writes=[c.uaB[ic]], tag="s7_nrm")
    ua_l = [c.ua[i][:] for i in range(NCH)]

    def consume_o(oc, tq0, width, c0, guards):
        E("dve", lambda e: e.tensor_tensor(
            c.xc[oc][:, tq0:tq0 + width], c.pr(0, 128, c0, c0 + width),
            c.xc[oc][:, tq0:tq0 + width], ALU.add),
          reads=guards + [c.xcB[oc]], writes=[c.xcB[oc]], tag="s7_res")

    emit_proj(c, W["wout"], ua_l, c.uaB, NCH, consume_o, "s7o")


def prep_l1_attn_b(inputs):
    f32 = np.float32
    esel8 = np.zeros((128, 16), f32)
    eselT = np.zeros((8, 16, 128), f32)
    sel2 = np.zeros((128, 64 * 128), f32)
    sel2T = np.zeros((128, 64 * 128), f32)
    for ic in range(8):
        for p in range(128):
            h = 2 * ic + p // 64
            for kb in range(8):
                sel2[p, (ic * 8 + kb) * 128 + h * 8 + kb] = 1.0
                sel2T[h * 8 + kb, (ic * 8 + kb) * 128 + p] = 1.0
            eselT[ic, h % 16, p] = 0.0
    for pp in range(128):
        esel8[pp, pp // 8] = 1.0
    for ic in range(8):
        for p in range(128):
            h = 2 * ic + p // 64
            eselT[ic, h, p] = 1.0
    gb = np.stack([inputs["L1_attn_b_ln_scale"], inputs["L1_attn_b_ln_bias"]])
    masks = []
    for core in range(8):
        m = np.zeros((128, 8), f32)
        for p in range(128):
            if p % 8 > core:
                m[p, 0] = -1e10
        masks.append(np.ascontiguousarray(m))
    return {
        "qkvk": np.ascontiguousarray(
            inputs["L1_attn_b_qkv_kernel"].transpose(1, 0, 2)
            .reshape(3072, 1024), f32),
        "wout": np.ascontiguousarray(inputs["L1_attn_b_out_kernel"], f32),
        "gb_pp": np.ascontiguousarray(
            gb.reshape(2, 8, 128).transpose(2, 0, 1).reshape(128, 16), f32),
        "esel8": esel8.astype(BF),
        "eselT": np.ascontiguousarray(
            eselT.transpose(1, 0, 2).reshape(16, 1024)).astype(BF),
        "sel2": sel2.astype(BF),
        "sel2T": sel2T.astype(BF),
        "mask_pp": masks,
    }


def run_launch(subs, x_fm, inputs, extra=None):
    c = C("L" + "_".join(s for s in subs))
    x_in = c.din("x_in", [np.ascontiguousarray(x_fm[i]) for i in range(B)],
                 percore=True)
    x_out = c.dout("x_out", (1024, 2048), F32)
    load_x(c, x_in)

    def din_w(pfx, W):
        out = {}
        for k, v in W.items():
            dt = BF16 if v.dtype == BF else F32
            out[k] = c.din(f"{pfx}_{k}", v, dt=dt)
        return out

    for s in subs:
        if s in ("s6", "s8"):
            emit_l1_ffn(c, din_w(s, prep_l1_ffn(inputs)),
                        it=0 if s == "s6" else 1)
        elif s == "s3":
            emit_l0_attn_b(c, din_w(s, prep_l0_attn_b(inputs)))
        elif s == "s1":
            emit_l0_attn_a(c, din_w(s, prep_l0_attn_a(inputs)))
        elif s == "s5":
            emit_l1_attn_a(c, din_w(s, prep_l1_attn_a(inputs)))
        elif s == "s7a":
            Wh = prep_l1_attn_b(inputs)
            Wh.pop("mask_pp")
            W = din_w(s, {"qkvk": Wh["qkvk"], "gb_pp": Wh["gb_pp"]})
            qkv_out = c.dout("qkv_out", (3 * 1024, 2048), BF16)
            qkvB = c.kb.buf("qkv_out")
            emit_s7_qkv(c, W, qkv_out, qkvB)
        elif s == "s7b":
            Wh = prep_l1_attn_b(inputs)
            masks = Wh.pop("mask_pp")
            W = din_w(s, {k: Wh[k] for k in
                          ("wout", "esel8", "eselT", "sel2", "sel2T")})
            W["mask_pp"] = c.din("s7_mask_pp", masks, percore=True)
            q_in = c.din("s7_q", extra["q"], dt=BF16, percore=True)
            kv_all = c.din("s7_kv", extra["kv"], dt=BF16)
            emit_s7_attn(c, W, q_in, kv_all, c.kb.buf("kv_all"))
        elif s in ("s2", "s4"):
            W = din_w(s, prep_l0_ffn(inputs))
            W["x_dram"] = x_in
            W["spill"] = False
            emit_ffn_prepass(c, W)
            emit_l0_ffn(c, W, it=0 if s == "s2" else 1)
        else:
            raise NotImplementedError(s)
    store_x(c, x_out)
    stats = c.finish()
    res = c.run()
    out = np.stack([res.results[i]["x_out"] for i in range(B)])
    ex = {}
    if "s7a" in subs:
        ex["qkv"] = [np.asarray(res.results[i]["qkv_out"]) for i in range(B)]
    return out, stats, ex


LAUNCH_PLAN = [["s1"], ["s2"], ["s3"], ["s4"], ["s5"], ["s6"],
               ["s7a"], ["s7b"], ["s8"]]
FUSED2 = True


def run_fused_A(x_fm, inputs):
    c = C("fA")
    x_in = c.din("x_in", [np.ascontiguousarray(x_fm[i]) for i in range(B)],
                 percore=True)
    x_out = c.dout("x_out", (1024, 2048), F32)
    qkv_out = c.dout("qkv_out", (3 * 1024, 2048), BF16)
    qkvB = c.kb.buf("qkv_out")
    xspill = c.dint("xspill", (1024, 2048), F32)

    def din_w(pfx, W):
        out = {}
        for k, v in W.items():
            dt = BF16 if v.dtype == BF else F32
            out[k] = c.din(f"{pfx}_{k}", v, dt=dt)
        return out

    load_x(c, x_in)
    emit_l0_attn_a(c, din_w("s1", prep_l0_attn_a(inputs)))
    Wf = din_w("s2", prep_l0_ffn(inputs))
    Wf["x_dram"] = xspill
    Wf["spill"] = True
    emit_ffn_prepass(c, Wf)
    emit_l0_ffn(c, Wf, it=0)
    emit_l0_attn_b(c, din_w("s3", prep_l0_attn_b(inputs)))
    emit_l0_ffn(c, Wf, it=1)
    emit_l1_attn_a(c, din_w("s5", prep_l1_attn_a(inputs)))
    W6 = din_w("s6", prep_l1_ffn(inputs))
    emit_l1_ffn(c, W6, it=0)
    W7 = prep_l1_attn_b(inputs)
    W7.pop("mask_pp")
    W7a = din_w("s7a", {"qkvk": W7["qkvk"], "gb_pp": W7["gb_pp"]})
    emit_s7_qkv(c, W7a, qkv_out, qkvB)
    store_x(c, x_out)
    stats = c.finish()
    res = c.run()
    out = np.stack([res.results[i]["x_out"] for i in range(B)])
    qkv = [np.asarray(res.results[i]["qkv_out"]) for i in range(B)]
    return out, stats, qkv


def run_fused_B(x_fm, inputs, extra):
    c = C("fB")
    x_in = c.din("x_in", [np.ascontiguousarray(x_fm[i]) for i in range(B)],
                 percore=True)
    x_out = c.dout("x_out", (1024, 2048), F32)

    def din_w(pfx, W):
        out = {}
        for k, v in W.items():
            dt = BF16 if v.dtype == BF else F32
            out[k] = c.din(f"{pfx}_{k}", v, dt=dt)
        return out

    load_x(c, x_in)
    Wh = prep_l1_attn_b(inputs)
    masks = Wh.pop("mask_pp")
    W = din_w("s7b", {k: Wh[k] for k in
                      ("wout", "esel8", "eselT", "sel2", "sel2T")})
    W["mask_pp"] = c.din("s7_mask_pp", masks, percore=True)
    q_in = c.din("s7_q", extra["q"], dt=BF16, percore=True)
    kv_all = c.din("s7_kv", extra["kv"], dt=BF16)
    emit_s7_attn(c, W, q_in, kv_all, c.kb.buf("kv_all"))
    emit_l1_ffn(c, din_w("s8", prep_l1_ffn(inputs)), it=1)
    store_x(c, x_out)
    stats = c.finish()
    res = c.run()
    return np.stack([res.results[i]["x_out"] for i in range(B)]), stats


def kernel(**inputs):
    x = inputs["L0_attn_a_x"]
    x_fm = np.ascontiguousarray(x.transpose(0, 2, 1)).astype(np.float32)
    if FUSED2:
        x_fm, _, qkv = run_fused_A(x_fm, inputs)
        extra = {
            "q": [np.ascontiguousarray(qkv[b][0:1024]) for b in range(B)],
            "kv": np.ascontiguousarray(
                np.concatenate([qkv[b][1024:3072] for b in range(B)],
                               axis=0)),
        }
        x_fm, _ = run_fused_B(x_fm, inputs, extra)
        return np.ascontiguousarray(x_fm.transpose(0, 2, 1))
    extra = None
    for subs in LAUNCH_PLAN:
        x_fm, _, ex = run_launch(subs, x_fm, inputs, extra=extra)
        if "qkv" in ex:
            qkv = ex["qkv"]
            extra = {
                "q": [np.ascontiguousarray(qkv[b][0:1024]) for b in range(B)],
                "kv": np.ascontiguousarray(
                    np.concatenate([qkv[b][1024:3072] for b in range(B)],
                                   axis=0)),
            }
    return np.ascontiguousarray(x_fm.transpose(0, 2, 1))
